# version 06
# baseline (speedup 1.0000x reference)
import numpy as np
import ml_dtypes
import concourse.bass as bass
import concourse.mybir as mybir
from concourse.bass_utils import run_bass_kernel_spmd

F32 = mybir.dt.float32
BF16 = mybir.dt.bfloat16
I32 = mybir.dt.int32
AF = mybir.ActivationFunctionType
ALU = mybir.AluOpType
AX = mybir.AxisListType

S = 4096
D = 1024
NPC = D // 128
EPS = 1e-5
NEG = -1.0e30


class Sem:
    def __init__(self, h):
        self.h = h
        self.n = 0


import os
STRICT = os.environ.get("BASS_STRICT_SCHED", "") == "1"


class Sch:

    def __init__(self, sems=None):
        self.steps = []
        self.sems = sems

    def op(self, eng, fn, waits=(), inc=None):
        if STRICT and self.sems is not None:
            w = [(s.h, s.n) for s in self.sems if s.n > 0]
        else:
            w = [(s.h, v) for (s, v) in waits if v > 0]
        incs = []
        if inc is not None:
            if isinstance(inc, list):
                for sem, k in inc:
                    sem.n += k
                    incs.append((sem.h, k))
            else:
                sem, k = inc
                sem.n += k
                incs.append((sem.h, k))
        self.steps.append((eng, fn, w, incs))

    def emit(self, eng_name, eng, flush=None):
        last = {}
        for (e, fn, waits, incs) in self.steps:
            if e != eng_name:
                continue
            for (h, v) in waits:
                if last.get(h, -1) < v:
                    eng.wait_ge(h, v)
                    last[h] = v
            inst = fn(eng)
            if incs and flush is not None:
                inst = flush(eng)
            for (h, k) in incs:
                assert inst is not None
                inst = inst.then_inc(h, k)


def at(sem):
    return (sem, sem.n)


def lag(sem, k):
    return (sem, max(0, sem.n - k))


def recip(eng, out_ap, in_ap, bias=None):
    nc = eng.bass
    ins_ = [eng.lower_ap(in_ap.opt()),
            eng.lower_ap(bias.opt()) if bias is not None
            else mybir.ImmediateValue(dtype=F32, value=0.0),
            mybir.ImmediateValue(dtype=F32, value=1.0),
            mybir.ImmediateValue(dtype=F32, value=0.0)]
    outs_ = [eng.lower_ap(out_ap.opt())]
    inst = mybir.InstActivation(name=nc.get_next_instruction_name(),
                                ins=ins_, outs=outs_, func=AF.Reciprocal)
    return eng.add_instruction(inst)


def hbm_fm(ap, tsl=None):
    if tsl is not None:
        ap = ap[:, tsl]
    return ap.rearrange("(c p) t -> p c t", p=128)


class SM:

    def __init__(self, nc, stack, names):
        for n in names:
            h = stack.enter_context(nc.semaphore("s_" + n))
            setattr(self, n, Sem(h))


SEM_NAMES = ("mm", "pv", "smk", "dve", "evv", "act", "evs", "stats", "xh",
             "xcfree", "xbfree", "xbrv", "xbrs",
             "c_const", "c_wa", "c_wb", "c_xc", "c_ko", "c_vo0", "c_vo1",
             "c_vb0", "c_vb1", "c_vb2", "c_vb3", "c_kb0", "c_kb1",
             "c_x10", "c_x11", "c_trA", "c_trC")


def emit_ln_stats(sch, sm, xsrc_h, xc, xb, xsq, PS0, PS1, tmpA, tmpB, tmpC,
                  a_row, b_row, ones_bf, eps_col):
    for tc in range(8):
        tsl = slice(tc * 512, (tc + 1) * 512)
        sch.op("sync", lambda e, t=tsl: e.dma_start(
            out=xc[:], in_=hbm_fm(xsrc_h, t)), waits=[at(sm.xcfree)],
            inc=(sm.c_xc, 16))
        sch.op("vector", lambda e: e.tensor_copy(xb[:], xc[:]),
               waits=[at(sm.c_xc), at(sm.xbfree)], inc=(sm.xbrv, 1))
        sch.op("scalar", lambda e: e.activation(xsq[:], xc[:], AF.Square),
               waits=[at(sm.c_xc), at(sm.xbfree)], inc=(sm.xbrs, 1))
        for pc in range(NPC):
            sch.op("tensor", lambda e, p=pc: e.matmul(
                PS0[:], ones_bf[:], xb[:, p, :],
                start=(p == 0), stop=(p == NPC - 1)),
                waits=[at(sm.xbrv), at(sm.xbrs), at(sm.evv)] if pc == 0 else [])
        for pc in range(NPC):
            sch.op("tensor", lambda e, p=pc: e.matmul(
                PS1[:], ones_bf[:], xsq[:, p, :],
                start=(p == 0), stop=(p == NPC - 1)),
                inc=(sm.mm, 1) if pc == NPC - 1 else None)
        sch.op("vector", lambda e: e.tensor_scalar(
            tmpA[:], PS0[:], 1.0 / D, None, ALU.mult), waits=[at(sm.mm)])
        sch.op("vector", lambda e: e.tensor_scalar(
            tmpB[:], PS1[:], 1.0 / D, None, ALU.mult), inc=(sm.evv, 1))
        sch.op("vector", lambda e: e.tensor_mul(tmpC[:], tmpA[:], tmpA[:]))
        sch.op("vector", lambda e: e.tensor_sub(tmpB[:], tmpB[:], tmpC[:]),
               inc=(sm.dve, 1))
        sch.op("scalar", lambda e: e.activation(
            tmpC[:], tmpB[:], AF.Ln, bias=eps_col[:]), waits=[at(sm.dve)])
        sch.op("scalar", lambda e: e.activation(
            tmpC[:], tmpC[:], AF.Exp, scale=-0.5), inc=(sm.act, 1))
        sch.op("vector", lambda e: e.scalar_tensor_tensor(
            tmpA[:], tmpA[:], -1.0, tmpC[:], ALU.mult, ALU.mult),
            waits=[at(sm.act)])
        sch.op("vector", lambda e, t=tsl: e.tensor_copy(
            a_row[0:1, t], tmpC[0:1, :]))
        sch.op("vector", lambda e, t=tsl: e.tensor_copy(
            b_row[0:1, t], tmpA[0:1, :]), inc=(sm.stats, 1))
        sch.op("vector", lambda e: e.engine_nop(), waits=[at(sm.mm)],
               inc=(sm.xcfree, 1))
        sch.op("vector", lambda e: e.engine_nop(), inc=(sm.xbfree, 1))


def emit_xhat_chunk(sch, sm, xsrc_h, tsl, xc, xh, PB0, PB1, abc,
                    a_row, b_row, ones1, t1):
    sch.op("sync", lambda e, t=tsl: e.dma_start(
        out=xc[:], in_=hbm_fm(xsrc_h, t)), waits=[at(sm.xcfree)],
        inc=(sm.c_xc, 16))
    sch.op("tensor", lambda e, t=tsl: e.matmul(
        PB0[:], ones1[:], a_row[0:1, t], start=True, stop=True),
        waits=[at(sm.stats), at(sm.evv), at(sm.evs)])
    sch.op("tensor", lambda e, t=tsl: e.matmul(
        PB1[:], ones1[:], b_row[0:1, t], start=True, stop=True),
        inc=(sm.mm, 1))
    sch.op("vector", lambda e: e.tensor_copy(abc[:, :, 0], PB0[:]),
           waits=[at(sm.mm)])
    sch.op("vector", lambda e: e.tensor_copy(abc[:, :, 1], PB1[:]),
           inc=(sm.evv, 1))
    for pc in range(NPC):
        w = [at(sm.c_xc), at(sm.stats), at(sm.mm)] if pc == 0 else []
        sch.op("vector", lambda e, p=pc: e.tensor_mul(
            t1[:], xc[:, p, :], abc[:, :, 0]), waits=w)
        sch.op("vector", lambda e, p=pc: e.tensor_add(
            xh[:, p, :], t1[:], abc[:, :, 1]),
            inc=(sm.xh, 1) if pc == NPC - 1 else None)


LN_C = float(np.log(np.sqrt(1024.0) / np.log(1025.0)))
SEM_NAMES = SEM_NAMES + ("c_d0", "c_d1", "c_d2", "c_d3", "c_d4", "c_d5", "c_d6",
                         "w1a", "w1b", "w2a", "w2b")


def emit_ln_xhat(sch, sm, xsrc_h, tsl, xc, xb, xsq, xh, PS0, PS1, scr,
                 ones_bf, eps_col, t1):
    sA, sB, sC = scr[:, :, 0], scr[:, :, 1], scr[:, :, 2]
    sch.op("sync", lambda e, t=tsl: e.dma_start(
        out=xc[:], in_=hbm_fm(xsrc_h, t)), waits=[at(sm.xcfree)],
        inc=(sm.c_xc, 16))
    sch.op("vector", lambda e: e.tensor_copy(xb[:], xc[:]),
           waits=[at(sm.c_xc), at(sm.mm)], inc=(sm.xbrv, 1))
    sch.op("scalar", lambda e: e.activation(xsq[:], xc[:], AF.Square),
           waits=[at(sm.c_xc), at(sm.mm)], inc=(sm.xbrs, 1))
    for pc in range(NPC):
        sch.op("tensor", lambda e, p=pc: e.matmul(
            PS0[:], ones_bf[:], xb[:, p, :],
            start=(p == 0), stop=(p == NPC - 1)),
            waits=[at(sm.xbrv), at(sm.xbrs), at(sm.evv), at(sm.evs)]
            if pc == 0 else [])
    for pc in range(NPC):
        sch.op("tensor", lambda e, p=pc: e.matmul(
            PS1[:], ones_bf[:], xsq[:, p, :],
            start=(p == 0), stop=(p == NPC - 1)),
            inc=(sm.mm, 1) if pc == NPC - 1 else None)
    sch.op("vector", lambda e: e.tensor_scalar(
        sA[:], PS0[:], 1.0 / D, None, ALU.mult), waits=[at(sm.mm)])
    sch.op("vector", lambda e: e.tensor_scalar(
        sB[:], PS1[:], 1.0 / D, None, ALU.mult), inc=(sm.evv, 1))
    sch.op("vector", lambda e: e.tensor_mul(sC[:], sA[:], sA[:]))
    sch.op("vector", lambda e: e.tensor_sub(sB[:], sB[:], sC[:]),
           inc=(sm.dve, 1))
    sch.op("scalar", lambda e: e.activation(
        sC[:], sB[:], AF.Ln, bias=eps_col[:]), waits=[at(sm.dve)])
    sch.op("scalar", lambda e: e.activation(
        sC[:], sC[:], AF.Exp, scale=-0.5), inc=(sm.act, 1))
    sch.op("vector", lambda e: e.scalar_tensor_tensor(
        sA[:], sA[:], -1.0, sC[:], ALU.mult, ALU.mult), waits=[at(sm.act)])
    for pc in range(NPC):
        sch.op("vector", lambda e, p=pc: e.tensor_mul(
            t1[:], xc[:, p, :], sC[:]))
        sch.op("vector", lambda e, p=pc: e.tensor_add(
            xh[:, p, :], t1[:], sA[:]),
            inc=(sm.xh, 1) if pc == NPC - 1 else None)




def emit_l0a(nc, sch, sm, ctx, io):
    xT = io['l0a_xT']
    wqT = io['l0a_wqT']
    wkT = io['l0a_wkT']
    wvT = io['l0a_wvT']
    woT = io['l0a_woT']
    masks = io['l0a_masks']
    ones_h = io['l0a_ones_h']
    x1T = io['l0a_x1T']
    v_scr = io['l0a_v_scr']
    k_scr = io['l0a_k_scr']
    E = ctx.enter_context
    kbuf = E(nc.sbuf_tensor("l0a_kbuf", [128, 2, NPC, 512], BF16))
    kout = E(nc.sbuf_tensor("l0a_kout", [128, NPC, 512], BF16))
    wa = E(nc.sbuf_tensor("l0a_wa", [128, NPC, 1024], BF16))
    wb = E(nc.sbuf_tensor("l0a_wb", [128, NPC, 1024], BF16))
    xc = E(nc.sbuf_tensor("l0a_xc", [128, NPC, 512], F32))
    xb = E(nc.sbuf_tensor("l0a_xb", [128, NPC, 512], BF16))
    xh = E(nc.sbuf_tensor("l0a_xh", [128, NPC, 512], BF16))
    xsq = xh
    qtb = xb
    sc = E(nc.sbuf_tensor("l0a_sc", [128, S], F32))
    att = E(nc.sbuf_tensor("l0a_att", [128, S], BF16))
    attT = E(nc.sbuf_tensor("l0a_attT", [128, 32, 128], BF16,
                            is_transpose_dest=True))
    vbuf = E(nc.sbuf_tensor("l0a_vbuf", [128, 4, D], BF16))
    vout = E(nc.sbuf_tensor("l0a_vout", [128, 2, D], BF16))
    ctxb = E(nc.sbuf_tensor("l0a_ctxb", [128, D], BF16))
    ctxT = E(nc.sbuf_tensor("l0a_ctxT", [128, NPC, 512], BF16,
                            is_transpose_dest=True))
    x1c = E(nc.sbuf_tensor("l0a_x1c", [128, 2, 512], F32))
    tmpA = E(nc.sbuf_tensor("l0a_tmpA", [128, 512], F32))
    tmpB = E(nc.sbuf_tensor("l0a_tmpB", [128, 512], F32))
    tmpC = E(nc.sbuf_tensor("l0a_tmpC", [128, 512], F32))
    abc = E(nc.sbuf_tensor("l0a_abc", [128, 512, 2], F32))
    t1 = E(nc.sbuf_tensor("l0a_t1", [128, 512], F32))
    a_row = E(nc.sbuf_tensor("l0a_a_row", [1, S], F32))
    b_row = E(nc.sbuf_tensor("l0a_b_row", [1, S], F32))
    mrow = E(nc.sbuf_tensor("l0a_mrow", [128, 1], F32))
    srow = E(nc.sbuf_tensor("l0a_srow", [128, 1], F32))
    rrow = E(nc.sbuf_tensor("l0a_rrow", [128, 1], F32))
    maskt = E(nc.sbuf_tensor("l0a_maskt", [128, 4, 512], BF16))
    ones_bf = E(nc.sbuf_tensor("l0a_ones_bf", [128, 128], BF16))
    ones1 = E(nc.sbuf_tensor("l0a_ones1", [1, 128], F32))
    eps_col = E(nc.sbuf_tensor("l0a_eps_col", [128, 1], F32))
    flush_sc = E(nc.sbuf_tensor("l0a_flush_sc", [1, 8], F32))
    P0 = E(nc.psum_tensor("l0a_P0", [128, 512], F32))
    P1 = E(nc.psum_tensor("l0a_P1", [128, 512], F32))
    P2 = E(nc.psum_tensor("l0a_P2", [128, 512], F32))
    P3 = E(nc.psum_tensor("l0a_P3", [128, 512], F32))
    P4 = E(nc.psum_tensor("l0a_P4", [128, 512], F32))
    P5 = E(nc.psum_tensor("l0a_P5", [128, 512], F32))

    VB = [sm.c_vb0, sm.c_vb1, sm.c_vb2, sm.c_vb3]
    KB = [sm.c_kb0, sm.c_kb1]
    VO = [sm.c_vo0, sm.c_vo1]
    X1 = [sm.c_x10, sm.c_x11]

    sch.op("gpsimd", lambda e: e.dma_start(out=ones_bf[:], in_=ones_h),
           inc=(sm.c_const, 16))
    sch.op("gpsimd", lambda e: e.dma_start(out=ones1[:], in_=ones_h[0:1, :]),
           inc=(sm.c_const, 16))
    sch.op("gpsimd", lambda e: e.dma_start(
        out=maskt[:], in_=masks.rearrange("m p t -> p m t")),
        inc=(sm.c_const, 16))
    sch.op("vector", lambda e: e.memset(eps_col[:], EPS), inc=(sm.dve, 1))
    consts = at(sm.c_const)

    sch.op("tensor", lambda e: e.nop(), waits=[consts])
    sch.op("scalar", lambda e: e.nop(), waits=[consts, at(sm.dve)])
    sch.op("vector", lambda e: e.engine_nop(), waits=[consts])
    emit_ln_stats(sch, sm, xT, xc, xb, xsq, P0, P1, tmpA, tmpB, tmpC,
                  a_row, b_row, ones_bf, eps_col)

    sch.op("gpsimd", lambda e: e.dma_start(
        out=wa[:], in_=wkT.rearrange("(c p) n -> p c n", p=128)),
        inc=(sm.c_wa, 16))
    sch.op("gpsimd", lambda e: e.dma_start(
        out=wb[:], in_=wvT.rearrange("(c p) n -> p c n", p=128)),
        inc=(sm.c_wb, 16))

    for tc in range(8):
        tsl = slice(tc * 512, (tc + 1) * 512)
        emit_xhat_chunk(sch, sm, xT, tsl, xc, xh, P2, P3, abc,
                        a_row, b_row, ones1, t1)
        for pc in range(NPC):
            for ic in range(NPC):
                w = [at(sm.xh), at(sm.c_wa), at(sm.evv), at(sm.evs)] \
                    if ic == 0 else []
                sch.op("tensor", lambda e, i=ic, p=pc: e.matmul(
                    P0[:], wa[:, i, p * 128:(p + 1) * 128], xh[:, i, :],
                    start=(i == 0), stop=(i == NPC - 1)), waits=w,
                    inc=(sm.mm, 1) if ic == NPC - 1 else None)
            sch.op("scalar", lambda e, p=pc: e.copy(
                kout[:, p, :], P0[:]),
                waits=[at(sm.mm), at(sm.c_ko)] if pc == 0 else [at(sm.mm)],
                inc=(sm.evs, 1))
        sch.op("gpsimd", lambda e, t=tsl: e.dma_start(
            out=hbm_fm(k_scr, t), in_=kout[:]),
            waits=[at(sm.evs)], inc=(sm.c_ko, 16))
        for tt in range(4):
            vsl = tc * 4 + tt
            for half in range(2):
                for ic in range(NPC):
                    w = [at(sm.c_wb), at(sm.evv), at(sm.evs)] if ic == 0 else []
                    sch.op("tensor", lambda e, i=ic, h=half, t2=tt: e.matmul(
                        P1[:], xh[:, i, t2 * 128:(t2 + 1) * 128],
                        wb[:, i, h * 512:(h + 1) * 512],
                        start=(i == 0), stop=(i == NPC - 1)), waits=w,
                        inc=(sm.mm, 1) if ic == NPC - 1 else None)
                sch.op("scalar", lambda e, h=half, b2=vsl % 2: e.copy(
                    vout[:, b2, h * 512:(h + 1) * 512], P1[:]),
                    waits=[at(sm.mm), at(VO[vsl % 2])], inc=(sm.evs, 1))
            sch.op("gpsimd", lambda e, v=vsl, b2=vsl % 2: e.dma_start(
                out=v_scr[v * 128:(v + 1) * 128, :], in_=vout[:, b2, :]),
                waits=[at(sm.evs)], inc=(VO[vsl % 2], 16))
        sch.op("vector", lambda e: e.engine_nop(), waits=[at(sm.mm)],
               inc=(sm.xcfree, 1))
    scr_ready = [at(sm.c_ko), at(VO[0]), at(VO[1])]

    sch.op("gpsimd", lambda e: e.dma_start(
        out=wa[:], in_=wqT.rearrange("(c p) n -> p c n", p=128)),
        waits=[at(sm.mm), (sm.c_wa, 16)], inc=(sm.c_wa, 16))
    sch.op("gpsimd", lambda e: e.dma_start(
        out=wb[:], in_=woT.rearrange("(c p) n -> p c n", p=128)),
        waits=[at(sm.mm), (sm.c_wb, 16)], inc=(sm.c_wb, 16))
    wq_pt = at(sm.c_wa)
    wo_pt = at(sm.c_wb)

    for g in range(8):
        gsl = slice(g * 512, (g + 1) * 512)
        emit_xhat_chunk(sch, sm, xT, gsl, xc, xh, P2, P3, abc,
                        a_row, b_row, ones1, t1)
        for pc in range(NPC):
            for ic in range(NPC):
                w = [at(sm.xh), wq_pt, at(sm.evv), at(sm.evs)] \
                    if ic == 0 else []
                sch.op("tensor", lambda e, i=ic, p=pc: e.matmul(
                    P0[:], wa[:, i, p * 128:(p + 1) * 128], xh[:, i, :],
                    start=(i == 0), stop=(i == NPC - 1)), waits=w,
                    inc=(sm.mm, 1) if ic == NPC - 1 else None)
            sch.op("scalar", lambda e, p=pc: e.copy(
                qtb[:, p, :], P0[:]), waits=[at(sm.mm)], inc=(sm.evs, 1))
        for qi in range(4):
            qt = g * 4 + qi
            nkc = qt // 4 + 1
            nkt = nkc * 4
            L = nkc * 512
            for kc in range(nkc):
                PS = P0 if kc % 2 == 0 else P1
                sch.op("sync", lambda e, k=kc: e.dma_start(
                    out=kbuf[:, k % 2, :, :],
                    in_=hbm_fm(k_scr, slice(k * 512, (k + 1) * 512))),
                    waits=scr_ready + [lag(sm.evs, 1)],
                    inc=(KB[kc % 2], 16))
                for ic in range(NPC):
                    w = [at(sm.evv), at(sm.evs), at(KB[kc % 2])] \
                        if ic == 0 else []
                    sch.op("tensor", lambda e, i=ic, k=kc, q=qi, p=PS: e.matmul(
                        p[:], qtb[:, i, q * 128:(q + 1) * 128],
                        kbuf[:, k % 2, i, :],
                        start=(i == 0), stop=(i == NPC - 1)), waits=w,
                        inc=(sm.mm, 1) if ic == NPC - 1 else None)
                sch.op("scalar", lambda e, k=kc, p=PS: e.activation(
                    sc[:, k * 512:(k + 1) * 512], p[:], AF.Copy,
                    scale=1.0 / 32.0),
                    waits=[at(sm.mm), at(sm.dve)] if kc == 0 else [at(sm.mm)],
                    inc=(sm.evs, 1))
            sch.op("vector", lambda e, q=qt, l=L: e.tensor_add(
                sc[:, l - 512:l], sc[:, l - 512:l], maskt[:, q % 4, :]),
                waits=[at(sm.evs)])
            sch.op("vector", lambda e, l=L: e.reduce_max(
                mrow[:], sc[:, 0:l], axis=AX.X, negate=True),
                inc=(sm.dve, 1))
            sch.op("scalar", lambda e, l=L: e.activation(
                att[:, 0:l], sc[:, 0:l], AF.Exp, bias=mrow[:],
                accum_out=srow[:]), waits=[at(sm.dve), at(sm.c_trA)],
                inc=(sm.act, 1))
            sch.op("vector", lambda e: e.reciprocal(rrow[:], srow[:]),
                   waits=[at(sm.act)], inc=(sm.dve, 1))
            sch.op("sync", lambda e, n=nkt: e.dma_start(
                out=attT[:, 0:n, :], in_=att[:, 0:n * 128], transpose=True),
                waits=[at(sm.act), at(sm.pv), at(sm.mm)], inc=(sm.c_trA, 16))
            for kt in range(nkt):
                sch.op("sync", lambda e, k=kt: e.dma_start(
                    out=vbuf[:, k % 4, :], in_=v_scr[k * 128:(k + 1) * 128, :]),
                    waits=scr_ready + [lag(sm.pv, 3), lag(sm.mm, 2)],
                    inc=(VB[kt % 4], 16))
                for half in range(2):
                    PC = P4 if half == 0 else P5
                    w = [at(VB[kt % 4]), at(sm.c_trA)]
                    if kt == 0:
                        w += [at(sm.evv), at(sm.evs)]
                    sch.op("tensor", lambda e, k=kt, h=half, p=PC, n=nkt: e.matmul(
                        p[:], attT[:, k, :],
                        vbuf[:, k % 4, h * 512:(h + 1) * 512],
                        start=(k == 0), stop=(k == n - 1)),
                        waits=w if half == 0 else [],
                        inc=((sm.mm, 1) if kt == nkt - 1
                             else (sm.pv, 1)) if half == 1 else None)
            for half in range(2):
                PC = P4 if half == 0 else P5
                sch.op("scalar", lambda e, h=half, p=PC: e.activation(
                    ctxb[:, h * 512:(h + 1) * 512], p[:], AF.Copy,
                    scale=rrow[:]), waits=[at(sm.mm), at(sm.dve)],
                    inc=(sm.evs, 1) if half == 1 else None)
            sch.op("sync", lambda e, q=qi: e.dma_start(
                out=ctxT[:, :, q * 128:(q + 1) * 128], in_=ctxb[:],
                transpose=True), waits=[at(sm.evs), at(sm.mm)],
                inc=(sm.c_trC, 16))
        for pc in range(NPC):
            PS = P2 if pc % 2 == 0 else P3
            for ic in range(NPC):
                w = [at(sm.c_trC), wo_pt, at(sm.evv), at(sm.evs)] \
                    if ic == 0 else []
                sch.op("tensor", lambda e, i=ic, p=pc, ps=PS: e.matmul(
                    ps[:], wb[:, i, p * 128:(p + 1) * 128], ctxT[:, i, :],
                    start=(i == 0), stop=(i == NPC - 1)), waits=w,
                    inc=(sm.mm, 1) if ic == NPC - 1 else None)
            sch.op("vector", lambda e, p=pc, ps=PS: e.tensor_add(
                x1c[:, p % 2, :], ps[:], xc[:, p, :]),
                waits=[at(sm.mm), at(X1[pc % 2])], inc=(sm.evv, 1))
            sch.op("gpsimd", lambda e, p=pc, g2=g: e.dma_start(
                out=hbm_fm(x1T, slice(g2 * 512, (g2 + 1) * 512))[:, p, :],
                in_=x1c[:, p % 2, :]), waits=[at(sm.evv)],
                inc=(X1[pc % 2], 16))
        sch.op("vector", lambda e: e.engine_nop(), waits=[at(sm.mm)],
               inc=(sm.xcfree, 1))

    sch.op("gpsimd", lambda e: e.nop(), waits=[at(X1[0]), at(X1[1])])


def emit_l0f(nc, sch, sm, ctx, io):
    xT = io['l0f_xT']
    wgT = io['l0f_wgT']
    wuT = io['l0f_wuT']
    wdT = io['l0f_wdT']
    ones_h = io['l0f_ones_h']
    yT = io['l0f_yT']
    NF = 24
    E = ctx.enter_context
    wgb = E(nc.sbuf_tensor("l0f_wgb", [128, NPC, 3072], BF16))
    wub = E(nc.sbuf_tensor("l0f_wub", [128, NPC, 3072], BF16))
    wdb = E(nc.sbuf_tensor("l0f_wdb", [128, NF, D], BF16))
    hbuf = E(nc.sbuf_tensor("l0f_hbuf", [128, NF, 512], BF16))
    xc = E(nc.sbuf_tensor("l0f_xc", [128, NPC, 512], F32))
    xh = E(nc.sbuf_tensor("l0f_xh", [128, NPC, 512], BF16))
    scr = E(nc.sbuf_tensor("l0f_scr", [128, 512, 3], F32))
    t1 = E(nc.sbuf_tensor("l0f_t1", [128, 512], F32))
    sil = E(nc.sbuf_tensor("l0f_sil", [128, 512], BF16))
    yc = E(nc.sbuf_tensor("l0f_yc", [128, 2, 512], F32))
    ones_bf = E(nc.sbuf_tensor("l0f_ones_bf", [128, 128], BF16))
    eps_col = E(nc.sbuf_tensor("l0f_eps_col", [128, 1], F32))
    flush_sc = E(nc.sbuf_tensor("l0f_flush_sc", [1, 8], F32))
    P0 = E(nc.psum_tensor("l0f_P0", [128, 512], F32))
    P1 = E(nc.psum_tensor("l0f_P1", [128, 512], F32))
    P2 = E(nc.psum_tensor("l0f_P2", [128, 512], F32))
    P3 = E(nc.psum_tensor("l0f_P3", [128, 512], F32))
    X1 = [sm.c_x10, sm.c_x11]

    sch.op("gpsimd", lambda e: e.dma_start(out=ones_bf[:], in_=ones_h),
           inc=(sm.c_const, 16))
    sch.op("gpsimd", lambda e: e.dma_start(
        out=wgb[:], in_=wgT.rearrange("(c p) n -> p c n", p=128)),
        inc=(sm.c_wa, 16))
    sch.op("gpsimd", lambda e: e.dma_start(
        out=wub[:], in_=wuT.rearrange("(c p) n -> p c n", p=128)),
        inc=(sm.c_wa, 16))
    sch.op("gpsimd", lambda e: e.dma_start(
        out=wdb[:], in_=wdT.rearrange("(c p) n -> p c n", p=128)),
        inc=(sm.c_wa, 16))
    sch.op("vector", lambda e: e.memset(eps_col[:], EPS), inc=(sm.dve, 1))
    sch.op("tensor", lambda e: e.nop(), waits=[at(sm.c_const), at(sm.c_wa)])
    sch.op("scalar", lambda e: e.nop(), waits=[at(sm.dve)])

    for tc in range(8):
        tsl = slice(tc * 512, (tc + 1) * 512)
        emit_ln_xhat(sch, sm, xT, tsl, xc,
                     hbuf.ap()[:, 0:NPC, :], xh, xh, P0, P1, scr,
                     ones_bf, eps_col, t1)
        for ft in range(NF):
            PA = P0 if ft % 2 == 0 else P2
            PB = P1 if ft % 2 == 0 else P3
            for ic in range(NPC):
                w = [at(sm.xh), at(sm.evv), at(sm.evs)] if ic == 0 else []
                sch.op("tensor", lambda e, i=ic, f=ft, p=PA: e.matmul(
                    p[:], wgb[:, i, f * 128:(f + 1) * 128], xh[:, i, :],
                    start=(i == 0), stop=(i == NPC - 1)), waits=w,
                    inc=(sm.mm, 1) if ic == NPC - 1 else None)
            for ic in range(NPC):
                sch.op("tensor", lambda e, i=ic, f=ft, p=PB: e.matmul(
                    p[:], wub[:, i, f * 128:(f + 1) * 128], xh[:, i, :],
                    start=(i == 0), stop=(i == NPC - 1)),
                    inc=(sm.mm, 1) if ic == NPC - 1 else None)
            sch.op("scalar", lambda e, p=PA: e.activation(
                sil[:], p[:], AF.Silu, scale=1.0 / 127.0),
                waits=[at(sm.mm), at(sm.evv)], inc=(sm.evs, 1))
            sch.op("vector", lambda e, f=ft, p=PB: e.tensor_mul(
                hbuf[:, f, :], sil[:], p[:]),
                waits=[at(sm.evs)], inc=(sm.evv, 1))
        for rnd in range(2):
            for ft in range(NF):
                for pq in range(4):
                    pc = rnd * 4 + pq
                    PD = [P0, P1, P2, P3][pq]
                    w = [at(sm.evv), at(sm.evs)] if (ft == 0 and pq == 0) else []
                    sch.op("tensor", lambda e, f=ft, p=pc, ps=PD: e.matmul(
                        ps[:], wdb[:, f, p * 128:(p + 1) * 128],
                        hbuf[:, f, :],
                        start=(f == 0), stop=(f == NF - 1)), waits=w,
                        inc=(sm.mm, 1) if (ft == NF - 1 and pq == 3) else None)
            for pq in range(4):
                pc = rnd * 4 + pq
                PD = [P0, P1, P2, P3][pq]
                sch.op("vector", lambda e, ps=PD: e.tensor_scalar(
                    t1[:], ps[:], 1.0 / 16129.0, None, ALU.mult),
                    waits=[at(sm.mm), at(sm.evs)])
                sch.op("vector", lambda e, p=pc: e.tensor_add(
                    t1[:], t1[:], xc[:, p, :]), inc=(sm.evv, 1))
                sch.op("scalar", lambda e, b2=pc % 2: e.activation(
                    yc[:, b2, :], t1[:], AF.Copy),
                    waits=[at(sm.evv), at(X1[pc % 2])], inc=(sm.evs, 1))
                sch.op("gpsimd", lambda e, p=pc, t=tsl, b2=pc % 2: e.dma_start(
                    out=hbm_fm(yT, t)[:, p, :], in_=yc[:, b2, :]),
                    waits=[at(sm.evs)], inc=(X1[pc % 2], 16))
        sch.op("vector", lambda e: e.engine_nop(), waits=[at(sm.mm)],
               inc=(sm.xcfree, 1))
    sch.op("gpsimd", lambda e: e.nop(), waits=[at(X1[0]), at(X1[1])])


def emit_l1a(nc, sch, sm, ctx, io):
    xT = io['l1a_xT']
    wq = io['l1a_wq']
    wk = io['l1a_wk']
    wv = io['l1a_wv']
    wo = io['l1a_wo']
    bq = io['l1a_bq']
    bk = io['l1a_bk']
    bvr = io['l1a_bvr']
    bo = io['l1a_bo']
    aqkv = io['l1a_aqkv']
    ones_h = io['l1a_ones_h']
    eye_h = io['l1a_eye_h']
    blk_h = io['l1a_blk_h']
    sel_h = io['l1a_sel_h']
    x3T = io['l1a_x3T']
    q_scr = io['l1a_q_scr']
    k_scr = io['l1a_k_scr']
    v_scr = io['l1a_v_scr']
    c_scr = io['l1a_c_scr']
    H, DH = 16, 64
    E = ctx.enter_context
    wsb = [E(nc.sbuf_tensor("wsb%d" % i, [128, NPC, D], BF16))
           for i in range(3)]
    xc = E(nc.sbuf_tensor("l1a_xc", [128, NPC, 512], F32))
    xh = E(nc.sbuf_tensor("l1a_xh", [128, NPC, 512], BF16))
    xq2 = E(nc.sbuf_tensor("l1a_xq2", [128, NPC, 512], BF16))
    scr = E(nc.sbuf_tensor("l1a_scr", [128, 512, 3], F32))
    t1 = E(nc.sbuf_tensor("l1a_t1", [128, 512], F32))
    s1k = E(nc.sbuf_tensor("l1a_s1k", [128, 1024], F32))
    s2k = E(nc.sbuf_tensor("l1a_s2k", [128, 1024], F32))
    yt = E(nc.sbuf_tensor("l1a_yt", [128, 1024], BF16))
    ysq = E(nc.sbuf_tensor("l1a_ysq", [128, 512], BF16))
    wsqsc = E(nc.sbuf_tensor("l1a_wsqsc", [128, 1024], BF16))
    xsqbc = E(nc.sbuf_tensor("l1a_xsqbc", [128, 512], F32))
    xsqT = E(nc.sbuf_tensor("l1a_xsqT", [128, 4], F32))
    wsqe = E(nc.sbuf_tensor("l1a_wsqe", [128, NPC, 2], F32))
    wsqvb = E(nc.sbuf_tensor("l1a_wsqvb", [128, 1024], F32))
    bvb = E(nc.sbuf_tensor("l1a_bvb", [128, 1024], F32))
    bcols = E(nc.sbuf_tensor("l1a_bcols", [128, 2, NPC], F32))
    bscal = E(nc.sbuf_tensor("l1a_bscal", [128, 2, NPC], F32))
    bocol = E(nc.sbuf_tensor("l1a_bocol", [128, NPC, 1], F32))
    scal = E(nc.sbuf_tensor("l1a_scal", [128, 3], F32))
    aqs = E(nc.sbuf_tensor("l1a_aqs", [1, 3], F32))
    qn_sb = E(nc.sbuf_tensor("l1a_qn_sb", [16, S], BF16))
    kn_sb = E(nc.sbuf_tensor("l1a_kn_sb", [16, S], BF16))
    qnT = E(nc.sbuf_tensor("l1a_qnT", [128, 32, 16], F32))
    kbc = E(nc.sbuf_tensor("l1a_kbc", [128, NPC, 512], BF16))
    qh_sb = E(nc.sbuf_tensor("l1a_qh_sb", [64, S], BF16))
    kh_sb = E(nc.sbuf_tensor("l1a_kh_sb", [64, S], BF16))
    vh_sb = E(nc.sbuf_tensor("l1a_vh_sb", [128, 32, DH], BF16))
    attb = E(nc.sbuf_tensor("l1a_attb", [128, S], BF16))
    rb = attb
    attTg = E(nc.sbuf_tensor("l1a_attTg", [128, 32, 512], BF16,
                             is_transpose_dest=True))
    ctxc = E(nc.sbuf_tensor("l1a_ctxc", [64, 512], BF16))
    mrow = E(nc.sbuf_tensor("l1a_mrow", [128, 1], F32))
    srow = E(nc.sbuf_tensor("l1a_srow", [128, 1], F32))
    rrow = E(nc.sbuf_tensor("l1a_rrow", [128, 1], F32))
    yc = E(nc.sbuf_tensor("l1a_yc", [128, 2, 512], F32))
    ones_bf = E(nc.sbuf_tensor("l1a_ones_bf", [128, 128], BF16))
    ones1 = E(nc.sbuf_tensor("l1a_ones1", [1, 128], F32))
    eyeb = E(nc.sbuf_tensor("l1a_eyeb", [128, 128], BF16))
    blkb = E(nc.sbuf_tensor("l1a_blkb", [128, NPC, 16], BF16))
    selb = E(nc.sbuf_tensor("l1a_selb", [16, 2048], BF16))
    eps_col = E(nc.sbuf_tensor("l1a_eps_col", [128, 1], F32))
    flush_sc = E(nc.sbuf_tensor("l1a_flush_sc", [1, 8], F32))
    epz_col = E(nc.sbuf_tensor("l1a_epz_col", [128, 1], F32))
    P0 = E(nc.psum_tensor("l1a_P0", [128, 512], F32))
    P1 = E(nc.psum_tensor("l1a_P1", [128, 512], F32))
    P2 = E(nc.psum_tensor("l1a_P2", [128, 512], F32))
    P3 = E(nc.psum_tensor("l1a_P3", [128, 512], F32))
    P4 = E(nc.psum_tensor("l1a_P4", [128, 512], F32))
    PN = E(nc.psum_tensor("l1a_PN", [16, 512], F32))
    PC6 = E(nc.psum_tensor("l1a_PC6", [64, 512], F32))
    sqb = xh.ap().rearrange("p c t -> p (c t)")
    db = xq2.ap().rearrange("p c t -> p (c t)")
    X1 = [sm.c_x10, sm.c_x11]

    for fn in (lambda e: e.dma_start(out=ones_bf[:], in_=ones_h),
               lambda e: e.dma_start(out=ones1[:], in_=ones_h[0:1, :]),
               lambda e: e.dma_start(out=eyeb[:], in_=eye_h),
               lambda e: e.dma_start(
                   out=blkb[:], in_=blk_h.rearrange("c p n -> p c n")),
               lambda e: e.dma_start(out=selb[:], in_=sel_h),
               lambda e: e.dma_start(out=aqs[:], in_=aqkv),
               lambda e: e.dma_start(
                   out=bcols[:, 0, :],
                   in_=bq.rearrange("(c p) o -> p (c o)", p=128),
                   allow_slow_non_contiguous=True),
               lambda e: e.dma_start(
                   out=bcols[:, 1, :],
                   in_=bk.rearrange("(c p) o -> p (c o)", p=128),
                   allow_slow_non_contiguous=True),
               lambda e: e.dma_start(
                   out=bocol.ap().rearrange("p c o -> p (c o)"),
                   in_=bo.rearrange("(c p) o -> p (c o)", p=128),
                   allow_slow_non_contiguous=True),
               ):
        sch.op("gpsimd", fn, inc=(sm.c_const, 16))
    sch.op("sync", lambda e: e.dma_start(out=s2k[0:1, 0:1024], in_=bvr),
           inc=(sm.c_d0, 16))
    for i, w_h in enumerate((wq, wk, wv)):
        sch.op("gpsimd", lambda e, wh=w_h, j=i: e.dma_start(
            out=wsb[j][:], in_=wh.rearrange("(c p) n -> p c n", p=128)),
            inc=(sm.c_wa, 16))
    sch.op("vector", lambda e: e.memset(eps_col[:], EPS))
    sch.op("vector", lambda e: e.memset(epz_col[:], 1e-12), inc=(sm.dve, 1))
    consts = at(sm.c_const)
    sch.op("tensor", lambda e: e.nop(),
           waits=[consts, at(sm.c_wa), at(sm.c_d0)])
    sch.op("scalar", lambda e: e.nop(),
           waits=[consts, at(sm.dve), at(sm.c_wa)])
    sch.op("vector", lambda e: e.engine_nop(), waits=[consts])

    for j in range(3):
        for ic in range(NPC):
            sch.op("scalar", lambda e, i=ic, w=wsb[j]: e.activation(
                wsqsc[:], w[:, i, :], AF.Square),
                waits=[at(sm.mm)], inc=(sm.act, 1))
            for half in range(2):
                PS = P0 if half == 0 else P1
                sch.op("tensor", lambda e, h=half, i=ic, p=PS: e.matmul(
                    p[:], ones_bf[:], wsqsc[:, h * 512:(h + 1) * 512],
                    start=(i == 0), stop=(i == NPC - 1)),
                    waits=([at(sm.act), at(sm.evv), at(sm.evs)]
                           if half == 0 else [at(sm.act)]),
                    inc=(sm.mm, 1) if half == 1 else None)
        sch.op("vector", lambda e: e.tensor_copy(s1k[:, 0:512], P0[:]),
               waits=[at(sm.mm)])
        sch.op("vector", lambda e: e.tensor_copy(s1k[:, 512:1024], P1[:]),
               inc=(sm.evv, 1))
        if j < 2:
            for pc in range(NPC):
                sch.op("tensor", lambda e, p=pc: e.matmul(
                    P2[:, 0:1], s1k[0:1, p * 128:(p + 1) * 128],
                    ones1[0:1, 0:1], start=True, stop=True),
                    waits=[at(sm.evv), at(sm.evs)], inc=(sm.mm, 1))
                sch.op("vector", lambda e, p=pc, j2=j: e.tensor_scalar(
                    wsqe[:, p, j2:j2 + 1], P2[:, 0:1], 1.0, eps_col[:],
                    ALU.mult, ALU.add),
                    waits=[at(sm.mm)], inc=(sm.evs, 1))
        else:
            sch.op("vector", lambda e: e.tensor_scalar(
                wsqvb[:, 0:512], s1k[:, 0:512], 1.0, eps_col[:],
                ALU.mult, ALU.add))
            sch.op("vector", lambda e: e.tensor_scalar(
                wsqvb[:, 512:1024], s1k[:, 512:1024], 1.0, eps_col[:],
                ALU.mult, ALU.add), inc=(sm.evv, 1))
    sch.op("scalar", lambda e: e.activation(
        aqs[:], aqs[:], AF.Exp, scale=LN_C), inc=(sm.act, 1))
    for j in range(3):
        sch.op("tensor", lambda e, j2=j: e.matmul(
            P2[:, 0:1], ones1[:], aqs[0:1, j2:j2 + 1],
            start=True, stop=True),
            waits=[at(sm.act), at(sm.evs), at(sm.evv)], inc=(sm.mm, 1))
        sch.op("vector", lambda e, j2=j: e.tensor_copy(
            scal[:, j2:j2 + 1], P2[:, 0:1]),
            waits=[at(sm.mm)], inc=(sm.evs, 1))
    for j in range(2):
        sch.op("vector", lambda e, j2=j: e.tensor_scalar(
            bscal[:, j2, :], bcols[:, j2, :], scal[:, j2:j2 + 1], None,
            ALU.mult), waits=[at(sm.evs)], inc=(sm.dve, 1))
    for half in range(2):
        PS = P0 if half == 0 else P1
        sch.op("tensor", lambda e, h=half, p=PS: e.matmul(
            p[:], ones1[:], s2k[0:1, h * 512:(h + 1) * 512],
            start=True, stop=True),
            waits=[at(sm.evv), at(sm.evs)],
            inc=(sm.mm, 1) if half == 1 else None)
    sch.op("vector", lambda e: e.tensor_copy(bvb[:, 0:512], P0[:]),
           waits=[at(sm.mm)])
    sch.op("vector", lambda e: e.tensor_copy(bvb[:, 512:1024], P1[:]),
           inc=(sm.evv, 1))

    for tc in range(8):
        tsl = slice(tc * 512, (tc + 1) * 512)
        emit_ln_xhat(sch, sm, xT, tsl, xc, xq2, xh, xh, P0, P1, scr,
                     ones_bf, eps_col, t1)
        sch.op("scalar", lambda e: e.activation(xq2[:], xh[:], AF.Square),
               waits=[at(sm.xh), at(sm.mm)], inc=(sm.act, 1))
        for pc in range(NPC):
            sch.op("tensor", lambda e, p=pc: e.matmul(
                P0[:], ones_bf[:], xq2[:, p, :],
                start=(p == 0), stop=(p == NPC - 1)),
                waits=[at(sm.act), at(sm.evv), at(sm.evs)] if pc == 0 else [],
                inc=(sm.mm, 1) if pc == NPC - 1 else None)
        sch.op("vector", lambda e: e.tensor_copy(xsqbc[:], P0[:]),
               waits=[at(sm.mm)], inc=(sm.evv, 1))
        for tt in range(4):
            sch.op("tensor", lambda e, t2=tt: e.matmul(
                P1[:, 0:1], xsqbc[0:1, t2 * 128:(t2 + 1) * 128],
                ones1[0:1, 0:1], start=True, stop=True),
                waits=[at(sm.evv), at(sm.evs)], inc=(sm.mm, 1))
            sch.op("vector", lambda e, t2=tt: e.tensor_copy(
                xsqT[:, t2:t2 + 1], P1[:, 0:1]),
                waits=[at(sm.mm)], inc=(sm.evs, 1))
        for j in range(2):
            dsth = (q_scr, k_scr)[j]
            nrm = (qn_sb, kn_sb)[j]
            dsem = (sm.c_d1, sm.c_d2)[j]
            for pc in range(NPC):
                PD = P2 if pc % 2 == 0 else P3
                for ic in range(NPC):
                    w = [at(sm.evv), at(sm.evs), at(sm.xh)] if ic == 0 else []
                    sch.op("tensor",
                           lambda e, i=ic, p=pc, j2=j, ps=PD: e.matmul(
                               ps[:], wsb[j2][:, i, p * 128:(p + 1) * 128],
                               xh[:, i, :],
                               start=(i == 0), stop=(i == NPC - 1)),
                           waits=w,
                           inc=(sm.mm, 1) if ic == NPC - 1 else None)
                sch.op("vector", lambda e, p=pc, j2=j, ps=PD: e.tensor_scalar(
                    scr[:, :, 0], ps[:], -2.0, wsqe[:, p, j2:j2 + 1],
                    ALU.mult, ALU.add), waits=[at(sm.mm), at(sm.dve)],
                    inc=(sm.evv, 1))
                sch.op("scalar", lambda e, ps=PD: e.activation(
                    ysq[:], ps[:], AF.Square),
                    waits=[at(sm.mm), at(sm.act), at(sm.evv)],
                    inc=(sm.evs, 1))
                sch.op("vector", lambda e: e.tensor_add(
                    scr[:, :, 0], scr[:, :, 0], xsqbc[:]),
                    inc=(sm.evv, 1))
                sch.op("scalar", lambda e: recip(e, scr[:, :, 1],
                                                 scr[:, :, 0]),
                       waits=[at(sm.evv)], inc=(sm.act, 1))
                sch.op("vector", lambda e: e.tensor_mul(
                    scr[:, :, 2], ysq[:], scr[:, :, 1]),
                    waits=[at(sm.act)], inc=(sm.dve, 1))
                sch.op("scalar", lambda e, p=pc, j2=j: e.activation(
                    yt[:, 0:512], scr[:, :, 2], AF.Identity,
                    scale=scal[:, j2:j2 + 1], bias=bscal[:, j2, p:p + 1]),
                    waits=[at(dsem), at(sm.c_d3), at(sm.dve)],
                    inc=(sm.act, 1))
                sch.op("gpsimd", lambda e, p=pc, t=tsl, d=dsth: e.dma_start(
                    out=d[p * 128:(p + 1) * 128, t], in_=yt[:, 0:512]),
                    waits=[at(sm.dve)], inc=(dsem, 16))
                sch.op("scalar", lambda e: e.activation(
                    ysq[:], yt[:, 0:512], AF.Square),
                    waits=[at(sm.dve)], inc=(sm.act, 1))
                sch.op("tensor", lambda e, p=pc: e.matmul(
                    PN[:], blkb[:, p, :], ysq[:],
                    start=(p == 0), stop=(p == NPC - 1)),
                    waits=([at(sm.act), at(sm.evs)] if pc == 0
                           else [at(sm.act)]),
                    inc=(sm.mm, 1))
            sch.op("vector", lambda e, t=tsl, n=nrm: e.tensor_copy(
                n[0:16, t], PN[:]), waits=[at(sm.mm)], inc=(sm.evs, 1))
        for tt in range(4):
            for half in range(2):
                PD = P2 if half == 0 else P3
                for ic in range(NPC):
                    w = [at(sm.evv), at(sm.evs)] if ic == 0 else []
                    sch.op("tensor", lambda e, i=ic, t2=tt, h=half, ps=PD:
                           e.matmul(ps[:],
                                    xh[:, i, t2 * 128:(t2 + 1) * 128],
                                    wsb[2][:, i, h * 512:(h + 1) * 512],
                                    start=(i == 0), stop=(i == NPC - 1)),
                           waits=w,
                           inc=(sm.mm, 1) if ic == NPC - 1 else None)
            for half in range(2):
                PD = P2 if half == 0 else P3
                hs = slice(half * 512, (half + 1) * 512)
                sch.op("vector", lambda e, t2=tt, ps=PD, h=hs: e.tensor_scalar(
                    s1k[:, h], ps[:], -2.0, xsqT[:, t2:t2 + 1],
                    ALU.mult, ALU.add), waits=[at(sm.mm), at(sm.dve)],
                    inc=(sm.evv, 1))
                sch.op("scalar", lambda e, ps=PD, h=hs: e.activation(
                    yt[:, h], ps[:], AF.Square),
                    waits=[at(sm.mm), at(sm.act), at(sm.evv),
                           at(sm.c_d1), at(sm.c_d2)],
                    inc=(sm.evs, 1))
            sch.op("vector", lambda e: e.tensor_add(
                s1k[:], s1k[:], wsqvb[:]), inc=(sm.evv, 1))
            sch.op("scalar", lambda e: recip(e, s2k[:], s1k[:]),
                   waits=[at(sm.evv)], inc=(sm.act, 1))
            sch.op("vector", lambda e: e.tensor_mul(s1k[:], yt[:, 0:1024],
                                                    s2k[:]),
                   waits=[at(sm.act)])
            sch.op("vector", lambda e: e.tensor_add(
                s1k[:], s1k[:], bvb[:]), inc=(sm.dve, 1))
            sch.op("scalar", lambda e: e.activation(
                yt[:, 0:1024], s1k[:], AF.Identity, scale=scal[:, 2:3]),
                waits=[at(sm.c_d3), at(sm.dve)], inc=(sm.act, 1))
            sch.op("gpsimd", lambda e, t2=tt, c=tc: e.dma_start(
                out=v_scr[(c * 4 + t2) * 128:(c * 4 + t2 + 1) * 128, :],
                in_=yt[:, 0:1024]),
                waits=[at(sm.act)], inc=(sm.c_d3, 16))
    for tt in range(32):
        sch.op("tensor", lambda e, t2=tt: e.matmul(
            P2[:, 0:16], qn_sb[0:16, t2 * 128:(t2 + 1) * 128],
            eyeb[0:16, 0:16], start=True, stop=True),
            waits=[at(sm.evv), at(sm.evs)], inc=(sm.mm, 1))
        sch.op("vector", lambda e, t2=tt: e.tensor_scalar(
            qnT[:, t2, :], P2[:, 0:16], 1.0, eps_col[:],
            ALU.mult, ALU.add), waits=[at(sm.mm)], inc=(sm.evs, 1))
    pP_done = [at(sm.c_d1), at(sm.c_d2), at(sm.c_d3)]

    for h in range(H):
        hs = slice(h * 64, (h + 1) * 64)
        sch.op("sync", lambda e, s_=hs: e.dma_start(
            out=qh_sb[:], in_=q_scr[s_, :]),
            waits=pP_done + [at(sm.mm)], inc=(sm.c_d4, 16))
        sch.op("sync", lambda e, s_=hs: e.dma_start(
            out=kh_sb[:], in_=k_scr[s_, :]), inc=(sm.c_d4, 16))
        sch.op("sync", lambda e, s_=hs: e.dma_start(
            out=vh_sb[:], in_=v_scr[:, s_].rearrange(
                "(t p) c -> p t c", p=128)), inc=(sm.c_d4, 16))
        qkv_pt = at(sm.c_d4)
        for kc in range(NPC):
            sch.op("tensor", lambda e, k=kc, h2=h: e.matmul(
                P4[:], selb[:, h2 * 128:(h2 + 1) * 128],
                kn_sb[0:16, k * 512:(k + 1) * 512],
                start=True, stop=True),
                waits=[at(sm.evv), at(sm.evs)], inc=(sm.mm, 1))
            sch.op("scalar", lambda e, k=kc: e.copy(
                kbc[:, k, :], P4[:]), waits=[at(sm.mm), at(sm.evv)],
                inc=(sm.evs, 1))
        for qg in range(8):
            for qi in range(4):
                qt = qg * 4 + qi
                for kc in range(NPC):
                    PS = P0 if kc % 2 == 0 else P1
                    sch.op("tensor", lambda e, k=kc, q=qt, ps=PS: e.matmul(
                        ps[:], qh_sb[:, q * 128:(q + 1) * 128],
                        kh_sb[:, k * 512:(k + 1) * 512],
                        start=True, stop=True),
                        waits=[qkv_pt, at(sm.evs)], inc=(sm.mm, 1))
                    sch.op("scalar", lambda e, k=kc, ps=PS: e.activation(
                        sqb[:, k * 512:(k + 1) * 512], ps[:], AF.Square),
                        waits=([at(sm.mm), at(sm.dve)] if kc == 0
                               else [at(sm.mm)]),
                        inc=(sm.evs, 1))
                sch.op("vector", lambda e, q=qt, h2=h: e.tensor_scalar(
                    db[:], sqb[:], -2.0, qnT[:, q, h2:h2 + 1],
                    ALU.mult, ALU.add), waits=[at(sm.evs)])
                sch.op("vector", lambda e: e.tensor_add(
                    db[:], db[:], kbc.ap().rearrange("p c t -> p (c t)")),
                    inc=(sm.evv, 1))
                sch.op("scalar", lambda e: recip(e, rb[:], db[:],
                                                 bias=epz_col[:]),
                       waits=[at(sm.evv), at(sm.c_d5)], inc=(sm.act, 1))
                sch.op("vector", lambda e: e.tensor_mul(
                    db[:], sqb[:], rb[:]), waits=[at(sm.act)])
                sch.op("vector", lambda e: e.reduce_max(
                    mrow[:], db[:], axis=AX.X, negate=True),
                    inc=(sm.dve, 1))
                sch.op("scalar", lambda e: e.activation(
                    attb[:], db[:], AF.Exp, bias=mrow[:],
                    accum_out=srow[:]),
                    waits=[at(sm.dve), at(sm.c_d5)], inc=(sm.act, 1))
                sch.op("vector", lambda e: e.reciprocal(rrow[:], srow[:]),
                       waits=[at(sm.act)], inc=(sm.dve, 1))
                sch.op("scalar", lambda e: e.activation(
                    attb[:], attb[:], AF.Copy, scale=rrow[:]),
                    waits=[at(sm.dve)], inc=(sm.act, 1))
                sch.op("sync", lambda e, q=qi: e.dma_start(
                    out=attTg[:, :, q * 128:(q + 1) * 128], in_=attb[:],
                    transpose=True),
                    waits=[at(sm.act), at(sm.mm)], inc=(sm.c_d5, 16))
            for kt in range(32):
                sch.op("tensor", lambda e, k=kt: e.matmul(
                    PC6[:], vh_sb[:, k, :], attTg[:, k, :],
                    start=(k == 0), stop=(k == 31)),
                    waits=([qkv_pt, at(sm.c_d5), at(sm.evs)] if kt == 0
                           else []),
                    inc=(sm.mm, 1) if kt == 31 else None)
            sch.op("scalar", lambda e: e.copy(ctxc[:], PC6[:]),
                   waits=[at(sm.mm), at(sm.c_d6)], inc=(sm.evs, 1))
            sch.op("gpsimd", lambda e, s_=hs, g=qg: e.dma_start(
                out=c_scr[s_, g * 512:(g + 1) * 512], in_=ctxc[:]),
                waits=[at(sm.evs)], inc=(sm.c_d6, 16))

    sch.op("gpsimd", lambda e: e.dma_start(
        out=wsb[0][:], in_=wo.rearrange("(c p) n -> p c n", p=128)),
        waits=[at(sm.mm)], inc=(sm.c_wb, 16))
    ctx_pt = at(sm.c_d6)
    for g in range(8):
        gsl = slice(g * 512, (g + 1) * 512)
        sch.op("sync", lambda e, t=gsl: e.dma_start(
            out=xh[:], in_=hbm_fm(c_scr, t)),
            waits=[ctx_pt, at(sm.mm)], inc=(sm.c_d0, 16))
        sch.op("sync", lambda e, t=gsl: e.dma_start(
            out=xc[:], in_=hbm_fm(xT, t)), waits=[at(sm.xcfree)],
            inc=(sm.c_xc, 16))
        for pc in range(NPC):
            PS = P2 if pc % 2 == 0 else P3
            for ic in range(NPC):
                w = [at(sm.c_d0), at(sm.c_wb), at(sm.evv), at(sm.evs)] \
                    if ic == 0 else []
                sch.op("tensor", lambda e, i=ic, p=pc, ps=PS: e.matmul(
                    ps[:], wsb[0][:, i, p * 128:(p + 1) * 128],
                    xh[:, i, :],
                    start=(i == 0), stop=(i == NPC - 1)), waits=w,
                    inc=(sm.mm, 1) if ic == NPC - 1 else None)
            sch.op("scalar", lambda e, p=pc, ps=PS: e.activation(
                t1[:], ps[:], AF.Identity, bias=bocol[:, p, 0:1]),
                waits=[at(sm.mm), at(sm.evv)], inc=(sm.evs, 1))
            sch.op("vector", lambda e, p=pc: e.tensor_add(
                t1[:], t1[:], xc[:, p, :]),
                waits=[at(sm.evs), at(sm.c_xc)], inc=(sm.evv, 1))
            sch.op("scalar", lambda e, p=pc: e.activation(
                yc[:, p % 2, :], t1[:], AF.Copy),
                waits=[at(sm.evv), at(X1[pc % 2])], inc=(sm.evs, 1))
            sch.op("gpsimd", lambda e, p=pc, t=gsl: e.dma_start(
                out=hbm_fm(x3T, t)[:, p, :], in_=yc[:, p % 2, :]),
                waits=[at(sm.evs)], inc=(X1[pc % 2], 16))
        sch.op("vector", lambda e: e.engine_nop(), waits=[at(sm.mm)],
               inc=(sm.xcfree, 1))
    sch.op("gpsimd", lambda e: e.nop(), waits=[at(X1[0]), at(X1[1])])


def emit_l1f(nc, sch, sm, ctx, io):
    xT = io['l1f_xT']
    cenT = io['l1f_cenT']
    lngT = io['l1f_lngT']
    lnbT = io['l1f_lnbT']
    w1S = io['l1f_w1S']
    w2T = io['l1f_w2T']
    b1T = io['l1f_b1T']
    b2T = io['l1f_b2T']
    iot_h = io['l1f_iot_h']
    sel8_h = io['l1f_sel8_h']
    ones_h = io['l1f_ones_h']
    eye_h = io['l1f_eye_h']
    outT = io['l1f_outT']
    NF = 32
    E = ctx.enter_context
    hbuf = E(nc.sbuf_tensor("l1f_hbuf", [128, NF, 512], BF16))
    xc = E(nc.sbuf_tensor("l1f_xc", [128, NPC, 512], F32))
    xhat = E(nc.sbuf_tensor("l1f_xhat", [128, NPC, 512], BF16))
    xn = E(nc.sbuf_tensor("l1f_xn", [128, NPC, 512], BF16))
    scr = E(nc.sbuf_tensor("l1f_scr", [128, 512, 3], F32))
    t1 = E(nc.sbuf_tensor("l1f_t1", [128, 512], F32))
    t2 = E(nc.sbuf_tensor("l1f_t2", [128, 512], F32))
    w1s = E(nc.sbuf_tensor("l1f_w1s", [128, 4, NPC, 128], BF16))
    w2s = E(nc.sbuf_tensor("l1f_w2s", [128, 4, D], BF16))
    cenb = E(nc.sbuf_tensor("l1f_cenb", [128, NPC, 8], F32))
    lng = E(nc.sbuf_tensor("l1f_lng", [128, NPC, 8], F32))
    lnb = E(nc.sbuf_tensor("l1f_lnb", [128, NPC, 8], F32))
    b1c = E(nc.sbuf_tensor("l1f_b1c", [128, NF, 8], F32))
    b2c = E(nc.sbuf_tensor("l1f_b2c", [128, NPC, 8], F32))
    afft = E(nc.sbuf_tensor("l1f_afft", [128, 4, 8], F32))
    gmask = E(nc.sbuf_tensor("l1f_gmask", [128, 4, 8], F32))
    gE = E(nc.sbuf_tensor("l1f_gE", [8, 512], F32))
    gbc = E(nc.sbuf_tensor("l1f_gbc", [128, 512], F32))
    yco = E(nc.sbuf_tensor("l1f_yco", [128, 2, 512], F32))
    m8 = E(nc.sbuf_tensor("l1f_m8", [128, 1], F32))
    iot = E(nc.sbuf_tensor("l1f_iot", [128, 8], F32))
    sel8 = E(nc.sbuf_tensor("l1f_sel8", [8, 1024], F32))
    ones_bf = E(nc.sbuf_tensor("l1f_ones_bf", [128, 128], BF16))
    eyef = E(nc.sbuf_tensor("l1f_eyef", [128, 128], F32))
    eps_col = E(nc.sbuf_tensor("l1f_eps_col", [128, 1], F32))
    flush_sc = E(nc.sbuf_tensor("l1f_flush_sc", [1, 8], F32))
    P0 = E(nc.psum_tensor("l1f_P0", [128, 512], F32))
    P1 = E(nc.psum_tensor("l1f_P1", [128, 512], F32))
    P2 = E(nc.psum_tensor("l1f_P2", [128, 512], F32))
    P3 = E(nc.psum_tensor("l1f_P3", [128, 512], F32))
    P4 = E(nc.psum_tensor("l1f_P4", [128, 512], F32))
    P5 = E(nc.psum_tensor("l1f_P5", [128, 512], F32))
    P6 = E(nc.psum_tensor("l1f_P6", [128, 512], F32))
    P7 = E(nc.psum_tensor("l1f_P7", [128, 512], F32))
    PD8 = [P0, P1, P2, P3, P4, P5, P6, P7]
    X1 = [sm.c_x10, sm.c_x11]
    W1S = [sm.c_d1, sm.c_d2, sm.w1a, sm.w1b]
    W2S = [sm.c_d3, sm.c_d4, sm.w2a, sm.w2b]

    for fn in (lambda e: e.dma_start(out=ones_bf[:], in_=ones_h),
               lambda e: e.dma_start(out=eyef[:], in_=eye_h),
               lambda e: e.dma_start(out=iot[:], in_=iot_h),
               lambda e: e.dma_start(out=sel8[:], in_=sel8_h),
               lambda e: e.dma_start(
                   out=cenb[:], in_=cenT.rearrange("(c p) n -> p c n", p=128)),
               lambda e: e.dma_start(
                   out=lng[:], in_=lngT.rearrange("(c p) n -> p c n", p=128)),
               lambda e: e.dma_start(
                   out=lnb[:], in_=lnbT.rearrange("(c p) n -> p c n", p=128)),
               lambda e: e.dma_start(
                   out=b1c[:], in_=b1T.rearrange("(c p) n -> p c n", p=128)),
               lambda e: e.dma_start(
                   out=b2c[:], in_=b2T.rearrange("(c p) n -> p c n", p=128)),
               ):
        sch.op("gpsimd", fn, inc=(sm.c_const, 16))
    sch.op("vector", lambda e: e.memset(eps_col[:], EPS), inc=(sm.dve, 1))
    consts = at(sm.c_const)
    sch.op("tensor", lambda e: e.nop(), waits=[consts])
    sch.op("scalar", lambda e: e.nop(), waits=[consts, at(sm.dve)])
    sch.op("vector", lambda e: e.engine_nop(), waits=[consts])

    for tc in range(8):
        tsl = slice(tc * 512, (tc + 1) * 512)
        emit_ln_xhat(sch, sm, xT, tsl, xc, xn, xhat, xhat, P0, P1, scr,
                     ones_bf, eps_col, t1)
        for tt in range(4):
            for ic in range(NPC):
                w = [at(sm.evv), at(sm.evs), at(sm.xh)] if ic == 0 else []
                sch.op("tensor", lambda e, i=ic, t3=tt: e.matmul(
                    P0[:, 0:8], xc[:, i, t3 * 128:(t3 + 1) * 128],
                    cenb[:, i, :], start=(i == 0), stop=(i == NPC - 1)),
                    waits=w, inc=(sm.mm, 1) if ic == NPC - 1 else None)
            sch.op("vector", lambda e, t3=tt: e.tensor_copy(
                afft[:, t3, :], P0[:, 0:8]), waits=[at(sm.mm)],
                inc=(sm.evv, 1))
            sch.op("vector", lambda e, t3=tt: e.reduce_max(
                m8[:], afft[:, t3, :], axis=AX.X))
            sch.op("vector", lambda e, t3=tt: e.tensor_scalar(
                gmask[:, t3, :], afft[:, t3, :], m8[:], None,
                ALU.is_equal))
            sch.op("vector", lambda e, t3=tt: e.scalar_tensor_tensor(
                gmask[:, t3, :], gmask[:, t3, :], 1000.0, iot[:],
                ALU.mult, ALU.add))
            sch.op("vector", lambda e, t3=tt: e.reduce_max(
                m8[:], gmask[:, t3, :], axis=AX.X))
            sch.op("vector", lambda e, t3=tt: e.tensor_scalar(
                gmask[:, t3, :], gmask[:, t3, :], m8[:], None,
                ALU.is_equal), inc=(sm.dve, 1))
            sch.op("scalar", lambda e, t3=tt: e.activation(
                afft[:, t3, :], afft[:, t3, :], AF.Tanh, scale=0.5),
                waits=[at(sm.dve)], inc=(sm.act, 1))
            sch.op("vector", lambda e, t3=tt: e.tensor_scalar(
                afft[:, t3, :], afft[:, t3, :], 0.5, 0.5,
                ALU.mult, ALU.add), waits=[at(sm.act)])
            sch.op("vector", lambda e, t3=tt: e.tensor_mul(
                gmask[:, t3, :], gmask[:, t3, :], afft[:, t3, :]),
                inc=(sm.dve, 1))
            sch.op("tensor", lambda e, t3=tt: e.matmul(
                P1[0:8, 0:128], gmask[:, t3, :], eyef[:],
                start=True, stop=True),
                waits=[at(sm.dve), at(sm.evv), at(sm.evs)],
                inc=(sm.mm, 1))
            sch.op("vector", lambda e, t3=tt: e.tensor_copy(
                gE[0:8, t3 * 128:(t3 + 1) * 128], P1[0:8, 0:128]),
                waits=[at(sm.mm)], inc=(sm.evv, 1))
        for ex in range(8):
            sch.op("tensor", lambda e, x2=ex: e.matmul(
                P0[:], sel8[:, x2 * 128:(x2 + 1) * 128], gE[:],
                start=True, stop=True),
                waits=[at(sm.evv), at(sm.evs)], inc=(sm.mm, 1))
            sch.op("vector", lambda e: e.tensor_copy(gbc[:], P0[:]),
                   waits=[at(sm.mm)], inc=(sm.evv, 1))
            for pc in range(NPC):
                sch.op("vector", lambda e, p=pc, x2=ex: e.tensor_scalar(
                    xn[:, p, :], xhat[:, p, :], lng[:, p, x2:x2 + 1],
                    lnb[:, p, x2:x2 + 1], ALU.mult, ALU.add),
                    waits=[at(sm.mm)] if pc == 0 else [],
                    inc=(sm.dve, 1) if pc == NPC - 1 else None)
            for ft in range(NF):
                PH = P6 if ft % 2 == 0 else P7
                sl = ft % 4
                sch.op("gpsimd", lambda e, x2=ex, f=ft, s2=sl: e.dma_start(
                    out=w1s[:, s2, :, :],
                    in_=w1S[x2, f].rearrange("p (c n) -> p c n", n=128)),
                    waits=[lag(sm.mm, 3)],
                    inc=(W1S[sl], 16))
                for ic in range(NPC):
                    w = [at(W1S[sl]), at(sm.dve), at(sm.evs), at(sm.evv)] \
                        if ic == 0 else []
                    sch.op("tensor", lambda e, i=ic, s2=sl, p=PH: e.matmul(
                        p[:], w1s[:, s2, i, :], xn[:, i, :],
                        start=(i == 0), stop=(i == NPC - 1)), waits=w,
                        inc=(sm.mm, 1) if ic == NPC - 1 else None)
                sch.op("scalar", lambda e, f=ft, x2=ex, p=PH: e.activation(
                    hbuf[:, f, :], p[:], AF.Relu, bias=b1c[:, f, x2:x2 + 1]),
                    waits=[at(sm.mm)], inc=(sm.evs, 1))
            for ft in range(NF):
                sl = ft % 4
                sch.op("gpsimd", lambda e, x2=ex, f=ft, s2=sl: e.dma_start(
                    out=w2s[:, s2, :],
                    in_=w2T[x2, f * 128:(f + 1) * 128, :]),
                    waits=[lag(sm.mm, 3)], inc=(W2S[sl], 16))
                for pc in range(NPC):
                    w = [at(W2S[sl]), at(sm.evs)] if pc == 0 else []
                    if ft == 0 and pc == 0:
                        w.append(at(sm.evv))
                    sch.op("tensor", lambda e, f=ft, p=pc, s2=sl: e.matmul(
                        PD8[p][:], w2s[:, s2, p * 128:(p + 1) * 128],
                        hbuf[:, f, :],
                        start=(f == 0), stop=(f == NF - 1)), waits=w,
                        inc=(sm.mm, 1) if pc == NPC - 1 else None)
            for pc in range(NPC):
                sch.op("scalar", lambda e, p=pc, x2=ex: e.activation(
                    t1[:], PD8[p][:], AF.Identity, bias=b2c[:, p, x2:x2 + 1]),
                    waits=[at(sm.mm), at(sm.evv)], inc=(sm.evs, 1))
                sch.op("vector", lambda e: e.tensor_mul(t2[:], t1[:],
                                                        gbc[:]),
                       waits=[at(sm.evs)])
                sch.op("vector", lambda e, p=pc: e.tensor_add(
                    xc[:, p, :], xc[:, p, :], t2[:]),
                    inc=(sm.evv, 1))
        for pc in range(NPC):
            sch.op("scalar", lambda e, p=pc, b2=pc % 2: e.activation(
                t1[:] if False else yco[:, b2, :], xc[:, p, :], AF.Copy),
                waits=[at(sm.evv), at(X1[pc % 2])], inc=(sm.evs, 1))
            sch.op("gpsimd", lambda e, p=pc, t=tsl, b2=pc % 2: e.dma_start(
                out=hbm_fm(outT, t)[:, p, :], in_=yco[:, b2, :]),
                waits=[at(sm.evs)], inc=(X1[pc % 2], 16))
        sch.op("vector", lambda e: e.engine_nop(),
               waits=[at(X1[0]), at(X1[1])], inc=(sm.xcfree, 1))
    sch.op("gpsimd", lambda e: e.nop(), waits=[at(X1[0]), at(X1[1])])


def barrier(sch, sm):
    sch.op('vector', lambda e: e.engine_nop(), inc=(sm.dve, 1))
    allsems = [at(getattr(sm, n)) for n in SEM_NAMES]
    for eng in ('tensor', 'gpsimd', 'sync'):
        sch.op(eng, lambda e: e.nop(), waits=allsems)
    sch.op('vector', lambda e: e.engine_nop(), waits=allsems)
    sch.op('scalar', lambda e: e.nop(), waits=allsems)


def build_fused():
    import contextlib
    nc = bass.Bass("TRN2", target_bir_lowering=False)
    io = {}
    io['l0a_xT'] = nc.dram_tensor('l0a_xT', [D, S], F32, kind="ExternalInput").ap()
    io['l0a_wqT'] = nc.dram_tensor('l0a_wqT', [D, D], F32, kind="ExternalInput").ap()
    io['l0a_wkT'] = nc.dram_tensor('l0a_wkT', [D, D], F32, kind="ExternalInput").ap()
    io['l0a_wvT'] = nc.dram_tensor('l0a_wvT', [D, D], F32, kind="ExternalInput").ap()
    io['l0a_woT'] = nc.dram_tensor('l0a_woT', [D, D], F32, kind="ExternalInput").ap()
    io['l0a_masks'] = nc.dram_tensor('l0a_masks', [4, 128, 512], F32, kind="ExternalInput").ap()
    io['l0a_ones_h'] = nc.dram_tensor('l0a_ones_h', [128, 128], F32, kind="ExternalInput").ap()
    io['l0a_x1T'] = nc.dram_tensor('l0a_x1T', [D, S], F32).ap()
    io['l0a_v_scr'] = nc.dram_tensor('l0a_v_scr', [S, D], BF16).ap()
    io['l0a_k_scr'] = nc.dram_tensor('l0a_k_scr', [D, S], BF16).ap()
    io['l0f_xT'] = io['l0a_x1T']
    io['l0f_wgT'] = nc.dram_tensor('l0f_wgT', [D, 3072], I32, kind="ExternalInput").ap()
    io['l0f_wuT'] = nc.dram_tensor('l0f_wuT', [D, 3072], I32, kind="ExternalInput").ap()
    io['l0f_wdT'] = nc.dram_tensor('l0f_wdT', [3072, D], I32, kind="ExternalInput").ap()
    io['l0f_ones_h'] = io['l0a_ones_h']
    io['l0f_yT'] = nc.dram_tensor('l0f_yT', [D, S], F32).ap()
    io['l1a_xT'] = io['l0f_yT']
    io['l1a_wq'] = nc.dram_tensor('l1a_wq', [D, D], F32, kind="ExternalInput").ap()
    io['l1a_wk'] = nc.dram_tensor('l1a_wk', [D, D], F32, kind="ExternalInput").ap()
    io['l1a_wv'] = nc.dram_tensor('l1a_wv', [D, D], F32, kind="ExternalInput").ap()
    io['l1a_wo'] = nc.dram_tensor('l1a_wo', [D, D], F32, kind="ExternalInput").ap()
    io['l1a_bq'] = nc.dram_tensor('l1a_bq', [D, 1], F32, kind="ExternalInput").ap()
    io['l1a_bk'] = nc.dram_tensor('l1a_bk', [D, 1], F32, kind="ExternalInput").ap()
    io['l1a_bvr'] = nc.dram_tensor('l1a_bvr', [1, D], F32, kind="ExternalInput").ap()
    io['l1a_bo'] = nc.dram_tensor('l1a_bo', [D, 1], F32, kind="ExternalInput").ap()
    io['l1a_aqkv'] = nc.dram_tensor('l1a_aqkv', [1, 3], F32, kind="ExternalInput").ap()
    io['l1a_ones_h'] = io['l0a_ones_h']
    io['l1a_eye_h'] = nc.dram_tensor('l1a_eye_h', [128, 128], F32, kind="ExternalInput").ap()
    io['l1a_blk_h'] = nc.dram_tensor('l1a_blk_h', [8, 128, 16], F32, kind="ExternalInput").ap()
    io['l1a_sel_h'] = nc.dram_tensor('l1a_sel_h', [16, 2048], F32, kind="ExternalInput").ap()
    io['l1a_x3T'] = nc.dram_tensor('l1a_x3T', [D, S], F32).ap()
    io['l1a_q_scr'] = nc.dram_tensor('l1a_q_scr', [D, S], BF16, kind="Internal").ap()
    io['l1a_k_scr'] = nc.dram_tensor('l1a_k_scr', [D, S], BF16, kind="Internal").ap()
    io['l1a_v_scr'] = nc.dram_tensor('l1a_v_scr', [S, D], BF16, kind="Internal").ap()
    io['l1a_c_scr'] = nc.dram_tensor('l1a_c_scr', [D, S], BF16, kind="Internal").ap()
    io['l1f_xT'] = io['l1a_x3T']
    io['l1f_cenT'] = nc.dram_tensor('l1f_cenT', [D, 8], F32, kind="ExternalInput").ap()
    io['l1f_lngT'] = nc.dram_tensor('l1f_lngT', [D, 8], F32, kind="ExternalInput").ap()
    io['l1f_lnbT'] = nc.dram_tensor('l1f_lnbT', [D, 8], F32, kind="ExternalInput").ap()
    io['l1f_w1S'] = nc.dram_tensor('l1f_w1S', [8, 32, 128, 1024], F32, kind="ExternalInput").ap()
    io['l1f_w2T'] = nc.dram_tensor('l1f_w2T', [8, 4096, D], F32, kind="ExternalInput").ap()
    io['l1f_b1T'] = nc.dram_tensor('l1f_b1T', [4096, 8], F32, kind="ExternalInput").ap()
    io['l1f_b2T'] = nc.dram_tensor('l1f_b2T', [D, 8], F32, kind="ExternalInput").ap()
    io['l1f_iot_h'] = nc.dram_tensor('l1f_iot_h', [128, 8], F32, kind="ExternalInput").ap()
    io['l1f_sel8_h'] = nc.dram_tensor('l1f_sel8_h', [8, 1024], F32, kind="ExternalInput").ap()
    io['l1f_ones_h'] = io['l0a_ones_h']
    io['l1f_eye_h'] = io['l1a_eye_h']
    io['l1f_outT'] = nc.dram_tensor('l1f_outT', [D, S], F32, kind="ExternalOutput").ap()
    with contextlib.ExitStack() as outer:
        sm = SM(nc, outer, SEM_NAMES)
        sch = Sch([getattr(sm, n) for n in SEM_NAMES])
        flush_g = outer.enter_context(nc.sbuf_tensor("gflush", [1, 8], F32))
        for px, fn in (("l0a", emit_l0a), ("l0f", emit_l0f),
                       ("l1a", emit_l1a), ("l1f", emit_l1f)):
            with contextlib.ExitStack() as ph:
                fn(nc, sch, sm, ph, io)
            barrier(sch, sm)
        blk = outer.enter_context(nc.Block())

        @blk.tensor
        def _(t):
            sch.emit("tensor", t)

        @blk.vector
        def _(v):
            sch.emit("vector", v, flush=lambda e: e.engine_nop())

        @blk.scalar
        def _(s):
            sch.emit("scalar", s,
                     flush=lambda e: e.memzero(flush_g[0:1, 0:8]))

        @blk.gpsimd
        def _(g):
            sch.emit("gpsimd", g)

        @blk.sync
        def _(sy):
            sch.emit("sync", sy)

    return nc


_cache = {}


def _get(name, builder):
    if name not in _cache:
        _cache[name] = builder()
    return _cache[name]


def _masks_np():
    m = np.zeros((4, 128, 512), np.float32)
    for k in range(4):
        for i in range(128):
            m[k, i, k * 128 + i + 1:] = NEG
    return m


def _blk_np():
    b = np.zeros((8, 128, 16), np.float32)
    for pc in range(8):
        for r in range(128):
            b[pc, r, 2 * pc + r // 64] = 1.0
    return b


def _sel16_np():
    s = np.zeros((16, 2048), np.float32)
    for h in range(16):
        s[h, h * 128:(h + 1) * 128] = 1.0
    return s


def _sel8_np():
    s = np.zeros((8, 1024), np.float32)
    for e in range(8):
        s[e, e * 128:(e + 1) * 128] = 1.0
    return s


def _t(a):
    return np.ascontiguousarray(np.asarray(a).T)


def kernel(**inputs):
    x = np.asarray(inputs["L0_attn_in_features"])
    ones128 = np.ones((128, 128), np.float32)
    eye128 = np.eye(128, dtype=np.float32)
    D_ = 1024
    base = {
        "l0a_wqT": _t(inputs["L0_attn_w_q"]), "l0a_wkT": _t(inputs["L0_attn_w_k"]),
        "l0a_wvT": _t(inputs["L0_attn_w_v"]), "l0a_woT": _t(inputs["L0_attn_w_o"]),
        "l0a_masks": _masks_np(), "l0a_ones_h": ones128,
        "l0f_wgT": _t(inputs["L0_ffn_w_gate"]), "l0f_wuT": _t(inputs["L0_ffn_w_up"]),
        "l0f_wdT": _t(inputs["L0_ffn_w_down"]),
        "l1a_wq": np.ascontiguousarray(np.asarray(inputs["L1_attn_wq"], np.float32)),
        "l1a_wk": np.ascontiguousarray(np.asarray(inputs["L1_attn_wk"], np.float32)),
        "l1a_wv": np.ascontiguousarray(np.asarray(inputs["L1_attn_wv"], np.float32)),
        "l1a_wo": np.ascontiguousarray(np.asarray(inputs["L1_attn_wo"], np.float32)),
        "l1a_bq": np.asarray(inputs["L1_attn_bq"], np.float32).reshape(D_, 1),
        "l1a_bk": np.asarray(inputs["L1_attn_bk"], np.float32).reshape(D_, 1),
        "l1a_bvr": np.asarray(inputs["L1_attn_bv"], np.float32).reshape(1, D_),
        "l1a_bo": np.asarray(inputs["L1_attn_bo"], np.float32).reshape(D_, 1),
        "l1a_aqkv": np.stack([np.asarray(inputs["L1_attn_aq"], np.float32)[0],
                              np.asarray(inputs["L1_attn_ak"], np.float32)[0],
                              np.asarray(inputs["L1_attn_av"], np.float32)[0]]
                             ).reshape(1, 3),
        "l1a_eye_h": eye128, "l1a_blk_h": _blk_np(), "l1a_sel_h": _sel16_np(),
        "l1f_cenT": _t(inputs["L1_ffn_centroids"]),
        "l1f_lngT": _t(inputs["L1_ffn_ln_g"]), "l1f_lnbT": _t(inputs["L1_ffn_ln_b"]),
        "l1f_w1S": np.ascontiguousarray(
            np.asarray(inputs["L1_ffn_w1"], np.float32).reshape(
                8, 32, 128, 8, 128).transpose(0, 1, 4, 3, 2).reshape(
                8, 32, 128, 1024)),
        "l1f_w2T": np.ascontiguousarray(
            np.asarray(inputs["L1_ffn_w2"], np.float32).transpose(0, 2, 1)),
        "l1f_b1T": _t(inputs["L1_ffn_b1"]), "l1f_b2T": _t(inputs["L1_ffn_b2"]),
        "l1f_iot_h": np.broadcast_to(-np.arange(8, dtype=np.float32),
                                     (128, 8)).copy(),
        "l1f_sel8_h": _sel8_np(),
    }
    nc = _get("fused", build_fused)
    im = [dict(base, l0a_xT=_t(x[b])) for b in range(8)]
    res = run_bass_kernel_spmd(nc, im, core_ids=list(range(8)))
    out = np.stack([np.asarray(r["l1f_outT"]).T for r in res.results])
    return np.ascontiguousarray(out)


# version 07
# speedup vs baseline: 1.4635x; 1.4635x over previous
import numpy as np
import ml_dtypes
import concourse.bass as bass
import concourse.mybir as mybir
from concourse.bass_utils import run_bass_kernel_spmd

F32 = mybir.dt.float32
BF16 = mybir.dt.bfloat16
I32 = mybir.dt.int32
AF = mybir.ActivationFunctionType
ALU = mybir.AluOpType
AX = mybir.AxisListType

S = 4096
D = 1024
NPC = D // 128
EPS = 1e-5
NEG = -1.0e30


class Sem:
    def __init__(self, h):
        self.h = h
        self.n = 0


import os
STRICT = os.environ.get("BASS_STRICT_SCHED", "") == "1"


class Sch:

    def __init__(self, sems=None):
        self.steps = []
        self.sems = sems

    def op(self, eng, fn, waits=(), inc=None):
        if STRICT and self.sems is not None:
            w = [(s.h, s.n) for s in self.sems if s.n > 0]
        else:
            w = [(s.h, v) for (s, v) in waits if v > 0]
        incs = []
        if inc is not None:
            if isinstance(inc, list):
                for sem, k in inc:
                    sem.n += k
                    incs.append((sem.h, k))
            else:
                sem, k = inc
                sem.n += k
                incs.append((sem.h, k))
        self.steps.append((eng, fn, w, incs))

    def emit(self, eng_name, eng, flush=None):
        last = {}
        for (e, fn, waits, incs) in self.steps:
            if e != eng_name:
                continue
            for (h, v) in waits:
                if last.get(h, -1) < v:
                    eng.wait_ge(h, v)
                    last[h] = v
            inst = fn(eng)
            if incs and flush is not None:
                inst = flush(eng)
            for (h, k) in incs:
                assert inst is not None
                inst = inst.then_inc(h, k)


def at(sem):
    return (sem, sem.n)


def lag(sem, k):
    return (sem, max(0, sem.n - k))


def recip(eng, out_ap, in_ap, bias=None):
    nc = eng.bass
    ins_ = [eng.lower_ap(in_ap.opt()),
            eng.lower_ap(bias.opt()) if bias is not None
            else mybir.ImmediateValue(dtype=F32, value=0.0),
            mybir.ImmediateValue(dtype=F32, value=1.0),
            mybir.ImmediateValue(dtype=F32, value=0.0)]
    outs_ = [eng.lower_ap(out_ap.opt())]
    inst = mybir.InstActivation(name=nc.get_next_instruction_name(),
                                ins=ins_, outs=outs_, func=AF.Reciprocal)
    return eng.add_instruction(inst)


def hbm_fm(ap, tsl=None):
    if tsl is not None:
        ap = ap[:, tsl]
    return ap.rearrange("(c p) t -> p c t", p=128)


class SM:

    def __init__(self, nc, stack, names):
        for n in names:
            h = stack.enter_context(nc.semaphore("s_" + n))
            setattr(self, n, Sem(h))


SEM_NAMES = ("mm", "pv", "smk", "dve", "evv", "act", "evs", "stats", "xh",
             "xcfree", "xbfree", "xbrv", "xbrs",
             "c_const", "c_wa", "c_wb", "c_xc", "c_ko", "c_vo0", "c_vo1",
             "c_vb0", "c_vb1", "c_vb2", "c_vb3", "c_kb0", "c_kb1",
             "c_x10", "c_x11", "c_trA", "c_trC")


def emit_ln_stats(sch, sm, xsrc_h, xc, xb, xsq, PS0, PS1, tmpA, tmpB, tmpC,
                  a_row, b_row, ones_bf, eps_col):
    for tc in range(8):
        tsl = slice(tc * 512, (tc + 1) * 512)
        sch.op("sync", lambda e, t=tsl: e.dma_start(
            out=xc[:], in_=hbm_fm(xsrc_h, t)), waits=[at(sm.xcfree)],
            inc=(sm.c_xc, 16))
        sch.op("vector", lambda e: e.tensor_copy(xb[:], xc[:]),
               waits=[at(sm.c_xc), at(sm.xbfree)], inc=(sm.xbrv, 1))
        sch.op("scalar", lambda e: e.activation(xsq[:], xc[:], AF.Square),
               waits=[at(sm.c_xc), at(sm.xbfree)], inc=(sm.xbrs, 1))
        for pc in range(NPC):
            sch.op("tensor", lambda e, p=pc: e.matmul(
                PS0[:], ones_bf[:], xb[:, p, :],
                start=(p == 0), stop=(p == NPC - 1)),
                waits=[at(sm.xbrv), at(sm.xbrs), at(sm.evv)] if pc == 0 else [])
        for pc in range(NPC):
            sch.op("tensor", lambda e, p=pc: e.matmul(
                PS1[:], ones_bf[:], xsq[:, p, :],
                start=(p == 0), stop=(p == NPC - 1)),
                inc=(sm.mm, 1) if pc == NPC - 1 else None)
        sch.op("vector", lambda e: e.tensor_scalar(
            tmpA[:], PS0[:], 1.0 / D, None, ALU.mult), waits=[at(sm.mm)])
        sch.op("vector", lambda e: e.tensor_scalar(
            tmpB[:], PS1[:], 1.0 / D, None, ALU.mult), inc=(sm.evv, 1))
        sch.op("vector", lambda e: e.tensor_mul(tmpC[:], tmpA[:], tmpA[:]))
        sch.op("vector", lambda e: e.tensor_sub(tmpB[:], tmpB[:], tmpC[:]),
               inc=(sm.dve, 1))
        sch.op("scalar", lambda e: e.activation(
            tmpC[:], tmpB[:], AF.Ln, bias=eps_col[:]), waits=[at(sm.dve)])
        sch.op("scalar", lambda e: e.activation(
            tmpC[:], tmpC[:], AF.Exp, scale=-0.5), inc=(sm.act, 1))
        sch.op("vector", lambda e: e.scalar_tensor_tensor(
            tmpA[:], tmpA[:], -1.0, tmpC[:], ALU.mult, ALU.mult),
            waits=[at(sm.act)])
        sch.op("vector", lambda e, t=tsl: e.tensor_copy(
            a_row[0:1, t], tmpC[0:1, :]))
        sch.op("vector", lambda e, t=tsl: e.tensor_copy(
            b_row[0:1, t], tmpA[0:1, :]), inc=(sm.stats, 1))
        sch.op("vector", lambda e: e.engine_nop(), waits=[at(sm.mm)],
               inc=(sm.xcfree, 1))
        sch.op("vector", lambda e: e.engine_nop(), inc=(sm.xbfree, 1))


def emit_xhat_chunk(sch, sm, xsrc_h, tsl, xc, xh, PB0, PB1, abc,
                    a_row, b_row, ones1, t1):
    sch.op("sync", lambda e, t=tsl: e.dma_start(
        out=xc[:], in_=hbm_fm(xsrc_h, t)), waits=[at(sm.xcfree)],
        inc=(sm.c_xc, 16))
    sch.op("tensor", lambda e, t=tsl: e.matmul(
        PB0[:], ones1[:], a_row[0:1, t], start=True, stop=True),
        waits=[at(sm.stats), at(sm.evv), at(sm.evs)])
    sch.op("tensor", lambda e, t=tsl: e.matmul(
        PB1[:], ones1[:], b_row[0:1, t], start=True, stop=True),
        inc=(sm.mm, 1))
    sch.op("vector", lambda e: e.tensor_copy(abc[:, :, 0], PB0[:]),
           waits=[at(sm.mm)])
    sch.op("vector", lambda e: e.tensor_copy(abc[:, :, 1], PB1[:]),
           inc=(sm.evv, 1))
    for pc in range(NPC):
        w = [at(sm.c_xc), at(sm.stats), at(sm.mm)] if pc == 0 else []
        sch.op("vector", lambda e, p=pc: e.tensor_mul(
            t1[:], xc[:, p, :], abc[:, :, 0]), waits=w)
        sch.op("vector", lambda e, p=pc: e.tensor_add(
            xh[:, p, :], t1[:], abc[:, :, 1]),
            inc=(sm.xh, 1) if pc == NPC - 1 else None)


LN_C = float(np.log(np.sqrt(1024.0) / np.log(1025.0)))
SEM_NAMES = SEM_NAMES + ("c_d0", "c_d1", "c_d2", "c_d3", "c_d4", "c_d5", "c_d6",
                         "w1a", "w1b", "w2a", "w2b")


def emit_ln_xhat(sch, sm, xsrc_h, tsl, xc, xb, xsq, xh, PS0, PS1, scr,
                 ones_bf, eps_col, t1):
    sA, sB, sC = scr[:, :, 0], scr[:, :, 1], scr[:, :, 2]
    sch.op("sync", lambda e, t=tsl: e.dma_start(
        out=xc[:], in_=hbm_fm(xsrc_h, t)), waits=[at(sm.xcfree)],
        inc=(sm.c_xc, 16))
    sch.op("vector", lambda e: e.tensor_copy(xb[:], xc[:]),
           waits=[at(sm.c_xc), at(sm.mm)], inc=(sm.xbrv, 1))
    sch.op("scalar", lambda e: e.activation(xsq[:], xc[:], AF.Square),
           waits=[at(sm.c_xc), at(sm.mm)], inc=(sm.xbrs, 1))
    for pc in range(NPC):
        sch.op("tensor", lambda e, p=pc: e.matmul(
            PS0[:], ones_bf[:], xb[:, p, :],
            start=(p == 0), stop=(p == NPC - 1)),
            waits=[at(sm.xbrv), at(sm.xbrs), at(sm.evv), at(sm.evs)]
            if pc == 0 else [])
    for pc in range(NPC):
        sch.op("tensor", lambda e, p=pc: e.matmul(
            PS1[:], ones_bf[:], xsq[:, p, :],
            start=(p == 0), stop=(p == NPC - 1)),
            inc=(sm.mm, 1) if pc == NPC - 1 else None)
    sch.op("vector", lambda e: e.tensor_scalar(
        sA[:], PS0[:], 1.0 / D, None, ALU.mult), waits=[at(sm.mm)])
    sch.op("vector", lambda e: e.tensor_scalar(
        sB[:], PS1[:], 1.0 / D, None, ALU.mult), inc=(sm.evv, 1))
    sch.op("vector", lambda e: e.tensor_mul(sC[:], sA[:], sA[:]))
    sch.op("vector", lambda e: e.tensor_sub(sB[:], sB[:], sC[:]),
           inc=(sm.dve, 1))
    sch.op("scalar", lambda e: e.activation(
        sC[:], sB[:], AF.Ln, bias=eps_col[:]), waits=[at(sm.dve)])
    sch.op("scalar", lambda e: e.activation(
        sC[:], sC[:], AF.Exp, scale=-0.5), inc=(sm.act, 1))
    sch.op("vector", lambda e: e.scalar_tensor_tensor(
        sA[:], sA[:], -1.0, sC[:], ALU.mult, ALU.mult), waits=[at(sm.act)])
    for pc in range(NPC):
        sch.op("vector", lambda e, p=pc: e.tensor_mul(
            t1[:], xc[:, p, :], sC[:]))
        sch.op("vector", lambda e, p=pc: e.tensor_add(
            xh[:, p, :], t1[:], sA[:]),
            inc=(sm.xh, 1) if pc == NPC - 1 else None)




def emit_l0a(nc, sch, sm, ctx, io):
    xT = io['l0a_xT']
    wqT = io['l0a_wqT']
    wkT = io['l0a_wkT']
    wvT = io['l0a_wvT']
    woT = io['l0a_woT']
    masks = io['l0a_masks']
    ones_h = io['l0a_ones_h']
    x1T = io['l0a_x1T']
    v_scr = io['l0a_v_scr']
    k_scr = io['l0a_k_scr']
    E = ctx.enter_context
    kbuf = E(nc.sbuf_tensor("l0a_kbuf", [128, 2, NPC, 512], BF16))
    kout = E(nc.sbuf_tensor("l0a_kout", [128, NPC, 512], BF16))
    wa = E(nc.sbuf_tensor("l0a_wa", [128, NPC, 1024], BF16))
    wb = E(nc.sbuf_tensor("l0a_wb", [128, NPC, 1024], BF16))
    xc = E(nc.sbuf_tensor("l0a_xc", [128, NPC, 512], F32))
    xb = E(nc.sbuf_tensor("l0a_xb", [128, NPC, 512], BF16))
    xh = E(nc.sbuf_tensor("l0a_xh", [128, NPC, 512], BF16))
    xsq = xh
    qtb = xb
    sc = E(nc.sbuf_tensor("l0a_sc", [128, S], F32))
    att = E(nc.sbuf_tensor("l0a_att", [128, S], BF16))
    attT = E(nc.sbuf_tensor("l0a_attT", [128, 32, 128], BF16,
                            is_transpose_dest=True))
    vbuf = E(nc.sbuf_tensor("l0a_vbuf", [128, 4, D], BF16))
    vout = E(nc.sbuf_tensor("l0a_vout", [128, 2, D], BF16))
    ctxb = E(nc.sbuf_tensor("l0a_ctxb", [128, D], BF16))
    ctxT = E(nc.sbuf_tensor("l0a_ctxT", [128, NPC, 512], BF16,
                            is_transpose_dest=True))
    x1c = E(nc.sbuf_tensor("l0a_x1c", [128, 2, 512], F32))
    tmpA = E(nc.sbuf_tensor("l0a_tmpA", [128, 512], F32))
    tmpB = E(nc.sbuf_tensor("l0a_tmpB", [128, 512], F32))
    tmpC = E(nc.sbuf_tensor("l0a_tmpC", [128, 512], F32))
    abc = E(nc.sbuf_tensor("l0a_abc", [128, 512, 2], F32))
    t1 = E(nc.sbuf_tensor("l0a_t1", [128, 512], F32))
    a_row = E(nc.sbuf_tensor("l0a_a_row", [1, S], F32))
    b_row = E(nc.sbuf_tensor("l0a_b_row", [1, S], F32))
    mrow = E(nc.sbuf_tensor("l0a_mrow", [128, 1], F32))
    srow = E(nc.sbuf_tensor("l0a_srow", [128, 1], F32))
    rrow = E(nc.sbuf_tensor("l0a_rrow", [128, 1], F32))
    maskt = E(nc.sbuf_tensor("l0a_maskt", [128, 4, 512], BF16))
    ones_bf = E(nc.sbuf_tensor("l0a_ones_bf", [128, 128], BF16))
    ones1 = E(nc.sbuf_tensor("l0a_ones1", [1, 128], F32))
    eps_col = E(nc.sbuf_tensor("l0a_eps_col", [128, 1], F32))
    flush_sc = E(nc.sbuf_tensor("l0a_flush_sc", [1, 8], F32))
    P0 = E(nc.psum_tensor("l0a_P0", [128, 512], F32))
    P1 = E(nc.psum_tensor("l0a_P1", [128, 512], F32))
    P2 = E(nc.psum_tensor("l0a_P2", [128, 512], F32))
    P3 = E(nc.psum_tensor("l0a_P3", [128, 512], F32))
    P4 = E(nc.psum_tensor("l0a_P4", [128, 512], F32))
    P5 = E(nc.psum_tensor("l0a_P5", [128, 512], F32))

    VB = [sm.c_vb0, sm.c_vb1, sm.c_vb2, sm.c_vb3]
    KB = [sm.c_kb0, sm.c_kb1]
    VO = [sm.c_vo0, sm.c_vo1]
    X1 = [sm.c_x10, sm.c_x11]

    sch.op("gpsimd", lambda e: e.dma_start(out=ones_bf[:], in_=ones_h),
           inc=(sm.c_const, 16))
    sch.op("gpsimd", lambda e: e.dma_start(out=ones1[:], in_=ones_h[0:1, :]),
           inc=(sm.c_const, 16))
    sch.op("gpsimd", lambda e: e.dma_start(
        out=maskt[:], in_=masks.rearrange("m p t -> p m t")),
        inc=(sm.c_const, 16))
    sch.op("vector", lambda e: e.memset(eps_col[:], EPS), inc=(sm.dve, 1))
    consts = at(sm.c_const)

    sch.op("tensor", lambda e: e.nop(), waits=[consts])
    sch.op("scalar", lambda e: e.nop(), waits=[consts, at(sm.dve)])
    sch.op("vector", lambda e: e.engine_nop(), waits=[consts])
    emit_ln_stats(sch, sm, xT, xc, xb, xsq, P0, P1, tmpA, tmpB, tmpC,
                  a_row, b_row, ones_bf, eps_col)

    sch.op("gpsimd", lambda e: e.dma_start(
        out=wa[:], in_=wkT.rearrange("(c p) n -> p c n", p=128)),
        inc=(sm.c_wa, 16))
    sch.op("gpsimd", lambda e: e.dma_start(
        out=wb[:], in_=wvT.rearrange("(c p) n -> p c n", p=128)),
        inc=(sm.c_wb, 16))

    for tc in range(8):
        tsl = slice(tc * 512, (tc + 1) * 512)
        emit_xhat_chunk(sch, sm, xT, tsl, xc, xh, P2, P3, abc,
                        a_row, b_row, ones1, t1)
        for pc in range(NPC):
            for ic in range(NPC):
                w = [at(sm.xh), at(sm.c_wa), at(sm.evv), at(sm.evs)] \
                    if ic == 0 else []
                sch.op("tensor", lambda e, i=ic, p=pc: e.matmul(
                    P0[:], wa[:, i, p * 128:(p + 1) * 128], xh[:, i, :],
                    start=(i == 0), stop=(i == NPC - 1)), waits=w,
                    inc=(sm.mm, 1) if ic == NPC - 1 else None)
            sch.op("scalar", lambda e, p=pc: e.copy(
                kout[:, p, :], P0[:]),
                waits=[at(sm.mm), at(sm.c_ko)] if pc == 0 else [at(sm.mm)],
                inc=(sm.evs, 1))
        sch.op("gpsimd", lambda e, t=tsl: e.dma_start(
            out=hbm_fm(k_scr, t), in_=kout[:]),
            waits=[at(sm.evs)], inc=(sm.c_ko, 16))
        for tt in range(4):
            vsl = tc * 4 + tt
            for half in range(2):
                for ic in range(NPC):
                    w = [at(sm.c_wb), at(sm.evv), at(sm.evs)] if ic == 0 else []
                    sch.op("tensor", lambda e, i=ic, h=half, t2=tt: e.matmul(
                        P1[:], xh[:, i, t2 * 128:(t2 + 1) * 128],
                        wb[:, i, h * 512:(h + 1) * 512],
                        start=(i == 0), stop=(i == NPC - 1)), waits=w,
                        inc=(sm.mm, 1) if ic == NPC - 1 else None)
                sch.op("scalar", lambda e, h=half, b2=vsl % 2: e.copy(
                    vout[:, b2, h * 512:(h + 1) * 512], P1[:]),
                    waits=[at(sm.mm), at(VO[vsl % 2])], inc=(sm.evs, 1))
            sch.op("gpsimd", lambda e, v=vsl, b2=vsl % 2: e.dma_start(
                out=v_scr[v * 128:(v + 1) * 128, :], in_=vout[:, b2, :]),
                waits=[at(sm.evs)], inc=(VO[vsl % 2], 16))
        sch.op("vector", lambda e: e.engine_nop(), waits=[at(sm.mm)],
               inc=(sm.xcfree, 1))
    scr_ready = [at(sm.c_ko), at(VO[0]), at(VO[1])]

    sch.op("gpsimd", lambda e: e.dma_start(
        out=wa[:], in_=wqT.rearrange("(c p) n -> p c n", p=128)),
        waits=[at(sm.mm), (sm.c_wa, 16)], inc=(sm.c_wa, 16))
    sch.op("gpsimd", lambda e: e.dma_start(
        out=wb[:], in_=woT.rearrange("(c p) n -> p c n", p=128)),
        waits=[at(sm.mm), (sm.c_wb, 16)], inc=(sm.c_wb, 16))
    wq_pt = at(sm.c_wa)
    wo_pt = at(sm.c_wb)

    for g in range(8):
        gsl = slice(g * 512, (g + 1) * 512)
        emit_xhat_chunk(sch, sm, xT, gsl, xc, xh, P2, P3, abc,
                        a_row, b_row, ones1, t1)
        for pc in range(NPC):
            for ic in range(NPC):
                w = [at(sm.xh), wq_pt, at(sm.evv), at(sm.evs)] \
                    if ic == 0 else []
                sch.op("tensor", lambda e, i=ic, p=pc: e.matmul(
                    P0[:], wa[:, i, p * 128:(p + 1) * 128], xh[:, i, :],
                    start=(i == 0), stop=(i == NPC - 1)), waits=w,
                    inc=(sm.mm, 1) if ic == NPC - 1 else None)
            sch.op("scalar", lambda e, p=pc: e.copy(
                qtb[:, p, :], P0[:]), waits=[at(sm.mm)], inc=(sm.evs, 1))
        for qi in range(4):
            qt = g * 4 + qi
            nkc = qt // 4 + 1
            nkt = nkc * 4
            L = nkc * 512
            for kc in range(nkc):
                PS = P0 if kc % 2 == 0 else P1
                sch.op("sync", lambda e, k=kc: e.dma_start(
                    out=kbuf[:, k % 2, :, :],
                    in_=hbm_fm(k_scr, slice(k * 512, (k + 1) * 512))),
                    waits=scr_ready + [lag(sm.evs, 1)],
                    inc=(KB[kc % 2], 16))
                for ic in range(NPC):
                    w = [at(sm.evv), at(sm.evs), at(KB[kc % 2])] \
                        if ic == 0 else []
                    sch.op("tensor", lambda e, i=ic, k=kc, q=qi, p=PS: e.matmul(
                        p[:], qtb[:, i, q * 128:(q + 1) * 128],
                        kbuf[:, k % 2, i, :],
                        start=(i == 0), stop=(i == NPC - 1)), waits=w,
                        inc=(sm.mm, 1) if ic == NPC - 1 else None)
                sch.op("scalar", lambda e, k=kc, p=PS: e.activation(
                    sc[:, k * 512:(k + 1) * 512], p[:], AF.Copy,
                    scale=1.0 / 32.0),
                    waits=[at(sm.mm), at(sm.dve)] if kc == 0 else [at(sm.mm)],
                    inc=(sm.evs, 1))
            sch.op("vector", lambda e, q=qt, l=L: e.tensor_add(
                sc[:, l - 512:l], sc[:, l - 512:l], maskt[:, q % 4, :]),
                waits=[at(sm.evs)])
            sch.op("vector", lambda e, l=L: e.reduce_max(
                mrow[:], sc[:, 0:l], axis=AX.X, negate=True),
                inc=(sm.dve, 1))
            sch.op("scalar", lambda e, l=L: e.activation(
                att[:, 0:l], sc[:, 0:l], AF.Exp, bias=mrow[:],
                accum_out=srow[:]), waits=[at(sm.dve), at(sm.c_trA)],
                inc=(sm.act, 1))
            sch.op("vector", lambda e: e.reciprocal(rrow[:], srow[:]),
                   waits=[at(sm.act)], inc=(sm.dve, 1))
            sch.op("sync", lambda e, n=nkt: e.dma_start(
                out=attT[:, 0:n, :], in_=att[:, 0:n * 128], transpose=True),
                waits=[at(sm.act), at(sm.pv), at(sm.mm)], inc=(sm.c_trA, 16))
            for kt in range(nkt):
                sch.op("sync", lambda e, k=kt: e.dma_start(
                    out=vbuf[:, k % 4, :], in_=v_scr[k * 128:(k + 1) * 128, :]),
                    waits=scr_ready + [lag(sm.pv, 3), lag(sm.mm, 2)],
                    inc=(VB[kt % 4], 16))
                for half in range(2):
                    PC = P4 if half == 0 else P5
                    w = [at(VB[kt % 4]), at(sm.c_trA)]
                    if kt == 0:
                        w += [at(sm.evv), at(sm.evs)]
                    sch.op("tensor", lambda e, k=kt, h=half, p=PC, n=nkt: e.matmul(
                        p[:], attT[:, k, :],
                        vbuf[:, k % 4, h * 512:(h + 1) * 512],
                        start=(k == 0), stop=(k == n - 1)),
                        waits=w if half == 0 else [],
                        inc=((sm.mm, 1) if kt == nkt - 1
                             else (sm.pv, 1)) if half == 1 else None)
            for half in range(2):
                PC = P4 if half == 0 else P5
                sch.op("scalar", lambda e, h=half, p=PC: e.activation(
                    ctxb[:, h * 512:(h + 1) * 512], p[:], AF.Copy,
                    scale=rrow[:]), waits=[at(sm.mm), at(sm.dve)],
                    inc=(sm.evs, 1) if half == 1 else None)
            sch.op("sync", lambda e, q=qi: e.dma_start(
                out=ctxT[:, :, q * 128:(q + 1) * 128], in_=ctxb[:],
                transpose=True), waits=[at(sm.evs), at(sm.mm)],
                inc=(sm.c_trC, 16))
        for pc in range(NPC):
            PS = P2 if pc % 2 == 0 else P3
            for ic in range(NPC):
                w = [at(sm.c_trC), wo_pt, at(sm.evv), at(sm.evs)] \
                    if ic == 0 else []
                sch.op("tensor", lambda e, i=ic, p=pc, ps=PS: e.matmul(
                    ps[:], wb[:, i, p * 128:(p + 1) * 128], ctxT[:, i, :],
                    start=(i == 0), stop=(i == NPC - 1)), waits=w,
                    inc=(sm.mm, 1) if ic == NPC - 1 else None)
            sch.op("vector", lambda e, p=pc, ps=PS: e.tensor_add(
                x1c[:, p % 2, :], ps[:], xc[:, p, :]),
                waits=[at(sm.mm), at(X1[pc % 2])], inc=(sm.evv, 1))
            sch.op("gpsimd", lambda e, p=pc, g2=g: e.dma_start(
                out=hbm_fm(x1T, slice(g2 * 512, (g2 + 1) * 512))[:, p, :],
                in_=x1c[:, p % 2, :]), waits=[at(sm.evv)],
                inc=(X1[pc % 2], 16))
        sch.op("vector", lambda e: e.engine_nop(), waits=[at(sm.mm)],
               inc=(sm.xcfree, 1))

    sch.op("gpsimd", lambda e: e.nop(), waits=[at(X1[0]), at(X1[1])])


def emit_l0f(nc, sch, sm, ctx, io):
    xT = io['l0f_xT']
    wgT = io['l0f_wgT']
    wuT = io['l0f_wuT']
    wdT = io['l0f_wdT']
    ones_h = io['l0f_ones_h']
    yT = io['l0f_yT']
    NF = 24
    E = ctx.enter_context
    wgb = E(nc.sbuf_tensor("l0f_wgb", [128, NPC, 3072], BF16))
    wub = E(nc.sbuf_tensor("l0f_wub", [128, NPC, 3072], BF16))
    wdb = E(nc.sbuf_tensor("l0f_wdb", [128, NF, D], BF16))
    hbuf = E(nc.sbuf_tensor("l0f_hbuf", [128, NF, 512], BF16))
    xc = E(nc.sbuf_tensor("l0f_xc", [128, NPC, 512], F32))
    xh = E(nc.sbuf_tensor("l0f_xh", [128, NPC, 512], BF16))
    scr = E(nc.sbuf_tensor("l0f_scr", [128, 512, 3], F32))
    t1 = E(nc.sbuf_tensor("l0f_t1", [128, 512], F32))
    sil = E(nc.sbuf_tensor("l0f_sil", [128, 512], BF16))
    yc = E(nc.sbuf_tensor("l0f_yc", [128, 2, 512], F32))
    ones_bf = E(nc.sbuf_tensor("l0f_ones_bf", [128, 128], BF16))
    eps_col = E(nc.sbuf_tensor("l0f_eps_col", [128, 1], F32))
    flush_sc = E(nc.sbuf_tensor("l0f_flush_sc", [1, 8], F32))
    P0 = E(nc.psum_tensor("l0f_P0", [128, 512], F32))
    P1 = E(nc.psum_tensor("l0f_P1", [128, 512], F32))
    P2 = E(nc.psum_tensor("l0f_P2", [128, 512], F32))
    P3 = E(nc.psum_tensor("l0f_P3", [128, 512], F32))
    X1 = [sm.c_x10, sm.c_x11]

    sch.op("gpsimd", lambda e: e.dma_start(out=ones_bf[:], in_=ones_h),
           inc=(sm.c_const, 16))
    sch.op("gpsimd", lambda e: e.dma_start(
        out=wgb[:], in_=wgT.rearrange("(c p) n -> p c n", p=128)),
        inc=(sm.c_wa, 16))
    sch.op("gpsimd", lambda e: e.dma_start(
        out=wub[:], in_=wuT.rearrange("(c p) n -> p c n", p=128)),
        inc=(sm.c_wa, 16))
    sch.op("gpsimd", lambda e: e.dma_start(
        out=wdb[:], in_=wdT.rearrange("(c p) n -> p c n", p=128)),
        inc=(sm.c_wa, 16))
    sch.op("vector", lambda e: e.memset(eps_col[:], EPS), inc=(sm.dve, 1))
    sch.op("tensor", lambda e: e.nop(), waits=[at(sm.c_const), at(sm.c_wa)])
    sch.op("scalar", lambda e: e.nop(), waits=[at(sm.dve)])

    for tc in range(8):
        tsl = slice(tc * 512, (tc + 1) * 512)
        emit_ln_xhat(sch, sm, xT, tsl, xc,
                     hbuf.ap()[:, 0:NPC, :], xh, xh, P0, P1, scr,
                     ones_bf, eps_col, t1)
        for ft in range(NF):
            PA = P0 if ft % 2 == 0 else P2
            PB = P1 if ft % 2 == 0 else P3
            for ic in range(NPC):
                w = [at(sm.xh), at(sm.evv), at(sm.evs)] if ic == 0 else []
                sch.op("tensor", lambda e, i=ic, f=ft, p=PA: e.matmul(
                    p[:], wgb[:, i, f * 128:(f + 1) * 128], xh[:, i, :],
                    start=(i == 0), stop=(i == NPC - 1)), waits=w,
                    inc=(sm.mm, 1) if ic == NPC - 1 else None)
            for ic in range(NPC):
                sch.op("tensor", lambda e, i=ic, f=ft, p=PB: e.matmul(
                    p[:], wub[:, i, f * 128:(f + 1) * 128], xh[:, i, :],
                    start=(i == 0), stop=(i == NPC - 1)),
                    inc=(sm.mm, 1) if ic == NPC - 1 else None)
            sch.op("scalar", lambda e, p=PA: e.activation(
                sil[:], p[:], AF.Silu, scale=1.0 / 127.0),
                waits=[at(sm.mm), at(sm.evv)], inc=(sm.evs, 1))
            sch.op("vector", lambda e, f=ft, p=PB: e.tensor_mul(
                hbuf[:, f, :], sil[:], p[:]),
                waits=[at(sm.evs)], inc=(sm.evv, 1))
        for rnd in range(2):
            for ft in range(NF):
                for pq in range(4):
                    pc = rnd * 4 + pq
                    PD = [P0, P1, P2, P3][pq]
                    w = [at(sm.evv), at(sm.evs)] if (ft == 0 and pq == 0) else []
                    sch.op("tensor", lambda e, f=ft, p=pc, ps=PD: e.matmul(
                        ps[:], wdb[:, f, p * 128:(p + 1) * 128],
                        hbuf[:, f, :],
                        start=(f == 0), stop=(f == NF - 1)), waits=w,
                        inc=(sm.mm, 1) if (ft == NF - 1 and pq == 3) else None)
            for pq in range(4):
                pc = rnd * 4 + pq
                PD = [P0, P1, P2, P3][pq]
                sch.op("vector", lambda e, ps=PD: e.tensor_scalar(
                    t1[:], ps[:], 1.0 / 16129.0, None, ALU.mult),
                    waits=[at(sm.mm), at(sm.evs)])
                sch.op("vector", lambda e, p=pc: e.tensor_add(
                    t1[:], t1[:], xc[:, p, :]), inc=(sm.evv, 1))
                sch.op("scalar", lambda e, b2=pc % 2: e.activation(
                    yc[:, b2, :], t1[:], AF.Copy),
                    waits=[at(sm.evv), at(X1[pc % 2])], inc=(sm.evs, 1))
                sch.op("gpsimd", lambda e, p=pc, t=tsl, b2=pc % 2: e.dma_start(
                    out=hbm_fm(yT, t)[:, p, :], in_=yc[:, b2, :]),
                    waits=[at(sm.evs)], inc=(X1[pc % 2], 16))
        sch.op("vector", lambda e: e.engine_nop(), waits=[at(sm.mm)],
               inc=(sm.xcfree, 1))
    sch.op("gpsimd", lambda e: e.nop(), waits=[at(X1[0]), at(X1[1])])


def emit_l1a(nc, sch, sm, ctx, io):
    xT = io['l1a_xT']
    wq = io['l1a_wq']
    wk = io['l1a_wk']
    wv = io['l1a_wv']
    wo = io['l1a_wo']
    bq = io['l1a_bq']
    bk = io['l1a_bk']
    bvr = io['l1a_bvr']
    bo = io['l1a_bo']
    aqkv = io['l1a_aqkv']
    ones_h = io['l1a_ones_h']
    eye_h = io['l1a_eye_h']
    blk_h = io['l1a_blk_h']
    sel_h = io['l1a_sel_h']
    x3T = io['l1a_x3T']
    q_scr = io['l1a_q_scr']
    k_scr = io['l1a_k_scr']
    v_scr = io['l1a_v_scr']
    c_scr = io['l1a_c_scr']
    H, DH = 16, 64
    E = ctx.enter_context
    wsb = [E(nc.sbuf_tensor("wsb%d" % i, [128, NPC, D], BF16))
           for i in range(3)]
    xc = E(nc.sbuf_tensor("l1a_xc", [128, NPC, 512], F32))
    xh = E(nc.sbuf_tensor("l1a_xh", [128, NPC, 512], BF16))
    xq2 = E(nc.sbuf_tensor("l1a_xq2", [128, NPC, 512], BF16))
    scr = E(nc.sbuf_tensor("l1a_scr", [128, 512, 3], F32))
    t1 = E(nc.sbuf_tensor("l1a_t1", [128, 512], F32))
    s1k = E(nc.sbuf_tensor("l1a_s1k", [128, 1024], F32))
    s2k = E(nc.sbuf_tensor("l1a_s2k", [128, 1024], F32))
    yt = E(nc.sbuf_tensor("l1a_yt", [128, 1024], BF16))
    ysq = E(nc.sbuf_tensor("l1a_ysq", [128, 512], BF16))
    wsqsc = E(nc.sbuf_tensor("l1a_wsqsc", [128, 1024], BF16))
    xsqbc = E(nc.sbuf_tensor("l1a_xsqbc", [128, 512], F32))
    xsqT = E(nc.sbuf_tensor("l1a_xsqT", [128, 4], F32))
    wsqe = E(nc.sbuf_tensor("l1a_wsqe", [128, NPC, 2], F32))
    wsqvb = E(nc.sbuf_tensor("l1a_wsqvb", [128, 1024], F32))
    bvb = E(nc.sbuf_tensor("l1a_bvb", [128, 1024], F32))
    bcols = E(nc.sbuf_tensor("l1a_bcols", [128, 2, NPC], F32))
    bscal = E(nc.sbuf_tensor("l1a_bscal", [128, 2, NPC], F32))
    bocol = E(nc.sbuf_tensor("l1a_bocol", [128, NPC, 1], F32))
    scal = E(nc.sbuf_tensor("l1a_scal", [128, 3], F32))
    aqs = E(nc.sbuf_tensor("l1a_aqs", [1, 3], F32))
    qn_sb = E(nc.sbuf_tensor("l1a_qn_sb", [16, S], BF16))
    kn_sb = E(nc.sbuf_tensor("l1a_kn_sb", [16, S], BF16))
    qnT = E(nc.sbuf_tensor("l1a_qnT", [128, 32, 16], F32))
    kbc = E(nc.sbuf_tensor("l1a_kbc", [128, NPC, 512], BF16))
    qh_sb = E(nc.sbuf_tensor("l1a_qh_sb", [64, S], BF16))
    kh_sb = E(nc.sbuf_tensor("l1a_kh_sb", [64, S], BF16))
    vh_sb = E(nc.sbuf_tensor("l1a_vh_sb", [128, 32, DH], BF16))
    attb = E(nc.sbuf_tensor("l1a_attb", [128, S], BF16))
    attTg = E(nc.sbuf_tensor("l1a_attTg", [128, 32, 512], BF16,
                             is_transpose_dest=True))
    ctxc = E(nc.sbuf_tensor("l1a_ctxc", [64, 512], BF16))
    mrow2 = E(nc.sbuf_tensor("l1a_mrow2", [128, 2], F32))
    srow2 = E(nc.sbuf_tensor("l1a_srow2", [128, 2], F32))
    rql = E(nc.sbuf_tensor("l1a_rql", [128, 4], F32))
    eyef = E(nc.sbuf_tensor("l1a_eyef", [128, 128], F32))
    yc = s1k.ap().rearrange("p (c t) -> p c t", t=512)
    ones_bf = E(nc.sbuf_tensor("l1a_ones_bf", [128, 128], BF16))
    ones1 = E(nc.sbuf_tensor("l1a_ones1", [1, 128], F32))
    eyeb = E(nc.sbuf_tensor("l1a_eyeb", [128, 128], BF16))
    blkb = E(nc.sbuf_tensor("l1a_blkb", [128, NPC, 16], BF16))
    selb = E(nc.sbuf_tensor("l1a_selb", [16, 2048], BF16))
    eps_col = E(nc.sbuf_tensor("l1a_eps_col", [128, 1], F32))
    flush_sc = E(nc.sbuf_tensor("l1a_flush_sc", [1, 8], F32))
    epz_col = E(nc.sbuf_tensor("l1a_epz_col", [128, 1], F32))
    P0 = E(nc.psum_tensor("l1a_P0", [128, 512], F32))
    P1 = E(nc.psum_tensor("l1a_P1", [128, 512], F32))
    P2 = E(nc.psum_tensor("l1a_P2", [128, 512], F32))
    P3 = E(nc.psum_tensor("l1a_P3", [128, 512], F32))
    P4 = E(nc.psum_tensor("l1a_P4", [128, 512], F32))
    PN = E(nc.psum_tensor("l1a_PN", [16, 512], F32))
    PC6 = E(nc.psum_tensor("l1a_PC6", [64, 512], F32))
    w1fv = wsb[1].ap().rearrange("p c t -> p (c t)")
    w2fv = wsb[2].ap().rearrange("p c t -> p (c t)")
    sqX = [w1fv[:, 0:S], w2fv[:, 0:S]]
    dbX = [w1fv[:, S:2 * S], w2fv[:, S:2 * S]]
    atX = [attb.ap(), xh.ap().rearrange("p c t -> p (c t)")]
    rrowR = s1k.ap()[0:1, 0:512]
    bcb = s1k.ap()[0:64, 512:1024]
    ctxn = s2k.ap()[0:64, 0:512]
    X1 = [sm.c_x10, sm.c_x11]

    for fn in (lambda e: e.dma_start(out=ones_bf[:], in_=ones_h),
               lambda e: e.dma_start(out=ones1[:], in_=ones_h[0:1, :]),
               lambda e: e.dma_start(out=eyeb[:], in_=eye_h),
               lambda e: e.dma_start(out=eyef[:], in_=eye_h),
               lambda e: e.dma_start(
                   out=blkb[:], in_=blk_h.rearrange("c p n -> p c n")),
               lambda e: e.dma_start(out=selb[:], in_=sel_h),
               lambda e: e.dma_start(out=aqs[:], in_=aqkv),
               lambda e: e.dma_start(
                   out=bcols[:, 0, :],
                   in_=bq.rearrange("(c p) o -> p (c o)", p=128),
                   allow_slow_non_contiguous=True),
               lambda e: e.dma_start(
                   out=bcols[:, 1, :],
                   in_=bk.rearrange("(c p) o -> p (c o)", p=128),
                   allow_slow_non_contiguous=True),
               lambda e: e.dma_start(
                   out=bocol.ap().rearrange("p c o -> p (c o)"),
                   in_=bo.rearrange("(c p) o -> p (c o)", p=128),
                   allow_slow_non_contiguous=True),
               ):
        sch.op("gpsimd", fn, inc=(sm.c_const, 16))
    sch.op("sync", lambda e: e.dma_start(out=s2k[0:1, 0:1024], in_=bvr),
           inc=(sm.c_d0, 16))
    for i, w_h in enumerate((wq, wk, wv)):
        sch.op("gpsimd", lambda e, wh=w_h, j=i: e.dma_start(
            out=wsb[j][:], in_=wh.rearrange("(c p) n -> p c n", p=128)),
            inc=(sm.c_wa, 16))
    sch.op("vector", lambda e: e.memset(eps_col[:], EPS))
    sch.op("vector", lambda e: e.memset(epz_col[:], 1e-12), inc=(sm.dve, 1))
    consts = at(sm.c_const)
    sch.op("tensor", lambda e: e.nop(),
           waits=[consts, at(sm.c_wa), at(sm.c_d0)])
    sch.op("scalar", lambda e: e.nop(),
           waits=[consts, at(sm.dve), at(sm.c_wa)])
    sch.op("vector", lambda e: e.engine_nop(), waits=[consts])

    for j in range(3):
        for ic in range(NPC):
            sch.op("scalar", lambda e, i=ic, w=wsb[j]: e.activation(
                wsqsc[:], w[:, i, :], AF.Square),
                waits=[at(sm.mm)], inc=(sm.act, 1))
            for half in range(2):
                PS = P0 if half == 0 else P1
                sch.op("tensor", lambda e, h=half, i=ic, p=PS: e.matmul(
                    p[:], ones_bf[:], wsqsc[:, h * 512:(h + 1) * 512],
                    start=(i == 0), stop=(i == NPC - 1)),
                    waits=([at(sm.act), at(sm.evv), at(sm.evs)]
                           if half == 0 else [at(sm.act)]),
                    inc=(sm.mm, 1) if half == 1 else None)
        sch.op("vector", lambda e: e.tensor_copy(s1k[:, 0:512], P0[:]),
               waits=[at(sm.mm)])
        sch.op("vector", lambda e: e.tensor_copy(s1k[:, 512:1024], P1[:]),
               inc=(sm.evv, 1))
        if j < 2:
            for pc in range(NPC):
                sch.op("tensor", lambda e, p=pc: e.matmul(
                    P2[:, 0:1], s1k[0:1, p * 128:(p + 1) * 128],
                    ones1[0:1, 0:1], start=True, stop=True),
                    waits=[at(sm.evv), at(sm.evs)], inc=(sm.mm, 1))
                sch.op("vector", lambda e, p=pc, j2=j: e.tensor_scalar(
                    wsqe[:, p, j2:j2 + 1], P2[:, 0:1], 1.0, eps_col[:],
                    ALU.mult, ALU.add),
                    waits=[at(sm.mm)], inc=(sm.evs, 1))
        else:
            sch.op("vector", lambda e: e.tensor_scalar(
                wsqvb[:, 0:512], s1k[:, 0:512], 1.0, eps_col[:],
                ALU.mult, ALU.add))
            sch.op("vector", lambda e: e.tensor_scalar(
                wsqvb[:, 512:1024], s1k[:, 512:1024], 1.0, eps_col[:],
                ALU.mult, ALU.add), inc=(sm.evv, 1))
    sch.op("scalar", lambda e: e.activation(
        aqs[:], aqs[:], AF.Exp, scale=LN_C), inc=(sm.act, 1))
    for j in range(3):
        sch.op("tensor", lambda e, j2=j: e.matmul(
            P2[:, 0:1], ones1[:], aqs[0:1, j2:j2 + 1],
            start=True, stop=True),
            waits=[at(sm.act), at(sm.evs), at(sm.evv)], inc=(sm.mm, 1))
        sch.op("vector", lambda e, j2=j: e.tensor_copy(
            scal[:, j2:j2 + 1], P2[:, 0:1]),
            waits=[at(sm.mm)], inc=(sm.evs, 1))
    for j in range(2):
        sch.op("vector", lambda e, j2=j: e.tensor_scalar(
            bscal[:, j2, :], bcols[:, j2, :], scal[:, j2:j2 + 1], None,
            ALU.mult), waits=[at(sm.evs)], inc=(sm.dve, 1))
    for half in range(2):
        PS = P0 if half == 0 else P1
        sch.op("tensor", lambda e, h=half, p=PS: e.matmul(
            p[:], ones1[:], s2k[0:1, h * 512:(h + 1) * 512],
            start=True, stop=True),
            waits=[at(sm.evv), at(sm.evs)],
            inc=(sm.mm, 1) if half == 1 else None)
    sch.op("vector", lambda e: e.tensor_copy(bvb[:, 0:512], P0[:]),
           waits=[at(sm.mm)])
    sch.op("vector", lambda e: e.tensor_copy(bvb[:, 512:1024], P1[:]),
           inc=(sm.evv, 1))

    for tc in range(8):
        tsl = slice(tc * 512, (tc + 1) * 512)
        emit_ln_xhat(sch, sm, xT, tsl, xc, xq2, xh, xh, P0, P1, scr,
                     ones_bf, eps_col, t1)
        sch.op("scalar", lambda e: e.activation(xq2[:], xh[:], AF.Square),
               waits=[at(sm.xh), at(sm.mm)], inc=(sm.act, 1))
        for pc in range(NPC):
            sch.op("tensor", lambda e, p=pc: e.matmul(
                P0[:], ones_bf[:], xq2[:, p, :],
                start=(p == 0), stop=(p == NPC - 1)),
                waits=[at(sm.act), at(sm.evv), at(sm.evs)] if pc == 0 else [],
                inc=(sm.mm, 1) if pc == NPC - 1 else None)
        sch.op("vector", lambda e: e.tensor_copy(xsqbc[:], P0[:]),
               waits=[at(sm.mm)], inc=(sm.evv, 1))
        for tt in range(4):
            sch.op("tensor", lambda e, t2=tt: e.matmul(
                P1[:, 0:1], xsqbc[0:1, t2 * 128:(t2 + 1) * 128],
                ones1[0:1, 0:1], start=True, stop=True),
                waits=[at(sm.evv), at(sm.evs)], inc=(sm.mm, 1))
            sch.op("vector", lambda e, t2=tt: e.tensor_copy(
                xsqT[:, t2:t2 + 1], P1[:, 0:1]),
                waits=[at(sm.mm)], inc=(sm.evs, 1))
        for j in range(2):
            dsth = (q_scr, k_scr)[j]
            nrm = (qn_sb, kn_sb)[j]
            dsem = (sm.c_d1, sm.c_d2)[j]
            for pc in range(NPC):
                PD = P2 if pc % 2 == 0 else P3
                for ic in range(NPC):
                    w = [at(sm.evv), at(sm.evs), at(sm.xh)] if ic == 0 else []
                    sch.op("tensor",
                           lambda e, i=ic, p=pc, j2=j, ps=PD: e.matmul(
                               ps[:], wsb[j2][:, i, p * 128:(p + 1) * 128],
                               xh[:, i, :],
                               start=(i == 0), stop=(i == NPC - 1)),
                           waits=w,
                           inc=(sm.mm, 1) if ic == NPC - 1 else None)
                sch.op("vector", lambda e, p=pc, j2=j, ps=PD: e.tensor_scalar(
                    scr[:, :, 0], ps[:], -2.0, wsqe[:, p, j2:j2 + 1],
                    ALU.mult, ALU.add), waits=[at(sm.mm), at(sm.dve)],
                    inc=(sm.evv, 1))
                sch.op("scalar", lambda e, ps=PD: e.activation(
                    ysq[:], ps[:], AF.Square),
                    waits=[at(sm.mm), at(sm.act), at(sm.evv)],
                    inc=(sm.evs, 1))
                sch.op("vector", lambda e: e.tensor_add(
                    scr[:, :, 0], scr[:, :, 0], xsqbc[:]),
                    inc=(sm.evv, 1))
                sch.op("scalar", lambda e: recip(e, scr[:, :, 1],
                                                 scr[:, :, 0]),
                       waits=[at(sm.evv)], inc=(sm.act, 1))
                sch.op("vector", lambda e: e.tensor_mul(
                    scr[:, :, 2], ysq[:], scr[:, :, 1]),
                    waits=[at(sm.act)], inc=(sm.dve, 1))
                sch.op("scalar", lambda e, p=pc, j2=j: e.activation(
                    yt[:, 0:512], scr[:, :, 2], AF.Identity,
                    scale=scal[:, j2:j2 + 1], bias=bscal[:, j2, p:p + 1]),
                    waits=[at(dsem), at(sm.c_d3), at(sm.dve)],
                    inc=(sm.act, 1))
                sch.op("gpsimd", lambda e, p=pc, t=tsl, d=dsth: e.dma_start(
                    out=d[p * 128:(p + 1) * 128, t], in_=yt[:, 0:512]),
                    waits=[at(sm.dve)], inc=(dsem, 16))
                sch.op("scalar", lambda e: e.activation(
                    ysq[:], yt[:, 0:512], AF.Square),
                    waits=[at(sm.dve)], inc=(sm.act, 1))
                sch.op("tensor", lambda e, p=pc: e.matmul(
                    PN[:], blkb[:, p, :], ysq[:],
                    start=(p == 0), stop=(p == NPC - 1)),
                    waits=([at(sm.act), at(sm.evs)] if pc == 0
                           else [at(sm.act)]),
                    inc=(sm.mm, 1))
            sch.op("vector", lambda e, t=tsl, n=nrm: e.tensor_copy(
                n[0:16, t], PN[:]), waits=[at(sm.mm)], inc=(sm.evs, 1))
        for tt in range(4):
            for half in range(2):
                PD = P2 if half == 0 else P3
                for ic in range(NPC):
                    w = [at(sm.evv), at(sm.evs)] if ic == 0 else []
                    sch.op("tensor", lambda e, i=ic, t2=tt, h=half, ps=PD:
                           e.matmul(ps[:],
                                    xh[:, i, t2 * 128:(t2 + 1) * 128],
                                    wsb[2][:, i, h * 512:(h + 1) * 512],
                                    start=(i == 0), stop=(i == NPC - 1)),
                           waits=w,
                           inc=(sm.mm, 1) if ic == NPC - 1 else None)
            for half in range(2):
                PD = P2 if half == 0 else P3
                hs = slice(half * 512, (half + 1) * 512)
                sch.op("vector", lambda e, t2=tt, ps=PD, h=hs: e.tensor_scalar(
                    s1k[:, h], ps[:], -2.0, xsqT[:, t2:t2 + 1],
                    ALU.mult, ALU.add), waits=[at(sm.mm), at(sm.dve)],
                    inc=(sm.evv, 1))
                sch.op("scalar", lambda e, ps=PD, h=hs: e.activation(
                    yt[:, h], ps[:], AF.Square),
                    waits=[at(sm.mm), at(sm.act), at(sm.evv),
                           at(sm.c_d1), at(sm.c_d2)],
                    inc=(sm.evs, 1))
            sch.op("vector", lambda e: e.tensor_add(
                s1k[:], s1k[:], wsqvb[:]), inc=(sm.evv, 1))
            sch.op("scalar", lambda e: recip(e, s2k[:], s1k[:]),
                   waits=[at(sm.evv)], inc=(sm.act, 1))
            sch.op("vector", lambda e: e.tensor_mul(s1k[:], yt[:, 0:1024],
                                                    s2k[:]),
                   waits=[at(sm.act)])
            sch.op("vector", lambda e: e.tensor_add(
                s1k[:], s1k[:], bvb[:]), inc=(sm.dve, 1))
            sch.op("scalar", lambda e: e.activation(
                yt[:, 0:1024], s1k[:], AF.Identity, scale=scal[:, 2:3]),
                waits=[at(sm.c_d3), at(sm.dve)], inc=(sm.act, 1))
            sch.op("gpsimd", lambda e, t2=tt, c=tc: e.dma_start(
                out=v_scr[(c * 4 + t2) * 128:(c * 4 + t2 + 1) * 128, :],
                in_=yt[:, 0:1024]),
                waits=[at(sm.act)], inc=(sm.c_d3, 16))
    for tt in range(32):
        sch.op("tensor", lambda e, t2=tt: e.matmul(
            P2[:, 0:16], qn_sb[0:16, t2 * 128:(t2 + 1) * 128],
            eyeb[0:16, 0:16], start=True, stop=True),
            waits=[at(sm.evv), at(sm.evs)], inc=(sm.mm, 1))
        sch.op("vector", lambda e, t2=tt: e.tensor_scalar(
            qnT[:, t2, :], P2[:, 0:16], 1.0, eps_col[:],
            ALU.mult, ALU.add), waits=[at(sm.mm)], inc=(sm.evs, 1))
    pP_done = [at(sm.c_d1), at(sm.c_d2), at(sm.c_d3)]

    for h in range(H):
        hs = slice(h * 64, (h + 1) * 64)
        sch.op("sync", lambda e, s_=hs: e.dma_start(
            out=qh_sb[:], in_=q_scr[s_, :]),
            waits=pP_done + [at(sm.mm), at(sm.pv)], inc=(sm.c_d4, 16))
        sch.op("sync", lambda e, s_=hs: e.dma_start(
            out=kh_sb[:], in_=k_scr[s_, :]), inc=(sm.c_d4, 16))
        sch.op("sync", lambda e, s_=hs: e.dma_start(
            out=vh_sb[:], in_=v_scr[:, s_].rearrange(
                "(t p) c -> p t c", p=128)), inc=(sm.c_d4, 16))
        qkv_pt = at(sm.c_d4)
        for kc in range(NPC):
            sch.op("tensor", lambda e, k=kc, h2=h: e.matmul(
                P4[:], selb[:, h2 * 128:(h2 + 1) * 128],
                kn_sb[0:16, k * 512:(k + 1) * 512],
                start=True, stop=True),
                waits=[at(sm.dve), at(sm.evs)], inc=(sm.mm, 1))
            sch.op("scalar", lambda e, k=kc: e.copy(
                kbc[:, k, :], P4[:]), waits=[at(sm.mm), at(sm.dve)],
                inc=(sm.evs, 1))
        kbc_pt = at(sm.evs)
        for pr in range(16):
            qts = (2 * pr, 2 * pr + 1)
            for x in (0, 1):
                qt = qts[x]
                for kc in range(NPC):
                    PS = P0 if kc % 2 == 0 else P1
                    sch.op("tensor", lambda e, k=kc, q=qt, ps=PS: e.matmul(
                        ps[:], qh_sb[:, q * 128:(q + 1) * 128],
                        kh_sb[:, k * 512:(k + 1) * 512],
                        start=True, stop=True),
                        waits=[qkv_pt, lag(sm.smk, 1)], inc=(sm.mm, 1))
                    w = [at(sm.mm)]
                    if kc == 0:
                        w.append(lag(sm.dve, 1))
                    sch.op("scalar", lambda e, k=kc, ps=PS, x2=x:
                           e.activation(sqX[x2][:, k * 512:(k + 1) * 512],
                                        ps[:], AF.Square),
                           waits=w, inc=(sm.smk, 1))
                sch.op("vector", lambda e, q=qt, x2=x: e.tensor_scalar(
                    dbX[x2][:], sqX[x2][:], -2.0, qnT[:, q, h:h + 1],
                    ALU.mult, ALU.add),
                    waits=[at(sm.smk), at(sm.act), kbc_pt])
                sch.op("vector", lambda e, x2=x: e.tensor_add(
                    dbX[x2][:], dbX[x2][:],
                    kbc.ap().rearrange("p c t -> p (c t)")),
                    inc=(sm.dve, 1))
            pin_d = at(sm.dve)
            pin_max = [None, None]
            for x in (0, 1):
                sch.op("scalar", lambda e, x2=x: recip(
                    e, atX[x2][:], dbX[x2][:], bias=epz_col[:]),
                    waits=[pin_d, at(sm.c_d5)], inc=(sm.act, 1))
                pr_rc = at(sm.act)
                sch.op("vector", lambda e, x2=x: e.tensor_mul(
                    dbX[x2][:], sqX[x2][:], atX[x2][:]), waits=[pr_rc])
                sch.op("vector", lambda e, x2=x: e.reduce_max(
                    mrow2[:, x2:x2 + 1], dbX[x2][:], axis=AX.X, negate=True),
                    inc=(sm.dve, 1))
                pin_max[x] = at(sm.dve)
            for x in (0, 1):
                qi = qts[x] % 4
                sch.op("scalar", lambda e, x2=x: e.activation(
                    atX[x2][:], dbX[x2][:], AF.Exp, bias=mrow2[:, x2:x2 + 1],
                    accum_out=srow2[:, x2:x2 + 1]),
                    waits=[pin_max[x], at(sm.evv)], inc=(sm.act, 1))
                pin_exp = at(sm.act)
                sch.op("sync", lambda e, q=qi, x2=x: e.dma_start(
                    out=attTg[:, :, q * 128:(q + 1) * 128], in_=atX[x2][:],
                    transpose=True),
                    waits=[pin_exp, at(sm.pv)], inc=(sm.c_d5, 16))
                sch.op("vector", lambda e, q=qi, x2=x: e.reciprocal(
                    rql[:, q:q + 1], srow2[:, x2:x2 + 1]),
                    waits=[pin_exp], inc=(sm.evv, 1))
            if pr % 2 == 1:
                qg = pr // 2
                for kt in range(32):
                    sch.op("tensor", lambda e, k=kt: e.matmul(
                        PC6[:], vh_sb[:, k, :], attTg[:, k, :],
                        start=(k == 0), stop=(k == 31)),
                        waits=([qkv_pt, at(sm.c_d5), at(sm.evv)] if kt == 0
                               else []),
                        inc=(sm.pv, 1) if kt == 31 else None)
                pin_pv = at(sm.pv)
                for qi in range(4):
                    sch.op("tensor", lambda e, q=qi: e.matmul(
                        P2[0:1, q * 128:(q + 1) * 128], rql[:, q:q + 1],
                        eyef[:], start=True, stop=True),
                        waits=[at(sm.evv)], inc=(sm.mm, 1))
                sch.op("vector", lambda e: e.tensor_copy(
                    rrowR[0:1, :], P2[0:1, 0:512]),
                    waits=[at(sm.mm)], inc=(sm.evv, 1))
                sch.op("tensor", lambda e: e.matmul(
                    P3[0:64, :], ones1[0:1, 0:64], rrowR[0:1, :],
                    start=True, stop=True),
                    waits=[at(sm.evv)], inc=(sm.mm, 1))
                sch.op("vector", lambda e: e.tensor_copy(
                    bcb[:], P3[0:64, :]), waits=[at(sm.mm)],
                    inc=(sm.evv, 1))
                sch.op("vector", lambda e: e.tensor_mul(
                    ctxn[:], PC6[:], bcb[:]),
                    waits=[pin_pv, at(sm.evs)], inc=(sm.evv, 1))
                sch.op("scalar", lambda e: e.activation(
                    ctxc[:], ctxn[:], AF.Copy),
                    waits=[at(sm.evv), at(sm.c_d6)], inc=(sm.evs, 1))
                sch.op("gpsimd", lambda e, s_=hs, g=qg: e.dma_start(
                    out=c_scr[s_, g * 512:(g + 1) * 512], in_=ctxc[:]),
                    waits=[at(sm.evs)], inc=(sm.c_d6, 16))

    sch.op("gpsimd", lambda e: e.dma_start(
        out=wsb[0][:], in_=wo.rearrange("(c p) n -> p c n", p=128)),
        waits=[at(sm.mm)], inc=(sm.c_wb, 16))
    ctx_pt = at(sm.c_d6)
    for g in range(8):
        gsl = slice(g * 512, (g + 1) * 512)
        sch.op("sync", lambda e, t=gsl: e.dma_start(
            out=xh[:], in_=hbm_fm(c_scr, t)),
            waits=[ctx_pt, at(sm.mm), at(sm.c_d5)], inc=(sm.c_d0, 16))
        sch.op("sync", lambda e, t=gsl: e.dma_start(
            out=xc[:], in_=hbm_fm(xT, t)), waits=[at(sm.xcfree)],
            inc=(sm.c_xc, 16))
        for pc in range(NPC):
            PS = P2 if pc % 2 == 0 else P3
            for ic in range(NPC):
                w = [at(sm.c_d0), at(sm.c_wb), at(sm.evv), at(sm.evs)] \
                    if ic == 0 else []
                sch.op("tensor", lambda e, i=ic, p=pc, ps=PS: e.matmul(
                    ps[:], wsb[0][:, i, p * 128:(p + 1) * 128],
                    xh[:, i, :],
                    start=(i == 0), stop=(i == NPC - 1)), waits=w,
                    inc=(sm.mm, 1) if ic == NPC - 1 else None)
            sch.op("scalar", lambda e, p=pc, ps=PS: e.activation(
                t1[:], ps[:], AF.Identity, bias=bocol[:, p, 0:1]),
                waits=[at(sm.mm), at(sm.evv)], inc=(sm.evs, 1))
            sch.op("vector", lambda e, p=pc: e.tensor_add(
                t1[:], t1[:], xc[:, p, :]),
                waits=[at(sm.evs), at(sm.c_xc)], inc=(sm.evv, 1))
            sch.op("scalar", lambda e, p=pc: e.activation(
                yc[:, p % 2, :], t1[:], AF.Copy),
                waits=[at(sm.evv), at(X1[pc % 2])], inc=(sm.evs, 1))
            sch.op("gpsimd", lambda e, p=pc, t=gsl: e.dma_start(
                out=hbm_fm(x3T, t)[:, p, :], in_=yc[:, p % 2, :]),
                waits=[at(sm.evs)], inc=(X1[pc % 2], 16))
        sch.op("vector", lambda e: e.engine_nop(), waits=[at(sm.mm)],
               inc=(sm.xcfree, 1))
    sch.op("gpsimd", lambda e: e.nop(), waits=[at(X1[0]), at(X1[1])])


def emit_l1f(nc, sch, sm, ctx, io):
    xT = io['l1f_xT']
    cenT = io['l1f_cenT']
    lngT = io['l1f_lngT']
    lnbT = io['l1f_lnbT']
    w1S = io['l1f_w1S']
    w2T = io['l1f_w2T']
    b1T = io['l1f_b1T']
    b2T = io['l1f_b2T']
    iot_h = io['l1f_iot_h']
    sel8_h = io['l1f_sel8_h']
    ones_h = io['l1f_ones_h']
    eye_h = io['l1f_eye_h']
    outT = io['l1f_outT']
    NF = 32
    E = ctx.enter_context
    hbuf = E(nc.sbuf_tensor("l1f_hbuf", [128, NF, 512], BF16))
    xc = E(nc.sbuf_tensor("l1f_xc", [128, NPC, 512], F32))
    xhat = E(nc.sbuf_tensor("l1f_xhat", [128, NPC, 512], BF16))
    xn = E(nc.sbuf_tensor("l1f_xn", [128, NPC, 512], BF16))
    scr = E(nc.sbuf_tensor("l1f_scr", [128, 512, 3], F32))
    t1 = E(nc.sbuf_tensor("l1f_t1", [128, 512], F32))
    t2 = E(nc.sbuf_tensor("l1f_t2", [128, 512], F32))
    w1s = E(nc.sbuf_tensor("l1f_w1s", [128, 4, NPC, 128], BF16))
    w2s = E(nc.sbuf_tensor("l1f_w2s", [128, 4, D], BF16))
    cenb = E(nc.sbuf_tensor("l1f_cenb", [128, NPC, 8], F32))
    lng = E(nc.sbuf_tensor("l1f_lng", [128, NPC, 8], F32))
    lnb = E(nc.sbuf_tensor("l1f_lnb", [128, NPC, 8], F32))
    b1c = E(nc.sbuf_tensor("l1f_b1c", [128, NF, 8], F32))
    b2c = E(nc.sbuf_tensor("l1f_b2c", [128, NPC, 8], F32))
    afft = E(nc.sbuf_tensor("l1f_afft", [128, 4, 8], F32))
    gmask = E(nc.sbuf_tensor("l1f_gmask", [128, 4, 8], F32))
    gE = E(nc.sbuf_tensor("l1f_gE", [8, 512], F32))
    gbc = E(nc.sbuf_tensor("l1f_gbc", [128, 512], F32))
    yco = E(nc.sbuf_tensor("l1f_yco", [128, 2, 512], F32))
    m8 = E(nc.sbuf_tensor("l1f_m8", [128, 1], F32))
    iot = E(nc.sbuf_tensor("l1f_iot", [128, 8], F32))
    sel8 = E(nc.sbuf_tensor("l1f_sel8", [8, 1024], F32))
    ones_bf = E(nc.sbuf_tensor("l1f_ones_bf", [128, 128], BF16))
    eyef = E(nc.sbuf_tensor("l1f_eyef", [128, 128], F32))
    eps_col = E(nc.sbuf_tensor("l1f_eps_col", [128, 1], F32))
    flush_sc = E(nc.sbuf_tensor("l1f_flush_sc", [1, 8], F32))
    P0 = E(nc.psum_tensor("l1f_P0", [128, 512], F32))
    P1 = E(nc.psum_tensor("l1f_P1", [128, 512], F32))
    P2 = E(nc.psum_tensor("l1f_P2", [128, 512], F32))
    P3 = E(nc.psum_tensor("l1f_P3", [128, 512], F32))
    P4 = E(nc.psum_tensor("l1f_P4", [128, 512], F32))
    P5 = E(nc.psum_tensor("l1f_P5", [128, 512], F32))
    P6 = E(nc.psum_tensor("l1f_P6", [128, 512], F32))
    P7 = E(nc.psum_tensor("l1f_P7", [128, 512], F32))
    PD8 = [P0, P1, P2, P3, P4, P5, P6, P7]
    X1 = [sm.c_x10, sm.c_x11]
    W1S = [sm.c_d1, sm.c_d2, sm.w1a, sm.w1b]
    W2S = [sm.c_d3, sm.c_d4, sm.w2a, sm.w2b]

    for fn in (lambda e: e.dma_start(out=ones_bf[:], in_=ones_h),
               lambda e: e.dma_start(out=eyef[:], in_=eye_h),
               lambda e: e.dma_start(out=iot[:], in_=iot_h),
               lambda e: e.dma_start(out=sel8[:], in_=sel8_h),
               lambda e: e.dma_start(
                   out=cenb[:], in_=cenT.rearrange("(c p) n -> p c n", p=128)),
               lambda e: e.dma_start(
                   out=lng[:], in_=lngT.rearrange("(c p) n -> p c n", p=128)),
               lambda e: e.dma_start(
                   out=lnb[:], in_=lnbT.rearrange("(c p) n -> p c n", p=128)),
               lambda e: e.dma_start(
                   out=b1c[:], in_=b1T.rearrange("(c p) n -> p c n", p=128)),
               lambda e: e.dma_start(
                   out=b2c[:], in_=b2T.rearrange("(c p) n -> p c n", p=128)),
               ):
        sch.op("gpsimd", fn, inc=(sm.c_const, 16))
    sch.op("vector", lambda e: e.memset(eps_col[:], EPS), inc=(sm.dve, 1))
    consts = at(sm.c_const)
    sch.op("tensor", lambda e: e.nop(), waits=[consts])
    sch.op("scalar", lambda e: e.nop(), waits=[consts, at(sm.dve)])
    sch.op("vector", lambda e: e.engine_nop(), waits=[consts])

    for tc in range(8):
        tsl = slice(tc * 512, (tc + 1) * 512)
        emit_ln_xhat(sch, sm, xT, tsl, xc, xn, xhat, xhat, P0, P1, scr,
                     ones_bf, eps_col, t1)
        for tt in range(4):
            for ic in range(NPC):
                w = [at(sm.evv), at(sm.evs), at(sm.xh)] if ic == 0 else []
                sch.op("tensor", lambda e, i=ic, t3=tt: e.matmul(
                    P0[:, 0:8], xc[:, i, t3 * 128:(t3 + 1) * 128],
                    cenb[:, i, :], start=(i == 0), stop=(i == NPC - 1)),
                    waits=w, inc=(sm.mm, 1) if ic == NPC - 1 else None)
            sch.op("vector", lambda e, t3=tt: e.tensor_copy(
                afft[:, t3, :], P0[:, 0:8]), waits=[at(sm.mm)],
                inc=(sm.evv, 1))
            sch.op("vector", lambda e, t3=tt: e.reduce_max(
                m8[:], afft[:, t3, :], axis=AX.X))
            sch.op("vector", lambda e, t3=tt: e.tensor_scalar(
                gmask[:, t3, :], afft[:, t3, :], m8[:], None,
                ALU.is_equal))
            sch.op("vector", lambda e, t3=tt: e.scalar_tensor_tensor(
                gmask[:, t3, :], gmask[:, t3, :], 1000.0, iot[:],
                ALU.mult, ALU.add))
            sch.op("vector", lambda e, t3=tt: e.reduce_max(
                m8[:], gmask[:, t3, :], axis=AX.X))
            sch.op("vector", lambda e, t3=tt: e.tensor_scalar(
                gmask[:, t3, :], gmask[:, t3, :], m8[:], None,
                ALU.is_equal), inc=(sm.dve, 1))
            sch.op("scalar", lambda e, t3=tt: e.activation(
                afft[:, t3, :], afft[:, t3, :], AF.Tanh, scale=0.5),
                waits=[at(sm.dve)], inc=(sm.act, 1))
            sch.op("vector", lambda e, t3=tt: e.tensor_scalar(
                afft[:, t3, :], afft[:, t3, :], 0.5, 0.5,
                ALU.mult, ALU.add), waits=[at(sm.act)])
            sch.op("vector", lambda e, t3=tt: e.tensor_mul(
                gmask[:, t3, :], gmask[:, t3, :], afft[:, t3, :]),
                inc=(sm.dve, 1))
            sch.op("tensor", lambda e, t3=tt: e.matmul(
                P1[0:8, 0:128], gmask[:, t3, :], eyef[:],
                start=True, stop=True),
                waits=[at(sm.dve), at(sm.evv), at(sm.evs)],
                inc=(sm.mm, 1))
            sch.op("vector", lambda e, t3=tt: e.tensor_copy(
                gE[0:8, t3 * 128:(t3 + 1) * 128], P1[0:8, 0:128]),
                waits=[at(sm.mm)], inc=(sm.evv, 1))
        for ex in range(8):
            sch.op("tensor", lambda e, x2=ex: e.matmul(
                P0[:], sel8[:, x2 * 128:(x2 + 1) * 128], gE[:],
                start=True, stop=True),
                waits=[at(sm.evv), at(sm.evs)], inc=(sm.mm, 1))
            sch.op("vector", lambda e: e.tensor_copy(gbc[:], P0[:]),
                   waits=[at(sm.mm)], inc=(sm.evv, 1))
            for pc in range(NPC):
                sch.op("vector", lambda e, p=pc, x2=ex: e.tensor_scalar(
                    xn[:, p, :], xhat[:, p, :], lng[:, p, x2:x2 + 1],
                    lnb[:, p, x2:x2 + 1], ALU.mult, ALU.add),
                    waits=[at(sm.mm)] if pc == 0 else [],
                    inc=(sm.dve, 1) if pc == NPC - 1 else None)
            for ft in range(NF):
                PH = P6 if ft % 2 == 0 else P7
                sl = ft % 4
                sch.op("gpsimd", lambda e, x2=ex, f=ft, s2=sl: e.dma_start(
                    out=w1s[:, s2, :, :],
                    in_=w1S[x2, f].rearrange("p (c n) -> p c n", n=128)),
                    waits=[lag(sm.mm, 3)],
                    inc=(W1S[sl], 16))
                for ic in range(NPC):
                    w = [at(W1S[sl]), at(sm.dve), at(sm.evs), at(sm.evv)] \
                        if ic == 0 else []
                    sch.op("tensor", lambda e, i=ic, s2=sl, p=PH: e.matmul(
                        p[:], w1s[:, s2, i, :], xn[:, i, :],
                        start=(i == 0), stop=(i == NPC - 1)), waits=w,
                        inc=(sm.mm, 1) if ic == NPC - 1 else None)
                sch.op("scalar", lambda e, f=ft, x2=ex, p=PH: e.activation(
                    hbuf[:, f, :], p[:], AF.Relu, bias=b1c[:, f, x2:x2 + 1]),
                    waits=[at(sm.mm)], inc=(sm.evs, 1))
            for ft in range(NF):
                sl = ft % 4
                sch.op("gpsimd", lambda e, x2=ex, f=ft, s2=sl: e.dma_start(
                    out=w2s[:, s2, :],
                    in_=w2T[x2, f * 128:(f + 1) * 128, :]),
                    waits=[lag(sm.mm, 3)], inc=(W2S[sl], 16))
                for pc in range(NPC):
                    w = [at(W2S[sl]), at(sm.evs)] if pc == 0 else []
                    if ft == 0 and pc == 0:
                        w.append(at(sm.evv))
                    sch.op("tensor", lambda e, f=ft, p=pc, s2=sl: e.matmul(
                        PD8[p][:], w2s[:, s2, p * 128:(p + 1) * 128],
                        hbuf[:, f, :],
                        start=(f == 0), stop=(f == NF - 1)), waits=w,
                        inc=(sm.mm, 1) if pc == NPC - 1 else None)
            for pc in range(NPC):
                sch.op("scalar", lambda e, p=pc, x2=ex: e.activation(
                    t1[:], PD8[p][:], AF.Identity, bias=b2c[:, p, x2:x2 + 1]),
                    waits=[at(sm.mm), at(sm.evv)], inc=(sm.evs, 1))
                sch.op("vector", lambda e: e.tensor_mul(t2[:], t1[:],
                                                        gbc[:]),
                       waits=[at(sm.evs)])
                sch.op("vector", lambda e, p=pc: e.tensor_add(
                    xc[:, p, :], xc[:, p, :], t2[:]),
                    inc=(sm.evv, 1))
        for pc in range(NPC):
            sch.op("scalar", lambda e, p=pc, b2=pc % 2: e.activation(
                t1[:] if False else yco[:, b2, :], xc[:, p, :], AF.Copy),
                waits=[at(sm.evv), at(X1[pc % 2])], inc=(sm.evs, 1))
            sch.op("gpsimd", lambda e, p=pc, t=tsl, b2=pc % 2: e.dma_start(
                out=hbm_fm(outT, t)[:, p, :], in_=yco[:, b2, :]),
                waits=[at(sm.evs)], inc=(X1[pc % 2], 16))
        sch.op("vector", lambda e: e.engine_nop(),
               waits=[at(X1[0]), at(X1[1])], inc=(sm.xcfree, 1))
    sch.op("gpsimd", lambda e: e.nop(), waits=[at(X1[0]), at(X1[1])])


def barrier(sch, sm):
    sch.op('vector', lambda e: e.engine_nop(), inc=(sm.dve, 1))
    allsems = [at(getattr(sm, n)) for n in SEM_NAMES]
    for eng in ('tensor', 'gpsimd', 'sync'):
        sch.op(eng, lambda e: e.nop(), waits=allsems)
    sch.op('vector', lambda e: e.engine_nop(), waits=allsems)
    sch.op('scalar', lambda e: e.nop(), waits=allsems)


def build_fused():
    import contextlib
    nc = bass.Bass("TRN2", target_bir_lowering=False)
    io = {}
    io['l0a_xT'] = nc.dram_tensor('l0a_xT', [D, S], F32, kind="ExternalInput").ap()
    io['l0a_wqT'] = nc.dram_tensor('l0a_wqT', [D, D], F32, kind="ExternalInput").ap()
    io['l0a_wkT'] = nc.dram_tensor('l0a_wkT', [D, D], F32, kind="ExternalInput").ap()
    io['l0a_wvT'] = nc.dram_tensor('l0a_wvT', [D, D], F32, kind="ExternalInput").ap()
    io['l0a_woT'] = nc.dram_tensor('l0a_woT', [D, D], F32, kind="ExternalInput").ap()
    io['l0a_masks'] = nc.dram_tensor('l0a_masks', [4, 128, 512], F32, kind="ExternalInput").ap()
    io['l0a_ones_h'] = nc.dram_tensor('l0a_ones_h', [128, 128], F32, kind="ExternalInput").ap()
    io['l0a_x1T'] = nc.dram_tensor('l0a_x1T', [D, S], F32).ap()
    io['l0a_v_scr'] = nc.dram_tensor('l0a_v_scr', [S, D], BF16).ap()
    io['l0a_k_scr'] = nc.dram_tensor('l0a_k_scr', [D, S], BF16).ap()
    io['l0f_xT'] = io['l0a_x1T']
    io['l0f_wgT'] = nc.dram_tensor('l0f_wgT', [D, 3072], I32, kind="ExternalInput").ap()
    io['l0f_wuT'] = nc.dram_tensor('l0f_wuT', [D, 3072], I32, kind="ExternalInput").ap()
    io['l0f_wdT'] = nc.dram_tensor('l0f_wdT', [3072, D], I32, kind="ExternalInput").ap()
    io['l0f_ones_h'] = io['l0a_ones_h']
    io['l0f_yT'] = nc.dram_tensor('l0f_yT', [D, S], F32).ap()
    io['l1a_xT'] = io['l0f_yT']
    io['l1a_wq'] = nc.dram_tensor('l1a_wq', [D, D], F32, kind="ExternalInput").ap()
    io['l1a_wk'] = nc.dram_tensor('l1a_wk', [D, D], F32, kind="ExternalInput").ap()
    io['l1a_wv'] = nc.dram_tensor('l1a_wv', [D, D], F32, kind="ExternalInput").ap()
    io['l1a_wo'] = nc.dram_tensor('l1a_wo', [D, D], F32, kind="ExternalInput").ap()
    io['l1a_bq'] = nc.dram_tensor('l1a_bq', [D, 1], F32, kind="ExternalInput").ap()
    io['l1a_bk'] = nc.dram_tensor('l1a_bk', [D, 1], F32, kind="ExternalInput").ap()
    io['l1a_bvr'] = nc.dram_tensor('l1a_bvr', [1, D], F32, kind="ExternalInput").ap()
    io['l1a_bo'] = nc.dram_tensor('l1a_bo', [D, 1], F32, kind="ExternalInput").ap()
    io['l1a_aqkv'] = nc.dram_tensor('l1a_aqkv', [1, 3], F32, kind="ExternalInput").ap()
    io['l1a_ones_h'] = io['l0a_ones_h']
    io['l1a_eye_h'] = nc.dram_tensor('l1a_eye_h', [128, 128], F32, kind="ExternalInput").ap()
    io['l1a_blk_h'] = nc.dram_tensor('l1a_blk_h', [8, 128, 16], F32, kind="ExternalInput").ap()
    io['l1a_sel_h'] = nc.dram_tensor('l1a_sel_h', [16, 2048], F32, kind="ExternalInput").ap()
    io['l1a_x3T'] = nc.dram_tensor('l1a_x3T', [D, S], F32).ap()
    io['l1a_q_scr'] = nc.dram_tensor('l1a_q_scr', [D, S], BF16, kind="Internal").ap()
    io['l1a_k_scr'] = nc.dram_tensor('l1a_k_scr', [D, S], BF16, kind="Internal").ap()
    io['l1a_v_scr'] = nc.dram_tensor('l1a_v_scr', [S, D], BF16, kind="Internal").ap()
    io['l1a_c_scr'] = nc.dram_tensor('l1a_c_scr', [D, S], BF16, kind="Internal").ap()
    io['l1f_xT'] = io['l1a_x3T']
    io['l1f_cenT'] = nc.dram_tensor('l1f_cenT', [D, 8], F32, kind="ExternalInput").ap()
    io['l1f_lngT'] = nc.dram_tensor('l1f_lngT', [D, 8], F32, kind="ExternalInput").ap()
    io['l1f_lnbT'] = nc.dram_tensor('l1f_lnbT', [D, 8], F32, kind="ExternalInput").ap()
    io['l1f_w1S'] = nc.dram_tensor('l1f_w1S', [8, 32, 128, 1024], F32, kind="ExternalInput").ap()
    io['l1f_w2T'] = nc.dram_tensor('l1f_w2T', [8, 4096, D], F32, kind="ExternalInput").ap()
    io['l1f_b1T'] = nc.dram_tensor('l1f_b1T', [4096, 8], F32, kind="ExternalInput").ap()
    io['l1f_b2T'] = nc.dram_tensor('l1f_b2T', [D, 8], F32, kind="ExternalInput").ap()
    io['l1f_iot_h'] = nc.dram_tensor('l1f_iot_h', [128, 8], F32, kind="ExternalInput").ap()
    io['l1f_sel8_h'] = nc.dram_tensor('l1f_sel8_h', [8, 1024], F32, kind="ExternalInput").ap()
    io['l1f_ones_h'] = io['l0a_ones_h']
    io['l1f_eye_h'] = io['l1a_eye_h']
    io['l1f_outT'] = nc.dram_tensor('l1f_outT', [D, S], F32, kind="ExternalOutput").ap()
    with contextlib.ExitStack() as outer:
        sm = SM(nc, outer, SEM_NAMES)
        sch = Sch([getattr(sm, n) for n in SEM_NAMES])
        flush_g = outer.enter_context(nc.sbuf_tensor("gflush", [1, 8], F32))
        for px, fn in (("l0a", emit_l0a), ("l0f", emit_l0f),
                       ("l1a", emit_l1a), ("l1f", emit_l1f)):
            with contextlib.ExitStack() as ph:
                fn(nc, sch, sm, ph, io)
            barrier(sch, sm)
        blk = outer.enter_context(nc.Block())

        @blk.tensor
        def _(t):
            sch.emit("tensor", t)

        @blk.vector
        def _(v):
            sch.emit("vector", v, flush=lambda e: e.engine_nop())

        @blk.scalar
        def _(s):
            sch.emit("scalar", s,
                     flush=lambda e: e.memzero(flush_g[0:1, 0:8]))

        @blk.gpsimd
        def _(g):
            sch.emit("gpsimd", g)

        @blk.sync
        def _(sy):
            sch.emit("sync", sy)

    return nc


_cache = {}


def _get(name, builder):
    if name not in _cache:
        _cache[name] = builder()
    return _cache[name]


def _masks_np():
    m = np.zeros((4, 128, 512), np.float32)
    for k in range(4):
        for i in range(128):
            m[k, i, k * 128 + i + 1:] = NEG
    return m


def _blk_np():
    b = np.zeros((8, 128, 16), np.float32)
    for pc in range(8):
        for r in range(128):
            b[pc, r, 2 * pc + r // 64] = 1.0
    return b


def _sel16_np():
    s = np.zeros((16, 2048), np.float32)
    for h in range(16):
        s[h, h * 128:(h + 1) * 128] = 1.0
    return s


def _sel8_np():
    s = np.zeros((8, 1024), np.float32)
    for e in range(8):
        s[e, e * 128:(e + 1) * 128] = 1.0
    return s


def _t(a):
    return np.ascontiguousarray(np.asarray(a).T)


def kernel(**inputs):
    x = np.asarray(inputs["L0_attn_in_features"])
    ones128 = np.ones((128, 128), np.float32)
    eye128 = np.eye(128, dtype=np.float32)
    D_ = 1024
    base = {
        "l0a_wqT": _t(inputs["L0_attn_w_q"]), "l0a_wkT": _t(inputs["L0_attn_w_k"]),
        "l0a_wvT": _t(inputs["L0_attn_w_v"]), "l0a_woT": _t(inputs["L0_attn_w_o"]),
        "l0a_masks": _masks_np(), "l0a_ones_h": ones128,
        "l0f_wgT": _t(inputs["L0_ffn_w_gate"]), "l0f_wuT": _t(inputs["L0_ffn_w_up"]),
        "l0f_wdT": _t(inputs["L0_ffn_w_down"]),
        "l1a_wq": np.ascontiguousarray(np.asarray(inputs["L1_attn_wq"], np.float32)),
        "l1a_wk": np.ascontiguousarray(np.asarray(inputs["L1_attn_wk"], np.float32)),
        "l1a_wv": np.ascontiguousarray(np.asarray(inputs["L1_attn_wv"], np.float32)),
        "l1a_wo": np.ascontiguousarray(np.asarray(inputs["L1_attn_wo"], np.float32)),
        "l1a_bq": np.asarray(inputs["L1_attn_bq"], np.float32).reshape(D_, 1),
        "l1a_bk": np.asarray(inputs["L1_attn_bk"], np.float32).reshape(D_, 1),
        "l1a_bvr": np.asarray(inputs["L1_attn_bv"], np.float32).reshape(1, D_),
        "l1a_bo": np.asarray(inputs["L1_attn_bo"], np.float32).reshape(D_, 1),
        "l1a_aqkv": np.stack([np.asarray(inputs["L1_attn_aq"], np.float32)[0],
                              np.asarray(inputs["L1_attn_ak"], np.float32)[0],
                              np.asarray(inputs["L1_attn_av"], np.float32)[0]]
                             ).reshape(1, 3),
        "l1a_eye_h": eye128, "l1a_blk_h": _blk_np(), "l1a_sel_h": _sel16_np(),
        "l1f_cenT": _t(inputs["L1_ffn_centroids"]),
        "l1f_lngT": _t(inputs["L1_ffn_ln_g"]), "l1f_lnbT": _t(inputs["L1_ffn_ln_b"]),
        "l1f_w1S": np.ascontiguousarray(
            np.asarray(inputs["L1_ffn_w1"], np.float32).reshape(
                8, 32, 128, 8, 128).transpose(0, 1, 4, 3, 2).reshape(
                8, 32, 128, 1024)),
        "l1f_w2T": np.ascontiguousarray(
            np.asarray(inputs["L1_ffn_w2"], np.float32).transpose(0, 2, 1)),
        "l1f_b1T": _t(inputs["L1_ffn_b1"]), "l1f_b2T": _t(inputs["L1_ffn_b2"]),
        "l1f_iot_h": np.broadcast_to(-np.arange(8, dtype=np.float32),
                                     (128, 8)).copy(),
        "l1f_sel8_h": _sel8_np(),
    }
    nc = _get("fused", build_fused)
    im = [dict(base, l0a_xT=_t(x[b])) for b in range(8)]
    res = run_bass_kernel_spmd(nc, im, core_ids=list(range(8)))
    out = np.stack([np.asarray(r["l1f_outT"]).T for r in res.results])
    return np.ascontiguousarray(out)


# version 08 (final)
# speedup vs baseline: 1.6124x; 1.1018x over previous
import numpy as np
import ml_dtypes
import concourse.bass as bass
import concourse.mybir as mybir
from concourse.bass_utils import run_bass_kernel_spmd

F32 = mybir.dt.float32
BF16 = mybir.dt.bfloat16
I32 = mybir.dt.int32
AF = mybir.ActivationFunctionType
ALU = mybir.AluOpType
AX = mybir.AxisListType

S = 4096
D = 1024
NPC = D // 128
EPS = 1e-5
NEG = -1.0e30


class Sem:
    def __init__(self, h):
        self.h = h
        self.n = 0


import os
STRICT = os.environ.get("BASS_STRICT_SCHED", "") == "1"


class Sch:

    def __init__(self, sems=None):
        self.steps = []
        self.sems = sems

    def op(self, eng, fn, waits=(), inc=None, noflush=False):
        if STRICT and self.sems is not None:
            w = [(s.h, s.n) for s in self.sems if s.n > 0]
        else:
            w = [(s.h, v) for (s, v) in waits if v > 0]
        incs = []
        if inc is not None:
            if isinstance(inc, list):
                for sem, k in inc:
                    sem.n += k
                    incs.append((sem.h, k))
            else:
                sem, k = inc
                sem.n += k
                incs.append((sem.h, k))
        self.steps.append((eng, fn, w, incs, noflush))

    def emit(self, eng_name, eng, flush=None):
        last = {}
        for (e, fn, waits, incs, nofl) in self.steps:
            if e != eng_name:
                continue
            for (h, v) in waits:
                if last.get(h, -1) < v:
                    eng.wait_ge(h, v)
                    last[h] = v
            inst = fn(eng)
            if incs and flush is not None and not nofl:
                inst = flush(eng)
            for (h, k) in incs:
                assert inst is not None
                inst = inst.then_inc(h, k)


def at(sem):
    return (sem, sem.n)


def lag(sem, k):
    return (sem, max(0, sem.n - k))


def recip(eng, out_ap, in_ap, bias=None):
    nc = eng.bass
    ins_ = [eng.lower_ap(in_ap.opt()),
            eng.lower_ap(bias.opt()) if bias is not None
            else mybir.ImmediateValue(dtype=F32, value=0.0),
            mybir.ImmediateValue(dtype=F32, value=1.0),
            mybir.ImmediateValue(dtype=F32, value=0.0)]
    outs_ = [eng.lower_ap(out_ap.opt())]
    inst = mybir.InstActivation(name=nc.get_next_instruction_name(),
                                ins=ins_, outs=outs_, func=AF.Reciprocal)
    return eng.add_instruction(inst)


def hbm_fm(ap, tsl=None):
    if tsl is not None:
        ap = ap[:, tsl]
    return ap.rearrange("(c p) t -> p c t", p=128)


class SM:

    def __init__(self, nc, stack, names):
        for n in names:
            h = stack.enter_context(nc.semaphore("s_" + n))
            setattr(self, n, Sem(h))


SEM_NAMES = ("mm", "pv", "smk", "dve", "evv", "act", "evs", "stats", "xh",
             "xcfree", "xbfree", "xbrv", "xbrs",
             "c_const", "c_wa", "c_wb", "c_xc", "c_ko", "c_vo0", "c_vo1",
             "c_vb0", "c_vb1", "c_vb2", "c_vb3", "c_kb0", "c_kb1",
             "c_x10", "c_x11", "c_trA", "c_trC")


def emit_ln_stats(sch, sm, xsrc_h, xc, xb, xsq, PS0, PS1, tmpA, tmpB, tmpC,
                  a_row, b_row, ones_bf, eps_col):
    for tc in range(8):
        tsl = slice(tc * 512, (tc + 1) * 512)
        sch.op("sync", lambda e, t=tsl: e.dma_start(
            out=xc[:], in_=hbm_fm(xsrc_h, t)), waits=[at(sm.xcfree)],
            inc=(sm.c_xc, 16))
        sch.op("vector", lambda e: e.tensor_copy(xb[:], xc[:]),
               waits=[at(sm.c_xc), at(sm.xbfree)], inc=(sm.xbrv, 1))
        sch.op("scalar", lambda e: e.activation(xsq[:], xc[:], AF.Square),
               waits=[at(sm.c_xc), at(sm.xbfree)], inc=(sm.xbrs, 1))
        for pc in range(NPC):
            sch.op("tensor", lambda e, p=pc: e.matmul(
                PS0[:], ones_bf[:], xb[:, p, :],
                start=(p == 0), stop=(p == NPC - 1)),
                waits=[at(sm.xbrv), at(sm.xbrs), at(sm.evv)] if pc == 0 else [])
        for pc in range(NPC):
            sch.op("tensor", lambda e, p=pc: e.matmul(
                PS1[:], ones_bf[:], xsq[:, p, :],
                start=(p == 0), stop=(p == NPC - 1)),
                inc=(sm.mm, 1) if pc == NPC - 1 else None)
        sch.op("vector", lambda e: e.tensor_scalar(
            tmpA[:], PS0[:], 1.0 / D, None, ALU.mult), waits=[at(sm.mm)])
        sch.op("vector", lambda e: e.tensor_scalar(
            tmpB[:], PS1[:], 1.0 / D, None, ALU.mult), inc=(sm.evv, 1))
        sch.op("vector", lambda e: e.tensor_mul(tmpC[:], tmpA[:], tmpA[:]))
        sch.op("vector", lambda e: e.tensor_sub(tmpB[:], tmpB[:], tmpC[:]),
               inc=(sm.dve, 1))
        sch.op("scalar", lambda e: e.activation(
            tmpC[:], tmpB[:], AF.Ln, bias=eps_col[:]), waits=[at(sm.dve)])
        sch.op("scalar", lambda e: e.activation(
            tmpC[:], tmpC[:], AF.Exp, scale=-0.5), inc=(sm.act, 1))
        sch.op("vector", lambda e: e.scalar_tensor_tensor(
            tmpA[:], tmpA[:], -1.0, tmpC[:], ALU.mult, ALU.mult),
            waits=[at(sm.act)])
        sch.op("vector", lambda e, t=tsl: e.tensor_copy(
            a_row[0:1, t], tmpC[0:1, :]))
        sch.op("vector", lambda e, t=tsl: e.tensor_copy(
            b_row[0:1, t], tmpA[0:1, :]), inc=(sm.stats, 1))
        sch.op("vector", lambda e: e.engine_nop(), waits=[at(sm.mm)],
               inc=(sm.xcfree, 1))
        sch.op("vector", lambda e: e.engine_nop(), inc=(sm.xbfree, 1))


def emit_xhat_chunk(sch, sm, xsrc_h, tsl, xc, xh, PB0, PB1, abc,
                    a_row, b_row, ones1, t1):
    sch.op("sync", lambda e, t=tsl: e.dma_start(
        out=xc[:], in_=hbm_fm(xsrc_h, t)), waits=[at(sm.xcfree)],
        inc=(sm.c_xc, 16))
    sch.op("tensor", lambda e, t=tsl: e.matmul(
        PB0[:], ones1[:], a_row[0:1, t], start=True, stop=True),
        waits=[at(sm.stats), at(sm.evv), at(sm.evs)])
    sch.op("tensor", lambda e, t=tsl: e.matmul(
        PB1[:], ones1[:], b_row[0:1, t], start=True, stop=True),
        inc=(sm.mm, 1))
    sch.op("vector", lambda e: e.tensor_copy(abc[:, :, 0], PB0[:]),
           waits=[at(sm.mm)])
    sch.op("vector", lambda e: e.tensor_copy(abc[:, :, 1], PB1[:]),
           inc=(sm.evv, 1))
    for pc in range(NPC):
        w = [at(sm.c_xc), at(sm.stats), at(sm.mm)] if pc == 0 else []
        sch.op("vector", lambda e, p=pc: e.tensor_mul(
            t1[:], xc[:, p, :], abc[:, :, 0]), waits=w)
        sch.op("vector", lambda e, p=pc: e.tensor_add(
            xh[:, p, :], t1[:], abc[:, :, 1]),
            inc=(sm.xh, 1) if pc == NPC - 1 else None)


LN_C = float(np.log(np.sqrt(1024.0) / np.log(1025.0)))
SEM_NAMES = SEM_NAMES + ("c_d0", "c_d1", "c_d2", "c_d3", "c_d4", "c_d5", "c_d6",
                         "w1a", "w1b", "w2a", "w2b")


def emit_ln_xhat(sch, sm, xsrc_h, tsl, xc, xb, xsq, xh, PS0, PS1, scr,
                 ones_bf, eps_col, t1):
    sA, sB, sC = scr[:, :, 0], scr[:, :, 1], scr[:, :, 2]
    sch.op("sync", lambda e, t=tsl: e.dma_start(
        out=xc[:], in_=hbm_fm(xsrc_h, t)), waits=[at(sm.xcfree)],
        inc=(sm.c_xc, 16))
    sch.op("vector", lambda e: e.tensor_copy(xb[:], xc[:]),
           waits=[at(sm.c_xc), at(sm.mm)], inc=(sm.xbrv, 1))
    sch.op("scalar", lambda e: e.activation(xsq[:], xc[:], AF.Square),
           waits=[at(sm.c_xc), at(sm.mm)], inc=(sm.xbrs, 1))
    for pc in range(NPC):
        sch.op("tensor", lambda e, p=pc: e.matmul(
            PS0[:], ones_bf[:], xb[:, p, :],
            start=(p == 0), stop=(p == NPC - 1)),
            waits=[at(sm.xbrv), at(sm.xbrs), at(sm.evv), at(sm.evs)]
            if pc == 0 else [])
    for pc in range(NPC):
        sch.op("tensor", lambda e, p=pc: e.matmul(
            PS1[:], ones_bf[:], xsq[:, p, :],
            start=(p == 0), stop=(p == NPC - 1)),
            inc=(sm.mm, 1) if pc == NPC - 1 else None)
    sch.op("vector", lambda e: e.tensor_scalar(
        sA[:], PS0[:], 1.0 / D, None, ALU.mult), waits=[at(sm.mm)])
    sch.op("vector", lambda e: e.tensor_scalar(
        sB[:], PS1[:], 1.0 / D, None, ALU.mult), inc=(sm.evv, 1))
    sch.op("vector", lambda e: e.tensor_mul(sC[:], sA[:], sA[:]))
    sch.op("vector", lambda e: e.tensor_sub(sB[:], sB[:], sC[:]),
           inc=(sm.dve, 1))
    sch.op("scalar", lambda e: e.activation(
        sC[:], sB[:], AF.Ln, bias=eps_col[:]), waits=[at(sm.dve)])
    sch.op("scalar", lambda e: e.activation(
        sC[:], sC[:], AF.Exp, scale=-0.5), inc=(sm.act, 1))
    sch.op("vector", lambda e: e.scalar_tensor_tensor(
        sA[:], sA[:], -1.0, sC[:], ALU.mult, ALU.mult), waits=[at(sm.act)])
    for pc in range(NPC):
        sch.op("vector", lambda e, p=pc: e.tensor_mul(
            t1[:], xc[:, p, :], sC[:]))
        sch.op("vector", lambda e, p=pc: e.tensor_add(
            xh[:, p, :], t1[:], sA[:]),
            inc=(sm.xh, 1) if pc == NPC - 1 else None)




def emit_l0a(nc, sch, sm, ctx, io):
    xT = io['l0a_xT']
    wqT = io['l0a_wqT']
    wkT = io['l0a_wkT']
    wvT = io['l0a_wvT']
    woT = io['l0a_woT']
    masks = io['l0a_masks']
    ones_h = io['l0a_ones_h']
    x1T = io['l0a_x1T']
    v_scr = io['l0a_v_scr']
    k_scr = io['l0a_k_scr']
    E = ctx.enter_context
    kbuf = E(nc.sbuf_tensor("l0a_kbuf", [128, 2, NPC, 512], BF16))
    kout = E(nc.sbuf_tensor("l0a_kout", [128, NPC, 512], BF16))
    wa = E(nc.sbuf_tensor("l0a_wa", [128, NPC, 1024], BF16))
    wb = E(nc.sbuf_tensor("l0a_wb", [128, NPC, 1024], BF16))
    xc = E(nc.sbuf_tensor("l0a_xc", [128, NPC, 512], F32))
    xb = E(nc.sbuf_tensor("l0a_xb", [128, NPC, 512], BF16))
    xh = E(nc.sbuf_tensor("l0a_xh", [128, NPC, 512], BF16))
    xsq = xh
    qtb = xb
    sc = E(nc.sbuf_tensor("l0a_sc", [128, S], F32))
    att = E(nc.sbuf_tensor("l0a_att", [128, S], BF16))
    attT = E(nc.sbuf_tensor("l0a_attT", [128, 32, 128], BF16,
                            is_transpose_dest=True))
    vbuf = E(nc.sbuf_tensor("l0a_vbuf", [128, 4, D], BF16))
    vout = E(nc.sbuf_tensor("l0a_vout", [128, 2, D], BF16))
    ctxb = E(nc.sbuf_tensor("l0a_ctxb", [128, D], BF16))
    ctxT = E(nc.sbuf_tensor("l0a_ctxT", [128, NPC, 512], BF16,
                            is_transpose_dest=True))
    x1c = E(nc.sbuf_tensor("l0a_x1c", [128, 2, 512], F32))
    tmpA = E(nc.sbuf_tensor("l0a_tmpA", [128, 512], F32))
    tmpB = E(nc.sbuf_tensor("l0a_tmpB", [128, 512], F32))
    tmpC = E(nc.sbuf_tensor("l0a_tmpC", [128, 512], F32))
    abc = E(nc.sbuf_tensor("l0a_abc", [128, 512, 2], F32))
    t1 = E(nc.sbuf_tensor("l0a_t1", [128, 512], F32))
    a_row = E(nc.sbuf_tensor("l0a_a_row", [1, S], F32))
    b_row = E(nc.sbuf_tensor("l0a_b_row", [1, S], F32))
    mrow = E(nc.sbuf_tensor("l0a_mrow", [128, 1], F32))
    srow = E(nc.sbuf_tensor("l0a_srow", [128, 1], F32))
    rrow = E(nc.sbuf_tensor("l0a_rrow", [128, 1], F32))
    maskt = E(nc.sbuf_tensor("l0a_maskt", [128, 4, 512], BF16))
    ones_bf = E(nc.sbuf_tensor("l0a_ones_bf", [128, 128], BF16))
    ones1 = E(nc.sbuf_tensor("l0a_ones1", [1, 128], F32))
    eps_col = E(nc.sbuf_tensor("l0a_eps_col", [128, 1], F32))
    flush_sc = E(nc.sbuf_tensor("l0a_flush_sc", [1, 8], F32))
    P0 = E(nc.psum_tensor("l0a_P0", [128, 512], F32))
    P1 = E(nc.psum_tensor("l0a_P1", [128, 512], F32))
    P2 = E(nc.psum_tensor("l0a_P2", [128, 512], F32))
    P3 = E(nc.psum_tensor("l0a_P3", [128, 512], F32))
    P4 = E(nc.psum_tensor("l0a_P4", [128, 512], F32))
    P5 = E(nc.psum_tensor("l0a_P5", [128, 512], F32))

    VB = [sm.c_vb0, sm.c_vb1, sm.c_vb2, sm.c_vb3]
    KB = [sm.c_kb0, sm.c_kb1]
    VO = [sm.c_vo0, sm.c_vo1]
    X1 = [sm.c_x10, sm.c_x11]

    sch.op("gpsimd", lambda e: e.dma_start(out=ones_bf[:], in_=ones_h),
           inc=(sm.c_const, 16))
    sch.op("gpsimd", lambda e: e.dma_start(out=ones1[:], in_=ones_h[0:1, :]),
           inc=(sm.c_const, 16))
    sch.op("gpsimd", lambda e: e.dma_start(
        out=maskt[:], in_=masks.rearrange("m p t -> p m t")),
        inc=(sm.c_const, 16))
    sch.op("vector", lambda e: e.memset(eps_col[:], EPS), inc=(sm.dve, 1))
    consts = at(sm.c_const)

    sch.op("tensor", lambda e: e.nop(), waits=[consts])
    sch.op("scalar", lambda e: e.nop(), waits=[consts, at(sm.dve)])
    sch.op("vector", lambda e: e.engine_nop(), waits=[consts])
    emit_ln_stats(sch, sm, xT, xc, xb, xsq, P0, P1, tmpA, tmpB, tmpC,
                  a_row, b_row, ones_bf, eps_col)

    sch.op("gpsimd", lambda e: e.dma_start(
        out=wa[:], in_=wkT.rearrange("(c p) n -> p c n", p=128)),
        inc=(sm.c_wa, 16))
    sch.op("gpsimd", lambda e: e.dma_start(
        out=wb[:], in_=wvT.rearrange("(c p) n -> p c n", p=128)),
        inc=(sm.c_wb, 16))

    for tc in range(8):
        tsl = slice(tc * 512, (tc + 1) * 512)
        emit_xhat_chunk(sch, sm, xT, tsl, xc, xh, P2, P3, abc,
                        a_row, b_row, ones1, t1)
        for pc in range(NPC):
            for ic in range(NPC):
                w = [at(sm.xh), at(sm.c_wa), at(sm.evv), at(sm.evs)] \
                    if ic == 0 else []
                sch.op("tensor", lambda e, i=ic, p=pc: e.matmul(
                    P0[:], wa[:, i, p * 128:(p + 1) * 128], xh[:, i, :],
                    start=(i == 0), stop=(i == NPC - 1)), waits=w,
                    inc=(sm.mm, 1) if ic == NPC - 1 else None)
            sch.op("scalar", lambda e, p=pc: e.copy(
                kout[:, p, :], P0[:]),
                waits=[at(sm.mm), at(sm.c_ko)] if pc == 0 else [at(sm.mm)],
                inc=(sm.evs, 1))
        sch.op("gpsimd", lambda e, t=tsl: e.dma_start(
            out=hbm_fm(k_scr, t), in_=kout[:]),
            waits=[at(sm.evs)], inc=(sm.c_ko, 16))
        for tt in range(4):
            vsl = tc * 4 + tt
            for half in range(2):
                for ic in range(NPC):
                    w = [at(sm.c_wb), at(sm.evv), at(sm.evs)] if ic == 0 else []
                    sch.op("tensor", lambda e, i=ic, h=half, t2=tt: e.matmul(
                        P1[:], xh[:, i, t2 * 128:(t2 + 1) * 128],
                        wb[:, i, h * 512:(h + 1) * 512],
                        start=(i == 0), stop=(i == NPC - 1)), waits=w,
                        inc=(sm.mm, 1) if ic == NPC - 1 else None)
                sch.op("scalar", lambda e, h=half, b2=vsl % 2: e.copy(
                    vout[:, b2, h * 512:(h + 1) * 512], P1[:]),
                    waits=[at(sm.mm), at(VO[vsl % 2])], inc=(sm.evs, 1))
            sch.op("gpsimd", lambda e, v=vsl, b2=vsl % 2: e.dma_start(
                out=v_scr[v * 128:(v + 1) * 128, :], in_=vout[:, b2, :]),
                waits=[at(sm.evs)], inc=(VO[vsl % 2], 16))
        sch.op("vector", lambda e: e.engine_nop(), waits=[at(sm.mm)],
               inc=(sm.xcfree, 1))
    scr_ready = [at(sm.c_ko), at(VO[0]), at(VO[1])]

    sch.op("gpsimd", lambda e: e.dma_start(
        out=wa[:], in_=wqT.rearrange("(c p) n -> p c n", p=128)),
        waits=[at(sm.mm), (sm.c_wa, 16)], inc=(sm.c_wa, 16))
    sch.op("gpsimd", lambda e: e.dma_start(
        out=wb[:], in_=woT.rearrange("(c p) n -> p c n", p=128)),
        waits=[at(sm.mm), (sm.c_wb, 16)], inc=(sm.c_wb, 16))
    wq_pt = at(sm.c_wa)
    wo_pt = at(sm.c_wb)

    for g in range(8):
        gsl = slice(g * 512, (g + 1) * 512)
        emit_xhat_chunk(sch, sm, xT, gsl, xc, xh, P2, P3, abc,
                        a_row, b_row, ones1, t1)
        for pc in range(NPC):
            for ic in range(NPC):
                w = [at(sm.xh), wq_pt, at(sm.evv), at(sm.evs)] \
                    if ic == 0 else []
                sch.op("tensor", lambda e, i=ic, p=pc: e.matmul(
                    P0[:], wa[:, i, p * 128:(p + 1) * 128], xh[:, i, :],
                    start=(i == 0), stop=(i == NPC - 1)), waits=w,
                    inc=(sm.mm, 1) if ic == NPC - 1 else None)
            sch.op("scalar", lambda e, p=pc: e.copy(
                qtb[:, p, :], P0[:]), waits=[at(sm.mm)], inc=(sm.evs, 1))
        for qi in range(4):
            qt = g * 4 + qi
            nkc = qt // 4 + 1
            nkt = nkc * 4
            L = nkc * 512
            for kc in range(nkc):
                PS = P0 if kc % 2 == 0 else P1
                sch.op("sync", lambda e, k=kc: e.dma_start(
                    out=kbuf[:, k % 2, :, :],
                    in_=hbm_fm(k_scr, slice(k * 512, (k + 1) * 512))),
                    waits=scr_ready + [lag(sm.evs, 1)],
                    inc=(KB[kc % 2], 16))
                for ic in range(NPC):
                    w = [at(sm.evv), at(sm.evs), at(KB[kc % 2])] \
                        if ic == 0 else []
                    sch.op("tensor", lambda e, i=ic, k=kc, q=qi, p=PS: e.matmul(
                        p[:], qtb[:, i, q * 128:(q + 1) * 128],
                        kbuf[:, k % 2, i, :],
                        start=(i == 0), stop=(i == NPC - 1)), waits=w,
                        inc=(sm.mm, 1) if ic == NPC - 1 else None)
                sch.op("scalar", lambda e, k=kc, p=PS: e.activation(
                    sc[:, k * 512:(k + 1) * 512], p[:], AF.Copy,
                    scale=1.0 / 32.0),
                    waits=[at(sm.mm), at(sm.dve)] if kc == 0 else [at(sm.mm)],
                    inc=(sm.evs, 1))
            sch.op("vector", lambda e, q=qt, l=L: e.tensor_add(
                sc[:, l - 512:l], sc[:, l - 512:l], maskt[:, q % 4, :]),
                waits=[at(sm.evs)])
            sch.op("vector", lambda e, l=L: e.reduce_max(
                mrow[:], sc[:, 0:l], axis=AX.X, negate=True),
                inc=(sm.dve, 1))
            sch.op("scalar", lambda e, l=L: e.activation(
                att[:, 0:l], sc[:, 0:l], AF.Exp, bias=mrow[:],
                accum_out=srow[:]), waits=[at(sm.dve), at(sm.c_trA)],
                inc=(sm.act, 1))
            sch.op("vector", lambda e: e.reciprocal(rrow[:], srow[:]),
                   waits=[at(sm.act)], inc=(sm.dve, 1))
            sch.op("sync", lambda e, n=nkt: e.dma_start(
                out=attT[:, 0:n, :], in_=att[:, 0:n * 128], transpose=True),
                waits=[at(sm.act), at(sm.pv), at(sm.mm)], inc=(sm.c_trA, 16))
            for kt in range(nkt):
                sch.op("sync", lambda e, k=kt: e.dma_start(
                    out=vbuf[:, k % 4, :], in_=v_scr[k * 128:(k + 1) * 128, :]),
                    waits=scr_ready + [lag(sm.pv, 3), lag(sm.mm, 2)],
                    inc=(VB[kt % 4], 16))
                for half in range(2):
                    PC = P4 if half == 0 else P5
                    w = [at(VB[kt % 4]), at(sm.c_trA)]
                    if kt == 0:
                        w += [at(sm.evv), at(sm.evs)]
                    sch.op("tensor", lambda e, k=kt, h=half, p=PC, n=nkt: e.matmul(
                        p[:], attT[:, k, :],
                        vbuf[:, k % 4, h * 512:(h + 1) * 512],
                        start=(k == 0), stop=(k == n - 1)),
                        waits=w if half == 0 else [],
                        inc=((sm.mm, 1) if kt == nkt - 1
                             else (sm.pv, 1)) if half == 1 else None)
            for half in range(2):
                PC = P4 if half == 0 else P5
                sch.op("scalar", lambda e, h=half, p=PC: e.activation(
                    ctxb[:, h * 512:(h + 1) * 512], p[:], AF.Copy,
                    scale=rrow[:]), waits=[at(sm.mm), at(sm.dve)],
                    inc=(sm.evs, 1) if half == 1 else None)
            sch.op("sync", lambda e, q=qi: e.dma_start(
                out=ctxT[:, :, q * 128:(q + 1) * 128], in_=ctxb[:],
                transpose=True), waits=[at(sm.evs), at(sm.mm)],
                inc=(sm.c_trC, 16))
        for pc in range(NPC):
            PS = P2 if pc % 2 == 0 else P3
            for ic in range(NPC):
                w = [at(sm.c_trC), wo_pt, at(sm.evv), at(sm.evs)] \
                    if ic == 0 else []
                sch.op("tensor", lambda e, i=ic, p=pc, ps=PS: e.matmul(
                    ps[:], wb[:, i, p * 128:(p + 1) * 128], ctxT[:, i, :],
                    start=(i == 0), stop=(i == NPC - 1)), waits=w,
                    inc=(sm.mm, 1) if ic == NPC - 1 else None)
            sch.op("vector", lambda e, p=pc, ps=PS: e.tensor_add(
                x1c[:, p % 2, :], ps[:], xc[:, p, :]),
                waits=[at(sm.mm), at(X1[pc % 2])], inc=(sm.evv, 1))
            sch.op("gpsimd", lambda e, p=pc, g2=g: e.dma_start(
                out=hbm_fm(x1T, slice(g2 * 512, (g2 + 1) * 512))[:, p, :],
                in_=x1c[:, p % 2, :]), waits=[at(sm.evv)],
                inc=(X1[pc % 2], 16))
        sch.op("vector", lambda e: e.engine_nop(), waits=[at(sm.mm)],
               inc=(sm.xcfree, 1))

    sch.op("gpsimd", lambda e: e.nop(), waits=[at(X1[0]), at(X1[1])])


def emit_l0f(nc, sch, sm, ctx, io):
    xT = io['l0f_xT']
    wgT = io['l0f_wgT']
    wuT = io['l0f_wuT']
    wdT = io['l0f_wdT']
    ones_h = io['l0f_ones_h']
    yT = io['l0f_yT']
    NF = 24
    E = ctx.enter_context
    wgb = E(nc.sbuf_tensor("l0f_wgb", [128, NPC, 3072], BF16))
    wub = E(nc.sbuf_tensor("l0f_wub", [128, NPC, 3072], BF16))
    wdb = E(nc.sbuf_tensor("l0f_wdb", [128, NF, D], BF16))
    hbuf = E(nc.sbuf_tensor("l0f_hbuf", [128, NF, 512], BF16))
    xc = E(nc.sbuf_tensor("l0f_xc", [128, NPC, 512], F32))
    xh = E(nc.sbuf_tensor("l0f_xh", [128, NPC, 512], BF16))
    scr = E(nc.sbuf_tensor("l0f_scr", [128, 512, 3], F32))
    t1 = E(nc.sbuf_tensor("l0f_t1", [128, 512], F32))
    sil = E(nc.sbuf_tensor("l0f_sil", [128, 512], BF16))
    yc = E(nc.sbuf_tensor("l0f_yc", [128, 2, 512], F32))
    ones_bf = E(nc.sbuf_tensor("l0f_ones_bf", [128, 128], BF16))
    eps_col = E(nc.sbuf_tensor("l0f_eps_col", [128, 1], F32))
    flush_sc = E(nc.sbuf_tensor("l0f_flush_sc", [1, 8], F32))
    P0 = E(nc.psum_tensor("l0f_P0", [128, 512], F32))
    P1 = E(nc.psum_tensor("l0f_P1", [128, 512], F32))
    P2 = E(nc.psum_tensor("l0f_P2", [128, 512], F32))
    P3 = E(nc.psum_tensor("l0f_P3", [128, 512], F32))
    X1 = [sm.c_x10, sm.c_x11]

    sch.op("gpsimd", lambda e: e.dma_start(out=ones_bf[:], in_=ones_h),
           inc=(sm.c_const, 16))
    sch.op("gpsimd", lambda e: e.dma_start(
        out=wgb[:], in_=wgT.rearrange("(c p) n -> p c n", p=128)),
        inc=(sm.c_wa, 16))
    sch.op("gpsimd", lambda e: e.dma_start(
        out=wub[:], in_=wuT.rearrange("(c p) n -> p c n", p=128)),
        inc=(sm.c_wa, 16))
    sch.op("gpsimd", lambda e: e.dma_start(
        out=wdb[:], in_=wdT.rearrange("(c p) n -> p c n", p=128)),
        inc=(sm.c_wa, 16))
    sch.op("vector", lambda e: e.memset(eps_col[:], EPS), inc=(sm.dve, 1))
    sch.op("tensor", lambda e: e.nop(), waits=[at(sm.c_const), at(sm.c_wa)])
    sch.op("scalar", lambda e: e.nop(), waits=[at(sm.dve)])

    for tc in range(8):
        tsl = slice(tc * 512, (tc + 1) * 512)
        emit_ln_xhat(sch, sm, xT, tsl, xc,
                     hbuf.ap()[:, 0:NPC, :], xh, xh, P0, P1, scr,
                     ones_bf, eps_col, t1)
        for ft in range(NF):
            PA = P0 if ft % 2 == 0 else P2
            PB = P1 if ft % 2 == 0 else P3
            for ic in range(NPC):
                w = [at(sm.xh), at(sm.evv), at(sm.evs)] if ic == 0 else []
                sch.op("tensor", lambda e, i=ic, f=ft, p=PA: e.matmul(
                    p[:], wgb[:, i, f * 128:(f + 1) * 128], xh[:, i, :],
                    start=(i == 0), stop=(i == NPC - 1)), waits=w,
                    inc=(sm.mm, 1) if ic == NPC - 1 else None)
            for ic in range(NPC):
                sch.op("tensor", lambda e, i=ic, f=ft, p=PB: e.matmul(
                    p[:], wub[:, i, f * 128:(f + 1) * 128], xh[:, i, :],
                    start=(i == 0), stop=(i == NPC - 1)),
                    inc=(sm.mm, 1) if ic == NPC - 1 else None)
            sch.op("scalar", lambda e, p=PA: e.activation(
                sil[:], p[:], AF.Silu, scale=1.0 / 127.0),
                waits=[at(sm.mm), at(sm.evv)], inc=(sm.evs, 1))
            sch.op("vector", lambda e, f=ft, p=PB: e.tensor_mul(
                hbuf[:, f, :], sil[:], p[:]),
                waits=[at(sm.evs)], inc=(sm.evv, 1))
        for rnd in range(2):
            for ft in range(NF):
                for pq in range(4):
                    pc = rnd * 4 + pq
                    PD = [P0, P1, P2, P3][pq]
                    w = [at(sm.evv), at(sm.evs)] if (ft == 0 and pq == 0) else []
                    sch.op("tensor", lambda e, f=ft, p=pc, ps=PD: e.matmul(
                        ps[:], wdb[:, f, p * 128:(p + 1) * 128],
                        hbuf[:, f, :],
                        start=(f == 0), stop=(f == NF - 1)), waits=w,
                        inc=(sm.mm, 1) if (ft == NF - 1 and pq == 3) else None)
            for pq in range(4):
                pc = rnd * 4 + pq
                PD = [P0, P1, P2, P3][pq]
                sch.op("vector", lambda e, ps=PD: e.tensor_scalar(
                    t1[:], ps[:], 1.0 / 16129.0, None, ALU.mult),
                    waits=[at(sm.mm), at(sm.evs)])
                sch.op("vector", lambda e, p=pc: e.tensor_add(
                    t1[:], t1[:], xc[:, p, :]), inc=(sm.evv, 1))
                sch.op("scalar", lambda e, b2=pc % 2: e.activation(
                    yc[:, b2, :], t1[:], AF.Copy),
                    waits=[at(sm.evv), at(X1[pc % 2])], inc=(sm.evs, 1))
                sch.op("gpsimd", lambda e, p=pc, t=tsl, b2=pc % 2: e.dma_start(
                    out=hbm_fm(yT, t)[:, p, :], in_=yc[:, b2, :]),
                    waits=[at(sm.evs)], inc=(X1[pc % 2], 16))
        sch.op("vector", lambda e: e.engine_nop(), waits=[at(sm.mm)],
               inc=(sm.xcfree, 1))
    sch.op("gpsimd", lambda e: e.nop(), waits=[at(X1[0]), at(X1[1])])


def emit_l1a(nc, sch, sm, ctx, io):
    xT = io['l1a_xT']
    wq = io['l1a_wq']
    wk = io['l1a_wk']
    wv = io['l1a_wv']
    wo = io['l1a_wo']
    bq = io['l1a_bq']
    bk = io['l1a_bk']
    bvr = io['l1a_bvr']
    bo = io['l1a_bo']
    aqkv = io['l1a_aqkv']
    ones_h = io['l1a_ones_h']
    eye_h = io['l1a_eye_h']
    blk_h = io['l1a_blk_h']
    sel_h = io['l1a_sel_h']
    x3T = io['l1a_x3T']
    q_scr = io['l1a_q_scr']
    k_scr = io['l1a_k_scr']
    v_scr = io['l1a_v_scr']
    c_scr = io['l1a_c_scr']
    H, DH = 16, 64
    E = ctx.enter_context
    wsb = [E(nc.sbuf_tensor("wsb%d" % i, [128, NPC, D], BF16))
           for i in range(3)]
    xc = E(nc.sbuf_tensor("l1a_xc", [128, NPC, 512], F32))
    xh = E(nc.sbuf_tensor("l1a_xh", [128, NPC, 512], BF16))
    xq2 = E(nc.sbuf_tensor("l1a_xq2", [128, NPC, 512], BF16))
    scr = E(nc.sbuf_tensor("l1a_scr", [128, 512, 3], F32))
    t1 = E(nc.sbuf_tensor("l1a_t1", [128, 512], F32))
    s1k = E(nc.sbuf_tensor("l1a_s1k", [128, 1024], F32))
    s2k = E(nc.sbuf_tensor("l1a_s2k", [128, 1024], F32))
    yt = E(nc.sbuf_tensor("l1a_yt", [128, 1024], BF16))
    ysq = E(nc.sbuf_tensor("l1a_ysq", [128, 512], BF16))
    wsqsc = E(nc.sbuf_tensor("l1a_wsqsc", [128, 1024], BF16))
    xsqbc = E(nc.sbuf_tensor("l1a_xsqbc", [128, 512], F32))
    xsqT = E(nc.sbuf_tensor("l1a_xsqT", [128, 4], F32))
    wsqe = E(nc.sbuf_tensor("l1a_wsqe", [128, NPC, 2], F32))
    wsqvb = E(nc.sbuf_tensor("l1a_wsqvb", [128, 1024], F32))
    bvb = E(nc.sbuf_tensor("l1a_bvb", [128, 1024], F32))
    bcols = E(nc.sbuf_tensor("l1a_bcols", [128, 2, NPC], F32))
    bscal = E(nc.sbuf_tensor("l1a_bscal", [128, 2, NPC], F32))
    bocol = E(nc.sbuf_tensor("l1a_bocol", [128, NPC, 1], F32))
    scal = E(nc.sbuf_tensor("l1a_scal", [128, 3], F32))
    aqs = E(nc.sbuf_tensor("l1a_aqs", [1, 3], F32))
    qn_sb = E(nc.sbuf_tensor("l1a_qn_sb", [16, S], BF16))
    kn_sb = E(nc.sbuf_tensor("l1a_kn_sb", [16, S], BF16))
    qnT = E(nc.sbuf_tensor("l1a_qnT", [128, 32, 16], F32))
    kbc = E(nc.sbuf_tensor("l1a_kbc", [128, NPC, 512], BF16))
    qh_sb = E(nc.sbuf_tensor("l1a_qh_sb", [64, S], BF16))
    kh_sb = E(nc.sbuf_tensor("l1a_kh_sb", [64, S], BF16))
    vh_sb = E(nc.sbuf_tensor("l1a_vh_sb", [128, 32, DH], BF16))
    attb = E(nc.sbuf_tensor("l1a_attb", [128, S], BF16))
    attTg = E(nc.sbuf_tensor("l1a_attTg", [128, 32, 512], BF16,
                             is_transpose_dest=True))
    ctxc = E(nc.sbuf_tensor("l1a_ctxc", [64, 512], BF16))
    mrow2 = E(nc.sbuf_tensor("l1a_mrow2", [128, 2], F32))
    srow2 = E(nc.sbuf_tensor("l1a_srow2", [128, 2], F32))
    rql = E(nc.sbuf_tensor("l1a_rql", [128, 4], F32))
    eyef = E(nc.sbuf_tensor("l1a_eyef", [128, 128], F32))
    yc = s1k.ap().rearrange("p (c t) -> p c t", t=512)
    ones_bf = E(nc.sbuf_tensor("l1a_ones_bf", [128, 128], BF16))
    ones1 = E(nc.sbuf_tensor("l1a_ones1", [1, 128], F32))
    eyeb = E(nc.sbuf_tensor("l1a_eyeb", [128, 128], BF16))
    blkb = E(nc.sbuf_tensor("l1a_blkb", [128, NPC, 16], BF16))
    selb = E(nc.sbuf_tensor("l1a_selb", [16, 2048], BF16))
    eps_col = E(nc.sbuf_tensor("l1a_eps_col", [128, 1], F32))
    flush_sc = E(nc.sbuf_tensor("l1a_flush_sc", [1, 8], F32))
    epz_col = E(nc.sbuf_tensor("l1a_epz_col", [128, 1], F32))
    P0 = E(nc.psum_tensor("l1a_P0", [128, 512], F32))
    P1 = E(nc.psum_tensor("l1a_P1", [128, 512], F32))
    P2 = E(nc.psum_tensor("l1a_P2", [128, 512], F32))
    P3 = E(nc.psum_tensor("l1a_P3", [128, 512], F32))
    P4 = E(nc.psum_tensor("l1a_P4", [128, 512], F32))
    PN = E(nc.psum_tensor("l1a_PN", [16, 512], F32))
    PC6 = E(nc.psum_tensor("l1a_PC6", [64, 512], F32))
    w1fv = wsb[1].ap().rearrange("p c t -> p (c t)")
    w2fv = wsb[2].ap().rearrange("p c t -> p (c t)")
    sqX = [w1fv[:, 0:S], w2fv[:, 0:S]]
    dbX = [w1fv[:, S:2 * S], w2fv[:, S:2 * S]]
    atX = [attb.ap(), xh.ap().rearrange("p c t -> p (c t)")]
    rrowR = s1k.ap()[0:1, 0:512]
    bcb = s1k.ap()[0:64, 512:1024]
    ctxn = s2k.ap()[0:64, 0:512]
    X1 = [sm.c_x10, sm.c_x11]

    for fn in (lambda e: e.dma_start(out=ones_bf[:], in_=ones_h),
               lambda e: e.dma_start(out=ones1[:], in_=ones_h[0:1, :]),
               lambda e: e.dma_start(out=eyeb[:], in_=eye_h),
               lambda e: e.dma_start(out=eyef[:], in_=eye_h),
               lambda e: e.dma_start(
                   out=blkb[:], in_=blk_h.rearrange("c p n -> p c n")),
               lambda e: e.dma_start(out=selb[:], in_=sel_h),
               lambda e: e.dma_start(out=aqs[:], in_=aqkv),
               lambda e: e.dma_start(
                   out=bcols[:, 0, :],
                   in_=bq.rearrange("(c p) o -> p (c o)", p=128),
                   allow_slow_non_contiguous=True),
               lambda e: e.dma_start(
                   out=bcols[:, 1, :],
                   in_=bk.rearrange("(c p) o -> p (c o)", p=128),
                   allow_slow_non_contiguous=True),
               lambda e: e.dma_start(
                   out=bocol.ap().rearrange("p c o -> p (c o)"),
                   in_=bo.rearrange("(c p) o -> p (c o)", p=128),
                   allow_slow_non_contiguous=True),
               ):
        sch.op("gpsimd", fn, inc=(sm.c_const, 16))
    sch.op("sync", lambda e: e.dma_start(out=s2k[0:1, 0:1024], in_=bvr),
           inc=(sm.c_d0, 16))
    for i, w_h in enumerate((wq, wk, wv)):
        sch.op("gpsimd", lambda e, wh=w_h, j=i: e.dma_start(
            out=wsb[j][:], in_=wh.rearrange("(c p) n -> p c n", p=128)),
            inc=(sm.c_wa, 16))
    sch.op("vector", lambda e: e.memset(eps_col[:], EPS))
    sch.op("vector", lambda e: e.memset(epz_col[:], 1e-12), inc=(sm.dve, 1))
    consts = at(sm.c_const)
    sch.op("tensor", lambda e: e.nop(),
           waits=[consts, at(sm.c_wa), at(sm.c_d0)])
    sch.op("scalar", lambda e: e.nop(),
           waits=[consts, at(sm.dve), at(sm.c_wa)])
    sch.op("vector", lambda e: e.engine_nop(), waits=[consts])

    for j in range(3):
        for ic in range(NPC):
            sch.op("scalar", lambda e, i=ic, w=wsb[j]: e.activation(
                wsqsc[:], w[:, i, :], AF.Square),
                waits=[at(sm.mm)], inc=(sm.act, 1))
            for half in range(2):
                PS = P0 if half == 0 else P1
                sch.op("tensor", lambda e, h=half, i=ic, p=PS: e.matmul(
                    p[:], ones_bf[:], wsqsc[:, h * 512:(h + 1) * 512],
                    start=(i == 0), stop=(i == NPC - 1)),
                    waits=([at(sm.act), at(sm.evv), at(sm.evs)]
                           if half == 0 else [at(sm.act)]),
                    inc=(sm.mm, 1) if half == 1 else None)
        sch.op("vector", lambda e: e.tensor_copy(s1k[:, 0:512], P0[:]),
               waits=[at(sm.mm)])
        sch.op("vector", lambda e: e.tensor_copy(s1k[:, 512:1024], P1[:]),
               inc=(sm.evv, 1))
        if j < 2:
            for pc in range(NPC):
                sch.op("tensor", lambda e, p=pc: e.matmul(
                    P2[:, 0:1], s1k[0:1, p * 128:(p + 1) * 128],
                    ones1[0:1, 0:1], start=True, stop=True),
                    waits=[at(sm.evv), at(sm.evs)], inc=(sm.mm, 1))
                sch.op("vector", lambda e, p=pc, j2=j: e.tensor_scalar(
                    wsqe[:, p, j2:j2 + 1], P2[:, 0:1], 1.0, eps_col[:],
                    ALU.mult, ALU.add),
                    waits=[at(sm.mm)], inc=(sm.evs, 1))
        else:
            sch.op("vector", lambda e: e.tensor_scalar(
                wsqvb[:, 0:512], s1k[:, 0:512], 1.0, eps_col[:],
                ALU.mult, ALU.add))
            sch.op("vector", lambda e: e.tensor_scalar(
                wsqvb[:, 512:1024], s1k[:, 512:1024], 1.0, eps_col[:],
                ALU.mult, ALU.add), inc=(sm.evv, 1))
    sch.op("scalar", lambda e: e.activation(
        aqs[:], aqs[:], AF.Exp, scale=LN_C), inc=(sm.act, 1))
    for j in range(3):
        sch.op("tensor", lambda e, j2=j: e.matmul(
            P2[:, 0:1], ones1[:], aqs[0:1, j2:j2 + 1],
            start=True, stop=True),
            waits=[at(sm.act), at(sm.evs), at(sm.evv)], inc=(sm.mm, 1))
        sch.op("vector", lambda e, j2=j: e.tensor_copy(
            scal[:, j2:j2 + 1], P2[:, 0:1]),
            waits=[at(sm.mm)], inc=(sm.evs, 1))
    for j in range(2):
        sch.op("vector", lambda e, j2=j: e.tensor_scalar(
            bscal[:, j2, :], bcols[:, j2, :], scal[:, j2:j2 + 1], None,
            ALU.mult), waits=[at(sm.evs)], inc=(sm.dve, 1))
    for half in range(2):
        PS = P0 if half == 0 else P1
        sch.op("tensor", lambda e, h=half, p=PS: e.matmul(
            p[:], ones1[:], s2k[0:1, h * 512:(h + 1) * 512],
            start=True, stop=True),
            waits=[at(sm.evv), at(sm.evs)],
            inc=(sm.mm, 1) if half == 1 else None)
    sch.op("vector", lambda e: e.tensor_copy(bvb[:, 0:512], P0[:]),
           waits=[at(sm.mm)])
    sch.op("vector", lambda e: e.tensor_copy(bvb[:, 512:1024], P1[:]),
           inc=(sm.evv, 1))

    for tc in range(8):
        tsl = slice(tc * 512, (tc + 1) * 512)
        emit_ln_xhat(sch, sm, xT, tsl, xc, xq2, xh, xh, P0, P1, scr,
                     ones_bf, eps_col, t1)
        sch.op("scalar", lambda e: e.activation(xq2[:], xh[:], AF.Square),
               waits=[at(sm.xh), at(sm.mm)], inc=(sm.act, 1))
        for pc in range(NPC):
            sch.op("tensor", lambda e, p=pc: e.matmul(
                P0[:], ones_bf[:], xq2[:, p, :],
                start=(p == 0), stop=(p == NPC - 1)),
                waits=[at(sm.act), at(sm.evv), at(sm.evs)] if pc == 0 else [],
                inc=(sm.mm, 1) if pc == NPC - 1 else None)
        sch.op("vector", lambda e: e.tensor_copy(xsqbc[:], P0[:]),
               waits=[at(sm.mm)], inc=(sm.evv, 1))
        for tt in range(4):
            sch.op("tensor", lambda e, t2=tt: e.matmul(
                P1[:, 0:1], xsqbc[0:1, t2 * 128:(t2 + 1) * 128],
                ones1[0:1, 0:1], start=True, stop=True),
                waits=[at(sm.evv), at(sm.evs)], inc=(sm.mm, 1))
            sch.op("vector", lambda e, t2=tt: e.tensor_copy(
                xsqT[:, t2:t2 + 1], P1[:, 0:1]),
                waits=[at(sm.mm)], inc=(sm.evs, 1))
        for j in range(2):
            dsth = (q_scr, k_scr)[j]
            nrm = (qn_sb, kn_sb)[j]
            dsem = (sm.c_d1, sm.c_d2)[j]
            for pc in range(NPC):
                PD = P2 if pc % 2 == 0 else P3
                for ic in range(NPC):
                    w = [at(sm.evv), at(sm.evs), at(sm.xh)] if ic == 0 else []
                    sch.op("tensor",
                           lambda e, i=ic, p=pc, j2=j, ps=PD: e.matmul(
                               ps[:], wsb[j2][:, i, p * 128:(p + 1) * 128],
                               xh[:, i, :],
                               start=(i == 0), stop=(i == NPC - 1)),
                           waits=w,
                           inc=(sm.mm, 1) if ic == NPC - 1 else None)
                sch.op("vector", lambda e, p=pc, j2=j, ps=PD: e.tensor_scalar(
                    scr[:, :, 0], ps[:], -2.0, wsqe[:, p, j2:j2 + 1],
                    ALU.mult, ALU.add), waits=[at(sm.mm), at(sm.dve)],
                    inc=(sm.evv, 1))
                sch.op("scalar", lambda e, ps=PD: e.activation(
                    ysq[:], ps[:], AF.Square),
                    waits=[at(sm.mm), at(sm.act), at(sm.evv)],
                    inc=(sm.evs, 1))
                sch.op("vector", lambda e: e.tensor_add(
                    scr[:, :, 0], scr[:, :, 0], xsqbc[:]),
                    inc=(sm.evv, 1))
                sch.op("scalar", lambda e: recip(e, scr[:, :, 1],
                                                 scr[:, :, 0]),
                       waits=[at(sm.evv)], inc=(sm.act, 1))
                sch.op("vector", lambda e: e.tensor_mul(
                    scr[:, :, 2], ysq[:], scr[:, :, 1]),
                    waits=[at(sm.act)], inc=(sm.dve, 1))
                sch.op("scalar", lambda e, p=pc, j2=j: e.activation(
                    yt[:, 0:512], scr[:, :, 2], AF.Identity,
                    scale=scal[:, j2:j2 + 1], bias=bscal[:, j2, p:p + 1]),
                    waits=[at(dsem), at(sm.c_d3), at(sm.dve)],
                    inc=(sm.act, 1))
                sch.op("gpsimd", lambda e, p=pc, t=tsl, d=dsth: e.dma_start(
                    out=d[p * 128:(p + 1) * 128, t], in_=yt[:, 0:512]),
                    waits=[at(sm.dve)], inc=(dsem, 16))
                sch.op("scalar", lambda e: e.activation(
                    ysq[:], yt[:, 0:512], AF.Square),
                    waits=[at(sm.dve)], inc=(sm.act, 1))
                sch.op("tensor", lambda e, p=pc: e.matmul(
                    PN[:], blkb[:, p, :], ysq[:],
                    start=(p == 0), stop=(p == NPC - 1)),
                    waits=([at(sm.act), at(sm.evs)] if pc == 0
                           else [at(sm.act)]),
                    inc=(sm.mm, 1))
            sch.op("vector", lambda e, t=tsl, n=nrm: e.tensor_copy(
                n[0:16, t], PN[:]), waits=[at(sm.mm)], inc=(sm.evs, 1))
        for tt in range(4):
            for half in range(2):
                PD = P2 if half == 0 else P3
                for ic in range(NPC):
                    w = [at(sm.evv), at(sm.evs)] if ic == 0 else []
                    sch.op("tensor", lambda e, i=ic, t2=tt, h=half, ps=PD:
                           e.matmul(ps[:],
                                    xh[:, i, t2 * 128:(t2 + 1) * 128],
                                    wsb[2][:, i, h * 512:(h + 1) * 512],
                                    start=(i == 0), stop=(i == NPC - 1)),
                           waits=w,
                           inc=(sm.mm, 1) if ic == NPC - 1 else None)
            for half in range(2):
                PD = P2 if half == 0 else P3
                hs = slice(half * 512, (half + 1) * 512)
                sch.op("vector", lambda e, t2=tt, ps=PD, h=hs: e.tensor_scalar(
                    s1k[:, h], ps[:], -2.0, xsqT[:, t2:t2 + 1],
                    ALU.mult, ALU.add), waits=[at(sm.mm), at(sm.dve)],
                    inc=(sm.evv, 1))
                sch.op("scalar", lambda e, ps=PD, h=hs: e.activation(
                    yt[:, h], ps[:], AF.Square),
                    waits=[at(sm.mm), at(sm.act), at(sm.evv),
                           at(sm.c_d1), at(sm.c_d2)],
                    inc=(sm.evs, 1))
            sch.op("vector", lambda e: e.tensor_add(
                s1k[:], s1k[:], wsqvb[:]), inc=(sm.evv, 1))
            sch.op("scalar", lambda e: recip(e, s2k[:], s1k[:]),
                   waits=[at(sm.evv)], inc=(sm.act, 1))
            sch.op("vector", lambda e: e.tensor_mul(s1k[:], yt[:, 0:1024],
                                                    s2k[:]),
                   waits=[at(sm.act)])
            sch.op("vector", lambda e: e.tensor_add(
                s1k[:], s1k[:], bvb[:]), inc=(sm.dve, 1))
            sch.op("scalar", lambda e: e.activation(
                yt[:, 0:1024], s1k[:], AF.Identity, scale=scal[:, 2:3]),
                waits=[at(sm.c_d3), at(sm.dve)], inc=(sm.act, 1))
            sch.op("gpsimd", lambda e, t2=tt, c=tc: e.dma_start(
                out=v_scr[(c * 4 + t2) * 128:(c * 4 + t2 + 1) * 128, :],
                in_=yt[:, 0:1024]),
                waits=[at(sm.act)], inc=(sm.c_d3, 16))
    for tt in range(32):
        sch.op("tensor", lambda e, t2=tt: e.matmul(
            P2[:, 0:16], qn_sb[0:16, t2 * 128:(t2 + 1) * 128],
            eyeb[0:16, 0:16], start=True, stop=True),
            waits=[at(sm.evv), at(sm.evs)], inc=(sm.mm, 1))
        sch.op("vector", lambda e, t2=tt: e.tensor_scalar(
            qnT[:, t2, :], P2[:, 0:16], 1.0, eps_col[:],
            ALU.mult, ALU.add), waits=[at(sm.mm)], inc=(sm.evs, 1))
    pP_done = [at(sm.c_d1), at(sm.c_d2), at(sm.c_d3)]

    for h in range(H):
        hs = slice(h * 64, (h + 1) * 64)
        sch.op("sync", lambda e, s_=hs: e.dma_start(
            out=qh_sb[:], in_=q_scr[s_, :]),
            waits=pP_done + [at(sm.mm), at(sm.pv)], inc=(sm.c_d4, 16))
        sch.op("sync", lambda e, s_=hs: e.dma_start(
            out=kh_sb[:], in_=k_scr[s_, :]), inc=(sm.c_d4, 16))
        sch.op("sync", lambda e, s_=hs: e.dma_start(
            out=vh_sb[:], in_=v_scr[:, s_].rearrange(
                "(t p) c -> p t c", p=128)), inc=(sm.c_d4, 16))
        qkv_pt = at(sm.c_d4)
        for kc in range(NPC):
            sch.op("tensor", lambda e, k=kc, h2=h: e.matmul(
                P4[:], selb[:, h2 * 128:(h2 + 1) * 128],
                kn_sb[0:16, k * 512:(k + 1) * 512],
                start=True, stop=True),
                waits=[at(sm.dve), at(sm.evs)], inc=(sm.mm, 1))
            sch.op("scalar", lambda e, k=kc: e.copy(
                kbc[:, k, :], P4[:]), waits=[at(sm.mm), at(sm.dve)],
                inc=(sm.evs, 1), noflush=True)
        kbc_pt = at(sm.evs)

        def tail_pv(g):
            for kt in range(32):
                sch.op("tensor", lambda e, k=kt: e.matmul(
                    PC6[:], vh_sb[:, k, :], attTg[:, k, :],
                    start=(k == 0), stop=(k == 31)),
                    waits=([qkv_pt, at(sm.c_d5), at(sm.evv)] if kt == 0
                           else []),
                    inc=(sm.pv, 1) if kt == 31 else None)
            for qi in range(4):
                sch.op("tensor", lambda e, q=qi: e.matmul(
                    P2[0:1, q * 128:(q + 1) * 128], rql[:, q:q + 1],
                    eyef[:], start=True, stop=True),
                    waits=[at(sm.evv)], inc=(sm.mm, 1))

        def tail_bc(g):
            pin_pv = at(sm.pv)
            sch.op("vector", lambda e: e.tensor_copy(
                rrowR[0:1, :], P2[0:1, 0:512]),
                waits=[at(sm.mm)], inc=(sm.evv, 1))
            sch.op("tensor", lambda e: e.matmul(
                P3[0:64, :], ones1[0:1, 0:64], rrowR[0:1, :],
                start=True, stop=True),
                waits=[at(sm.evv)], inc=(sm.mm, 1))
            sch.op("vector", lambda e: e.tensor_copy(
                bcb[:], P3[0:64, :]), waits=[at(sm.mm)],
                inc=(sm.evv, 1))
            sch.op("vector", lambda e: e.tensor_mul(
                ctxn[:], PC6[:], bcb[:]),
                waits=[pin_pv, at(sm.evs)], inc=(sm.evv, 1))
            sch.op("scalar", lambda e: e.activation(
                ctxc[:], ctxn[:], AF.Copy),
                waits=[at(sm.evv), at(sm.c_d6)], inc=(sm.evs, 1))
            sch.op("gpsimd", lambda e, s_=hs, g2=g: e.dma_start(
                out=c_scr[s_, g2 * 512:(g2 + 1) * 512], in_=ctxc[:]),
                waits=[at(sm.evs)], inc=(sm.c_d6, 16))

        for pr in range(16):
            qts = (2 * pr, 2 * pr + 1)
            pin_dx = [None, None]
            for x in (0, 1):
                qt = qts[x]
                for kc in range(NPC):
                    PS = P0 if kc % 2 == 0 else P1
                    sch.op("tensor", lambda e, k=kc, q=qt, ps=PS: e.matmul(
                        ps[:], qh_sb[:, q * 128:(q + 1) * 128],
                        kh_sb[:, k * 512:(k + 1) * 512],
                        start=True, stop=True),
                        waits=[qkv_pt, lag(sm.smk, 1)], inc=(sm.mm, 1))
                    w = [at(sm.mm)]
                    if kc == 0:
                        w.append(lag(sm.dve, 1))
                    sch.op("scalar", lambda e, k=kc, ps=PS, x2=x:
                           e.activation(sqX[x2][:, k * 512:(k + 1) * 512],
                                        ps[:], AF.Square),
                           waits=w, inc=(sm.smk, 1), noflush=True)
                sch.op("vector", lambda e, q=qt, x2=x: e.tensor_scalar(
                    dbX[x2][:], sqX[x2][:], -2.0, qnT[:, q, h:h + 1],
                    ALU.mult, ALU.add),
                    waits=[at(sm.smk), at(sm.act), kbc_pt])
                sch.op("vector", lambda e, x2=x: e.tensor_add(
                    dbX[x2][:], dbX[x2][:],
                    kbc.ap().rearrange("p c t -> p (c t)")),
                    inc=(sm.dve, 1))
                pin_dx[x] = at(sm.dve)
            if pr >= 2 and pr % 2 == 0:
                tail_pv(pr // 2 - 1)
            pin_max = [None, None]
            for x in (0, 1):
                sch.op("scalar", lambda e, x2=x: recip(
                    e, atX[x2][:], dbX[x2][:], bias=epz_col[:]),
                    waits=[pin_dx[x], at(sm.c_d5)], inc=(sm.act, 1),
                    noflush=True)
                pr_rc = at(sm.act)
                sch.op("vector", lambda e, x2=x: e.tensor_mul(
                    dbX[x2][:], sqX[x2][:], atX[x2][:]), waits=[pr_rc])
                sch.op("vector", lambda e, x2=x: e.reduce_max(
                    mrow2[:, x2:x2 + 1], dbX[x2][:], axis=AX.X, negate=True),
                    inc=(sm.dve, 1))
                pin_max[x] = at(sm.dve)
            for x in (0, 1):
                qi = qts[x] % 4
                sch.op("scalar", lambda e, x2=x: e.activation(
                    atX[x2][:], dbX[x2][:], AF.Exp, bias=mrow2[:, x2:x2 + 1],
                    accum_out=srow2[:, x2:x2 + 1]),
                    waits=[pin_max[x], at(sm.evv)], inc=(sm.act, 1))
                pin_exp = at(sm.act)
                sch.op("sync", lambda e, q=qi, x2=x: e.dma_start(
                    out=attTg[:, :, q * 128:(q + 1) * 128], in_=atX[x2][:],
                    transpose=True),
                    waits=[pin_exp, at(sm.pv)], inc=(sm.c_d5, 16))
                sch.op("vector", lambda e, q=qi, x2=x: e.reciprocal(
                    rql[:, q:q + 1], srow2[:, x2:x2 + 1]),
                    waits=[pin_exp, at(sm.mm)], inc=(sm.evv, 1))
                if x == 0 and pr >= 2 and pr % 2 == 0:
                    tail_bc(pr // 2 - 1)
        tail_pv(7)
        tail_bc(7)

    sch.op("gpsimd", lambda e: e.dma_start(
        out=wsb[0][:], in_=wo.rearrange("(c p) n -> p c n", p=128)),
        waits=[at(sm.mm)], inc=(sm.c_wb, 16))
    ctx_pt = at(sm.c_d6)
    for g in range(8):
        gsl = slice(g * 512, (g + 1) * 512)
        sch.op("sync", lambda e, t=gsl: e.dma_start(
            out=xh[:], in_=hbm_fm(c_scr, t)),
            waits=[ctx_pt, at(sm.mm), at(sm.c_d5)], inc=(sm.c_d0, 16))
        sch.op("sync", lambda e, t=gsl: e.dma_start(
            out=xc[:], in_=hbm_fm(xT, t)), waits=[at(sm.xcfree)],
            inc=(sm.c_xc, 16))
        for pc in range(NPC):
            PS = P2 if pc % 2 == 0 else P3
            for ic in range(NPC):
                w = [at(sm.c_d0), at(sm.c_wb), at(sm.evv), at(sm.evs)] \
                    if ic == 0 else []
                sch.op("tensor", lambda e, i=ic, p=pc, ps=PS: e.matmul(
                    ps[:], wsb[0][:, i, p * 128:(p + 1) * 128],
                    xh[:, i, :],
                    start=(i == 0), stop=(i == NPC - 1)), waits=w,
                    inc=(sm.mm, 1) if ic == NPC - 1 else None)
            sch.op("scalar", lambda e, p=pc, ps=PS: e.activation(
                t1[:], ps[:], AF.Identity, bias=bocol[:, p, 0:1]),
                waits=[at(sm.mm), at(sm.evv)], inc=(sm.evs, 1))
            sch.op("vector", lambda e, p=pc: e.tensor_add(
                t1[:], t1[:], xc[:, p, :]),
                waits=[at(sm.evs), at(sm.c_xc)], inc=(sm.evv, 1))
            sch.op("scalar", lambda e, p=pc: e.activation(
                yc[:, p % 2, :], t1[:], AF.Copy),
                waits=[at(sm.evv), at(X1[pc % 2])], inc=(sm.evs, 1))
            sch.op("gpsimd", lambda e, p=pc, t=gsl: e.dma_start(
                out=hbm_fm(x3T, t)[:, p, :], in_=yc[:, p % 2, :]),
                waits=[at(sm.evs)], inc=(X1[pc % 2], 16))
        sch.op("vector", lambda e: e.engine_nop(), waits=[at(sm.mm)],
               inc=(sm.xcfree, 1))
    sch.op("gpsimd", lambda e: e.nop(), waits=[at(X1[0]), at(X1[1])])


def emit_l1f(nc, sch, sm, ctx, io):
    xT = io['l1f_xT']
    cenT = io['l1f_cenT']
    lngT = io['l1f_lngT']
    lnbT = io['l1f_lnbT']
    w1S = io['l1f_w1S']
    w2T = io['l1f_w2T']
    b1T = io['l1f_b1T']
    b2T = io['l1f_b2T']
    iot_h = io['l1f_iot_h']
    sel8_h = io['l1f_sel8_h']
    ones_h = io['l1f_ones_h']
    eye_h = io['l1f_eye_h']
    outT = io['l1f_outT']
    NF = 32
    E = ctx.enter_context
    hbuf = E(nc.sbuf_tensor("l1f_hbuf", [128, NF, 512], BF16))
    xc = E(nc.sbuf_tensor("l1f_xc", [128, NPC, 512], F32))
    xhat = E(nc.sbuf_tensor("l1f_xhat", [128, NPC, 512], BF16))
    xn = E(nc.sbuf_tensor("l1f_xn", [128, NPC, 512], BF16))
    scr = E(nc.sbuf_tensor("l1f_scr", [128, 512, 3], F32))
    t1 = E(nc.sbuf_tensor("l1f_t1", [128, 512], F32))
    t2 = E(nc.sbuf_tensor("l1f_t2", [128, 512], F32))
    w1s = E(nc.sbuf_tensor("l1f_w1s", [128, 4, NPC, 128], BF16))
    w2s = E(nc.sbuf_tensor("l1f_w2s", [128, 4, D], BF16))
    cenb = E(nc.sbuf_tensor("l1f_cenb", [128, NPC, 8], F32))
    lng = E(nc.sbuf_tensor("l1f_lng", [128, NPC, 8], F32))
    lnb = E(nc.sbuf_tensor("l1f_lnb", [128, NPC, 8], F32))
    b1c = E(nc.sbuf_tensor("l1f_b1c", [128, NF, 8], F32))
    b2c = E(nc.sbuf_tensor("l1f_b2c", [128, NPC, 8], F32))
    afft = E(nc.sbuf_tensor("l1f_afft", [128, 4, 8], F32))
    gmask = E(nc.sbuf_tensor("l1f_gmask", [128, 4, 8], F32))
    gE = E(nc.sbuf_tensor("l1f_gE", [8, 512], F32))
    gbc = E(nc.sbuf_tensor("l1f_gbc", [128, 512], F32))
    yco = E(nc.sbuf_tensor("l1f_yco", [128, 2, 512], F32))
    m8 = E(nc.sbuf_tensor("l1f_m8", [128, 1], F32))
    iot = E(nc.sbuf_tensor("l1f_iot", [128, 8], F32))
    sel8 = E(nc.sbuf_tensor("l1f_sel8", [8, 1024], F32))
    ones_bf = E(nc.sbuf_tensor("l1f_ones_bf", [128, 128], BF16))
    eyef = E(nc.sbuf_tensor("l1f_eyef", [128, 128], F32))
    eps_col = E(nc.sbuf_tensor("l1f_eps_col", [128, 1], F32))
    flush_sc = E(nc.sbuf_tensor("l1f_flush_sc", [1, 8], F32))
    P0 = E(nc.psum_tensor("l1f_P0", [128, 512], F32))
    P1 = E(nc.psum_tensor("l1f_P1", [128, 512], F32))
    P2 = E(nc.psum_tensor("l1f_P2", [128, 512], F32))
    P3 = E(nc.psum_tensor("l1f_P3", [128, 512], F32))
    P4 = E(nc.psum_tensor("l1f_P4", [128, 512], F32))
    P5 = E(nc.psum_tensor("l1f_P5", [128, 512], F32))
    P6 = E(nc.psum_tensor("l1f_P6", [128, 512], F32))
    P7 = E(nc.psum_tensor("l1f_P7", [128, 512], F32))
    PD8 = [P0, P1, P2, P3, P4, P5, P6, P7]
    X1 = [sm.c_x10, sm.c_x11]
    W1S = [sm.c_d1, sm.c_d2, sm.w1a, sm.w1b]
    W2S = [sm.c_d3, sm.c_d4, sm.w2a, sm.w2b]

    for fn in (lambda e: e.dma_start(out=ones_bf[:], in_=ones_h),
               lambda e: e.dma_start(out=eyef[:], in_=eye_h),
               lambda e: e.dma_start(out=iot[:], in_=iot_h),
               lambda e: e.dma_start(out=sel8[:], in_=sel8_h),
               lambda e: e.dma_start(
                   out=cenb[:], in_=cenT.rearrange("(c p) n -> p c n", p=128)),
               lambda e: e.dma_start(
                   out=lng[:], in_=lngT.rearrange("(c p) n -> p c n", p=128)),
               lambda e: e.dma_start(
                   out=lnb[:], in_=lnbT.rearrange("(c p) n -> p c n", p=128)),
               lambda e: e.dma_start(
                   out=b1c[:], in_=b1T.rearrange("(c p) n -> p c n", p=128)),
               lambda e: e.dma_start(
                   out=b2c[:], in_=b2T.rearrange("(c p) n -> p c n", p=128)),
               ):
        sch.op("gpsimd", fn, inc=(sm.c_const, 16))
    sch.op("vector", lambda e: e.memset(eps_col[:], EPS), inc=(sm.dve, 1))
    consts = at(sm.c_const)
    sch.op("tensor", lambda e: e.nop(), waits=[consts])
    sch.op("scalar", lambda e: e.nop(), waits=[consts, at(sm.dve)])
    sch.op("vector", lambda e: e.engine_nop(), waits=[consts])

    for tc in range(8):
        tsl = slice(tc * 512, (tc + 1) * 512)
        emit_ln_xhat(sch, sm, xT, tsl, xc, xn, xhat, xhat, P0, P1, scr,
                     ones_bf, eps_col, t1)
        for tt in range(4):
            for ic in range(NPC):
                w = [at(sm.evv), at(sm.evs), at(sm.xh)] if ic == 0 else []
                sch.op("tensor", lambda e, i=ic, t3=tt: e.matmul(
                    P0[:, 0:8], xc[:, i, t3 * 128:(t3 + 1) * 128],
                    cenb[:, i, :], start=(i == 0), stop=(i == NPC - 1)),
                    waits=w, inc=(sm.mm, 1) if ic == NPC - 1 else None)
            sch.op("vector", lambda e, t3=tt: e.tensor_copy(
                afft[:, t3, :], P0[:, 0:8]), waits=[at(sm.mm)],
                inc=(sm.evv, 1))
            sch.op("vector", lambda e, t3=tt: e.reduce_max(
                m8[:], afft[:, t3, :], axis=AX.X))
            sch.op("vector", lambda e, t3=tt: e.tensor_scalar(
                gmask[:, t3, :], afft[:, t3, :], m8[:], None,
                ALU.is_equal))
            sch.op("vector", lambda e, t3=tt: e.scalar_tensor_tensor(
                gmask[:, t3, :], gmask[:, t3, :], 1000.0, iot[:],
                ALU.mult, ALU.add))
            sch.op("vector", lambda e, t3=tt: e.reduce_max(
                m8[:], gmask[:, t3, :], axis=AX.X))
            sch.op("vector", lambda e, t3=tt: e.tensor_scalar(
                gmask[:, t3, :], gmask[:, t3, :], m8[:], None,
                ALU.is_equal), inc=(sm.dve, 1))
            sch.op("scalar", lambda e, t3=tt: e.activation(
                afft[:, t3, :], afft[:, t3, :], AF.Tanh, scale=0.5),
                waits=[at(sm.dve)], inc=(sm.act, 1))
            sch.op("vector", lambda e, t3=tt: e.tensor_scalar(
                afft[:, t3, :], afft[:, t3, :], 0.5, 0.5,
                ALU.mult, ALU.add), waits=[at(sm.act)])
            sch.op("vector", lambda e, t3=tt: e.tensor_mul(
                gmask[:, t3, :], gmask[:, t3, :], afft[:, t3, :]),
                inc=(sm.dve, 1))
            sch.op("tensor", lambda e, t3=tt: e.matmul(
                P1[0:8, 0:128], gmask[:, t3, :], eyef[:],
                start=True, stop=True),
                waits=[at(sm.dve), at(sm.evv), at(sm.evs)],
                inc=(sm.mm, 1))
            sch.op("vector", lambda e, t3=tt: e.tensor_copy(
                gE[0:8, t3 * 128:(t3 + 1) * 128], P1[0:8, 0:128]),
                waits=[at(sm.mm)], inc=(sm.evv, 1))
        for ex in range(8):
            sch.op("tensor", lambda e, x2=ex: e.matmul(
                P0[:], sel8[:, x2 * 128:(x2 + 1) * 128], gE[:],
                start=True, stop=True),
                waits=[at(sm.evv), at(sm.evs)], inc=(sm.mm, 1))
            sch.op("vector", lambda e: e.tensor_copy(gbc[:], P0[:]),
                   waits=[at(sm.mm)], inc=(sm.evv, 1))
            for pc in range(NPC):
                sch.op("vector", lambda e, p=pc, x2=ex: e.tensor_scalar(
                    xn[:, p, :], xhat[:, p, :], lng[:, p, x2:x2 + 1],
                    lnb[:, p, x2:x2 + 1], ALU.mult, ALU.add),
                    waits=[at(sm.mm)] if pc == 0 else [],
                    inc=(sm.dve, 1) if pc == NPC - 1 else None)
            for ft in range(NF):
                PH = P6 if ft % 2 == 0 else P7
                sl = ft % 4
                sch.op("gpsimd", lambda e, x2=ex, f=ft, s2=sl: e.dma_start(
                    out=w1s[:, s2, :, :],
                    in_=w1S[x2, f].rearrange("p (c n) -> p c n", n=128)),
                    waits=[lag(sm.mm, 3)],
                    inc=(W1S[sl], 16))
                for ic in range(NPC):
                    w = [at(W1S[sl]), at(sm.dve), at(sm.evs), at(sm.evv)] \
                        if ic == 0 else []
                    sch.op("tensor", lambda e, i=ic, s2=sl, p=PH: e.matmul(
                        p[:], w1s[:, s2, i, :], xn[:, i, :],
                        start=(i == 0), stop=(i == NPC - 1)), waits=w,
                        inc=(sm.mm, 1) if ic == NPC - 1 else None)
                sch.op("scalar", lambda e, f=ft, x2=ex, p=PH: e.activation(
                    hbuf[:, f, :], p[:], AF.Relu, bias=b1c[:, f, x2:x2 + 1]),
                    waits=[at(sm.mm)], inc=(sm.evs, 1))
            for ft in range(NF):
                sl = ft % 4
                sch.op("gpsimd", lambda e, x2=ex, f=ft, s2=sl: e.dma_start(
                    out=w2s[:, s2, :],
                    in_=w2T[x2, f * 128:(f + 1) * 128, :]),
                    waits=[lag(sm.mm, 3)], inc=(W2S[sl], 16))
                for pc in range(NPC):
                    w = [at(W2S[sl]), at(sm.evs)] if pc == 0 else []
                    if ft == 0 and pc == 0:
                        w.append(at(sm.evv))
                    sch.op("tensor", lambda e, f=ft, p=pc, s2=sl: e.matmul(
                        PD8[p][:], w2s[:, s2, p * 128:(p + 1) * 128],
                        hbuf[:, f, :],
                        start=(f == 0), stop=(f == NF - 1)), waits=w,
                        inc=(sm.mm, 1) if pc == NPC - 1 else None)
            for pc in range(NPC):
                sch.op("scalar", lambda e, p=pc, x2=ex: e.activation(
                    t1[:], PD8[p][:], AF.Identity, bias=b2c[:, p, x2:x2 + 1]),
                    waits=[at(sm.mm), at(sm.evv)], inc=(sm.evs, 1))
                sch.op("vector", lambda e: e.tensor_mul(t2[:], t1[:],
                                                        gbc[:]),
                       waits=[at(sm.evs)])
                sch.op("vector", lambda e, p=pc: e.tensor_add(
                    xc[:, p, :], xc[:, p, :], t2[:]),
                    inc=(sm.evv, 1))
        for pc in range(NPC):
            sch.op("scalar", lambda e, p=pc, b2=pc % 2: e.activation(
                t1[:] if False else yco[:, b2, :], xc[:, p, :], AF.Copy),
                waits=[at(sm.evv), at(X1[pc % 2])], inc=(sm.evs, 1))
            sch.op("gpsimd", lambda e, p=pc, t=tsl, b2=pc % 2: e.dma_start(
                out=hbm_fm(outT, t)[:, p, :], in_=yco[:, b2, :]),
                waits=[at(sm.evs)], inc=(X1[pc % 2], 16))
        sch.op("vector", lambda e: e.engine_nop(),
               waits=[at(X1[0]), at(X1[1])], inc=(sm.xcfree, 1))
    sch.op("gpsimd", lambda e: e.nop(), waits=[at(X1[0]), at(X1[1])])


def barrier(sch, sm):
    sch.op('vector', lambda e: e.engine_nop(), inc=(sm.dve, 1))
    allsems = [at(getattr(sm, n)) for n in SEM_NAMES]
    for eng in ('tensor', 'gpsimd', 'sync'):
        sch.op(eng, lambda e: e.nop(), waits=allsems)
    sch.op('vector', lambda e: e.engine_nop(), waits=allsems)
    sch.op('scalar', lambda e: e.nop(), waits=allsems)


def build_fused():
    import contextlib
    nc = bass.Bass("TRN2", target_bir_lowering=False)
    io = {}
    io['l0a_xT'] = nc.dram_tensor('l0a_xT', [D, S], F32, kind="ExternalInput").ap()
    io['l0a_wqT'] = nc.dram_tensor('l0a_wqT', [D, D], F32, kind="ExternalInput").ap()
    io['l0a_wkT'] = nc.dram_tensor('l0a_wkT', [D, D], F32, kind="ExternalInput").ap()
    io['l0a_wvT'] = nc.dram_tensor('l0a_wvT', [D, D], F32, kind="ExternalInput").ap()
    io['l0a_woT'] = nc.dram_tensor('l0a_woT', [D, D], F32, kind="ExternalInput").ap()
    io['l0a_masks'] = nc.dram_tensor('l0a_masks', [4, 128, 512], F32, kind="ExternalInput").ap()
    io['l0a_ones_h'] = nc.dram_tensor('l0a_ones_h', [128, 128], F32, kind="ExternalInput").ap()
    io['l0a_x1T'] = nc.dram_tensor('l0a_x1T', [D, S], F32).ap()
    io['l0a_v_scr'] = nc.dram_tensor('l0a_v_scr', [S, D], BF16).ap()
    io['l0a_k_scr'] = nc.dram_tensor('l0a_k_scr', [D, S], BF16).ap()
    io['l0f_xT'] = io['l0a_x1T']
    io['l0f_wgT'] = nc.dram_tensor('l0f_wgT', [D, 3072], I32, kind="ExternalInput").ap()
    io['l0f_wuT'] = nc.dram_tensor('l0f_wuT', [D, 3072], I32, kind="ExternalInput").ap()
    io['l0f_wdT'] = nc.dram_tensor('l0f_wdT', [3072, D], I32, kind="ExternalInput").ap()
    io['l0f_ones_h'] = io['l0a_ones_h']
    io['l0f_yT'] = nc.dram_tensor('l0f_yT', [D, S], F32).ap()
    io['l1a_xT'] = io['l0f_yT']
    io['l1a_wq'] = nc.dram_tensor('l1a_wq', [D, D], F32, kind="ExternalInput").ap()
    io['l1a_wk'] = nc.dram_tensor('l1a_wk', [D, D], F32, kind="ExternalInput").ap()
    io['l1a_wv'] = nc.dram_tensor('l1a_wv', [D, D], F32, kind="ExternalInput").ap()
    io['l1a_wo'] = nc.dram_tensor('l1a_wo', [D, D], F32, kind="ExternalInput").ap()
    io['l1a_bq'] = nc.dram_tensor('l1a_bq', [D, 1], F32, kind="ExternalInput").ap()
    io['l1a_bk'] = nc.dram_tensor('l1a_bk', [D, 1], F32, kind="ExternalInput").ap()
    io['l1a_bvr'] = nc.dram_tensor('l1a_bvr', [1, D], F32, kind="ExternalInput").ap()
    io['l1a_bo'] = nc.dram_tensor('l1a_bo', [D, 1], F32, kind="ExternalInput").ap()
    io['l1a_aqkv'] = nc.dram_tensor('l1a_aqkv', [1, 3], F32, kind="ExternalInput").ap()
    io['l1a_ones_h'] = io['l0a_ones_h']
    io['l1a_eye_h'] = nc.dram_tensor('l1a_eye_h', [128, 128], F32, kind="ExternalInput").ap()
    io['l1a_blk_h'] = nc.dram_tensor('l1a_blk_h', [8, 128, 16], F32, kind="ExternalInput").ap()
    io['l1a_sel_h'] = nc.dram_tensor('l1a_sel_h', [16, 2048], F32, kind="ExternalInput").ap()
    io['l1a_x3T'] = nc.dram_tensor('l1a_x3T', [D, S], F32).ap()
    io['l1a_q_scr'] = nc.dram_tensor('l1a_q_scr', [D, S], BF16, kind="Internal").ap()
    io['l1a_k_scr'] = nc.dram_tensor('l1a_k_scr', [D, S], BF16, kind="Internal").ap()
    io['l1a_v_scr'] = nc.dram_tensor('l1a_v_scr', [S, D], BF16, kind="Internal").ap()
    io['l1a_c_scr'] = nc.dram_tensor('l1a_c_scr', [D, S], BF16, kind="Internal").ap()
    io['l1f_xT'] = io['l1a_x3T']
    io['l1f_cenT'] = nc.dram_tensor('l1f_cenT', [D, 8], F32, kind="ExternalInput").ap()
    io['l1f_lngT'] = nc.dram_tensor('l1f_lngT', [D, 8], F32, kind="ExternalInput").ap()
    io['l1f_lnbT'] = nc.dram_tensor('l1f_lnbT', [D, 8], F32, kind="ExternalInput").ap()
    io['l1f_w1S'] = nc.dram_tensor('l1f_w1S', [8, 32, 128, 1024], F32, kind="ExternalInput").ap()
    io['l1f_w2T'] = nc.dram_tensor('l1f_w2T', [8, 4096, D], F32, kind="ExternalInput").ap()
    io['l1f_b1T'] = nc.dram_tensor('l1f_b1T', [4096, 8], F32, kind="ExternalInput").ap()
    io['l1f_b2T'] = nc.dram_tensor('l1f_b2T', [D, 8], F32, kind="ExternalInput").ap()
    io['l1f_iot_h'] = nc.dram_tensor('l1f_iot_h', [128, 8], F32, kind="ExternalInput").ap()
    io['l1f_sel8_h'] = nc.dram_tensor('l1f_sel8_h', [8, 1024], F32, kind="ExternalInput").ap()
    io['l1f_ones_h'] = io['l0a_ones_h']
    io['l1f_eye_h'] = io['l1a_eye_h']
    io['l1f_outT'] = nc.dram_tensor('l1f_outT', [D, S], F32, kind="ExternalOutput").ap()
    with contextlib.ExitStack() as outer:
        sm = SM(nc, outer, SEM_NAMES)
        sch = Sch([getattr(sm, n) for n in SEM_NAMES])
        flush_g = outer.enter_context(nc.sbuf_tensor("gflush", [1, 8], F32))
        for px, fn in (("l0a", emit_l0a), ("l0f", emit_l0f),
                       ("l1a", emit_l1a), ("l1f", emit_l1f)):
            with contextlib.ExitStack() as ph:
                fn(nc, sch, sm, ph, io)
            barrier(sch, sm)
        blk = outer.enter_context(nc.Block())

        @blk.tensor
        def _(t):
            sch.emit("tensor", t)

        @blk.vector
        def _(v):
            sch.emit("vector", v, flush=lambda e: e.engine_nop())

        @blk.scalar
        def _(s):
            sch.emit("scalar", s,
                     flush=lambda e: e.memzero(flush_g[0:1, 0:8]))

        @blk.gpsimd
        def _(g):
            sch.emit("gpsimd", g)

        @blk.sync
        def _(sy):
            sch.emit("sync", sy)

    return nc


_cache = {}


def _get(name, builder):
    if name not in _cache:
        _cache[name] = builder()
    return _cache[name]


def _masks_np():
    m = np.zeros((4, 128, 512), np.float32)
    for k in range(4):
        for i in range(128):
            m[k, i, k * 128 + i + 1:] = NEG
    return m


def _blk_np():
    b = np.zeros((8, 128, 16), np.float32)
    for pc in range(8):
        for r in range(128):
            b[pc, r, 2 * pc + r // 64] = 1.0
    return b


def _sel16_np():
    s = np.zeros((16, 2048), np.float32)
    for h in range(16):
        s[h, h * 128:(h + 1) * 128] = 1.0
    return s


def _sel8_np():
    s = np.zeros((8, 1024), np.float32)
    for e in range(8):
        s[e, e * 128:(e + 1) * 128] = 1.0
    return s


def _t(a):
    return np.ascontiguousarray(np.asarray(a).T)


def kernel(**inputs):
    x = np.asarray(inputs["L0_attn_in_features"])
    ones128 = np.ones((128, 128), np.float32)
    eye128 = np.eye(128, dtype=np.float32)
    D_ = 1024
    base = {
        "l0a_wqT": _t(inputs["L0_attn_w_q"]), "l0a_wkT": _t(inputs["L0_attn_w_k"]),
        "l0a_wvT": _t(inputs["L0_attn_w_v"]), "l0a_woT": _t(inputs["L0_attn_w_o"]),
        "l0a_masks": _masks_np(), "l0a_ones_h": ones128,
        "l0f_wgT": _t(inputs["L0_ffn_w_gate"]), "l0f_wuT": _t(inputs["L0_ffn_w_up"]),
        "l0f_wdT": _t(inputs["L0_ffn_w_down"]),
        "l1a_wq": np.ascontiguousarray(np.asarray(inputs["L1_attn_wq"], np.float32)),
        "l1a_wk": np.ascontiguousarray(np.asarray(inputs["L1_attn_wk"], np.float32)),
        "l1a_wv": np.ascontiguousarray(np.asarray(inputs["L1_attn_wv"], np.float32)),
        "l1a_wo": np.ascontiguousarray(np.asarray(inputs["L1_attn_wo"], np.float32)),
        "l1a_bq": np.asarray(inputs["L1_attn_bq"], np.float32).reshape(D_, 1),
        "l1a_bk": np.asarray(inputs["L1_attn_bk"], np.float32).reshape(D_, 1),
        "l1a_bvr": np.asarray(inputs["L1_attn_bv"], np.float32).reshape(1, D_),
        "l1a_bo": np.asarray(inputs["L1_attn_bo"], np.float32).reshape(D_, 1),
        "l1a_aqkv": np.stack([np.asarray(inputs["L1_attn_aq"], np.float32)[0],
                              np.asarray(inputs["L1_attn_ak"], np.float32)[0],
                              np.asarray(inputs["L1_attn_av"], np.float32)[0]]
                             ).reshape(1, 3),
        "l1a_eye_h": eye128, "l1a_blk_h": _blk_np(), "l1a_sel_h": _sel16_np(),
        "l1f_cenT": _t(inputs["L1_ffn_centroids"]),
        "l1f_lngT": _t(inputs["L1_ffn_ln_g"]), "l1f_lnbT": _t(inputs["L1_ffn_ln_b"]),
        "l1f_w1S": np.ascontiguousarray(
            np.asarray(inputs["L1_ffn_w1"], np.float32).reshape(
                8, 32, 128, 8, 128).transpose(0, 1, 4, 3, 2).reshape(
                8, 32, 128, 1024)),
        "l1f_w2T": np.ascontiguousarray(
            np.asarray(inputs["L1_ffn_w2"], np.float32).transpose(0, 2, 1)),
        "l1f_b1T": _t(inputs["L1_ffn_b1"]), "l1f_b2T": _t(inputs["L1_ffn_b2"]),
        "l1f_iot_h": np.broadcast_to(-np.arange(8, dtype=np.float32),
                                     (128, 8)).copy(),
        "l1f_sel8_h": _sel8_np(),
    }
    nc = _get("fused", build_fused)
    im = [dict(base, l0a_xT=_t(x[b])) for b in range(8)]
    res = run_bass_kernel_spmd(nc, im, core_ids=list(range(8)))
    out = np.stack([np.asarray(r["l1f_outT"]).T for r in res.results])
    return np.ascontiguousarray(out)
